# version 18
# baseline (speedup 1.0000x reference)
import contextlib

import numpy as np
import ml_dtypes

import concourse.bass as bass
import concourse.mybir as mybir
from concourse.bass_utils import run_bass_kernel_spmd

F32 = mybir.dt.float32
BF16 = mybir.dt.bfloat16
AF = mybir.ActivationFunctionType
ALU = mybir.AluOpType
AX = mybir.AxisListType
BF = ml_dtypes.bfloat16

B, T, D = 4, 2048, 2048
H, KVH, DH = 16, 4, 128
IC = 16
CH = 512
WT = 1536
WC = 12
QOFF = 1023
NSLOT = 2
HT = 768
HC = 6
E, IEXP, ISH = 8, 1024, 4096
TOK, NTC = 1024, 8
UT, UC = 2048, 16
NCORE = 8
THETA = 10000.0


_PERM = np.empty(128, dtype=np.int64)
_PERM[:64] = np.arange(64) * 2
_PERM[64:] = np.arange(64) * 2 + 1


def _perm_heads(w, nheads):
    return w.reshape(nheads, 128, D)[:, _PERM, :].reshape(nheads * 128, D)


def _tile_rhs_out512(wT):
    nin, nout = wT.shape
    a = wT.reshape(nin // 128, 128, nout // 512, 512)
    return np.ascontiguousarray(
        a.transpose(2, 1, 0, 3).reshape(nout // 512, 128, (nin // 128) * 512))


def _tile_gu(w):
    I_, Din = w.shape
    a = w.reshape(I_ // 128, 128, Din // 128, 128)
    return np.ascontiguousarray(
        a.transpose(0, 3, 2, 1).reshape(I_ // 128, 128, (Din // 128) * 128))


_ROPE_CACHE = {}


def _rope_base():
    if "cs" in _ROPE_CACHE:
        return _ROPE_CACHE["cs"]
    import ctypes
    libm = ctypes.CDLL("libm.so.6")
    libm.cosf.restype = ctypes.c_float
    libm.cosf.argtypes = [ctypes.c_float]
    libm.sinf.restype = ctypes.c_float
    libm.sinf.argtypes = [ctypes.c_float]
    af = (np.arange(0, DH, 2, dtype=np.float32) / np.float32(DH)).astype(np.float32)
    p = np.power(np.float64(THETA), af.astype(np.float64)).astype(np.float32)
    inv = (np.float32(1.0) / p).astype(np.float32)
    ang = (np.arange(T, dtype=np.float32)[:, None] * inv[None, :]).astype(np.float32)
    flat = ang.ravel()
    cos = np.fromiter((libm.cosf(float(v)) for v in flat), np.float32,
                      count=flat.size).reshape(ang.shape)
    sin = np.fromiter((libm.sinf(float(v)) for v in flat), np.float32,
                      count=flat.size).reshape(ang.shape)
    _ROPE_CACHE["cs"] = (cos, sin)
    return cos, sin


def _rope_tables(positions, reps):
    cos_t, sin_t = _rope_base()
    idx = np.clip(positions.astype(np.int64), 0, T - 1)
    return (np.tile(cos_t[idx], (1, reps)), np.tile(sin_t[idx], (1, reps)))


def host_prepare(inputs):
    x = np.asarray(inputs["L0_attn_x"], dtype=np.float32)
    wq = np.asarray(inputs["L0_attn_wq"], dtype=np.float32)
    wk = np.asarray(inputs["L0_attn_wk"], dtype=np.float32)
    wv = np.asarray(inputs["L0_attn_wv"], dtype=np.float32)
    wo = np.asarray(inputs["L0_attn_wo"], dtype=np.float32)
    gw = np.asarray(inputs["L0_ffn_gate_w"], dtype=np.float32)
    egw = np.asarray(inputs["L0_ffn_expert_gate_w"], dtype=np.float32)
    euw = np.asarray(inputs["L0_ffn_expert_up_w"], dtype=np.float32)
    edw = np.asarray(inputs["L0_ffn_expert_down_w"], dtype=np.float32)
    sgw = np.asarray(inputs["L0_ffn_shared_gate_w"], dtype=np.float32)
    suw = np.asarray(inputs["L0_ffn_shared_up_w"], dtype=np.float32)
    sdw = np.asarray(inputs["L0_ffn_shared_down_w"], dtype=np.float32)
    seg = np.asarray(inputs["L0_ffn_shared_expert_gate_w"], dtype=np.float32)

    shared = {}
    shared["wq_t"] = _tile_rhs_out512(np.ascontiguousarray(_perm_heads(wq, H).T))
    shared["wk_t"] = _tile_rhs_out512(np.ascontiguousarray(_perm_heads(wk, KVH).T))
    shared["wv_t"] = _tile_rhs_out512(np.ascontiguousarray(wv.T))
    shared["wo_t"] = _tile_rhs_out512(np.ascontiguousarray(wo.T))
    gp = np.concatenate([gw.T, seg.T], axis=1)
    shared["gate_pack"] = np.ascontiguousarray(
        gp.reshape(IC, 128, 9).transpose(1, 0, 2).reshape(128, IC * 9))
    shared["wg_t"] = np.concatenate([_tile_gu(egw[e]) for e in range(E)])
    shared["wu_t"] = np.concatenate([_tile_gu(euw[e]) for e in range(E)])
    shared["wd_t"] = np.concatenate(
        [_tile_rhs_out512(np.ascontiguousarray(edw[e].T)) for e in range(E)])
    shared["wsg_t"] = _tile_gu(sgw)
    shared["wsu_t"] = _tile_gu(suw)
    wsd = _tile_rhs_out512(np.ascontiguousarray(sdw.T))
    shared["wsd_t"] = np.ascontiguousarray(
        wsd.reshape(4, 128, 4, 8 * 512).transpose(0, 2, 1, 3).reshape(16, 128, 4096))

    shared["idf"] = np.eye(128, dtype=np.float32)
    shared["idb"] = np.eye(128).astype(BF)
    shared["ones_col_b"] = np.ones((128, 1), dtype=BF)
    shared["ones_row_f"] = np.ones((1, 128), dtype=np.float32)
    d_ = np.arange(4)[:, None, None]
    p_ = np.arange(128)[None, :, None]
    i_ = np.arange(512)[None, None, :]
    cmask = np.where(i_ <= d_ * 128 + p_, np.float32(-1e30), np.float32(0.0))
    shared["cmask"] = np.ascontiguousarray(
        cmask.transpose(1, 0, 2).reshape(128, 4 * 512))
    shared["cmask16"] = shared["cmask"].astype(BF)

    in_maps = []
    for core in range(NCORE):
        b, half = core // 2, core % 2
        chunks = (2 * half, 2 * half + 1)
        m = dict(shared)
        ubase = half * 1024 - QOFF
        kpos_u = ubase + np.arange(UT)
        okmask = (kpos_u >= 0) & (kpos_u < T)
        xkv = np.zeros((UT, D), dtype=np.float32)
        xkv[okmask] = x[b, kpos_u[okmask]]
        val = np.where(okmask, np.float32(-12.0), np.float32(-1e9))
        m["x_kv"] = np.ascontiguousarray(xkv.reshape(UC, 128, D))
        m["x_own"] = np.ascontiguousarray(
            x[b, half * TOK:(half + 1) * TOK].reshape(NTC, 128, D))
        m["valid12"] = np.ascontiguousarray(val.reshape(UC, 128).T)
        cosk, sink = _rope_tables(np.maximum(kpos_u, 0).astype(np.float64), KVH)
        m["cosk"] = np.ascontiguousarray(
            cosk.reshape(UC, 128, 256).transpose(1, 0, 2)
            .reshape(128, UC * 256))
        m["sink"] = np.ascontiguousarray(
            sink.reshape(UC, 128, 256).transpose(1, 0, 2)
            .reshape(128, UC * 256))
        qpos = (half * TOK + np.arange(TOK)).astype(np.float64)
        cosq, sinq = _rope_tables(qpos, 4)
        m["cosq"] = np.ascontiguousarray(
            cosq.reshape(NTC, 128, 256).transpose(1, 0, 2).reshape(128, NTC * 256))
        m["sinq"] = np.ascontiguousarray(
            sinq.reshape(NTC, 128, 256).transpose(1, 0, 2).reshape(128, NTC * 256))
        in_maps.append(m)
    return in_maps



class Sem:
    __slots__ = ("h", "n", "dma")

    def __init__(self, h, dma):
        self.h, self.n, self.dma = h, 0, dma


NONE_T = (None, 0)


class Walk:
    ENGINES = ("sync", "tensor", "vector", "scalar", "gpsimd")

    def __init__(self):
        self.prog = {e: [] for e in self.ENGINES}
        self.hw = {e: {} for e in self.ENGINES}

    def _filter(self, eng, waits):
        hw = self.hw[eng]
        ws = []
        for s, v in waits:
            if s is None or v is None or v <= 0:
                continue
            if hw.get(id(s), -1) >= v:
                continue
            hw[id(s)] = v
            ws.append((s.h, v))
        return tuple(ws)

    def op(self, eng, fn, inc=None, waits=()):
        ws = self._filter(eng, waits)
        amt = 16 if (inc is not None and inc.dma) else 1
        if inc is not None:
            inc.n += amt
        semh = inc.h if inc is not None else None

        def thunk(e, fn=fn, ws=ws, semh=semh, amt=amt):
            for (h, v) in ws:
                e.wait_ge(h, v)
            ins = fn(e)
            if semh is not None:
                ins.then_inc(semh, amt)
        self.prog[eng].append(thunk)
        return inc.n if inc is not None else None

    def wait(self, eng, waits):
        ws = self._filter(eng, waits)
        if ws:
            def thunk(e, ws=ws):
                for (h, v) in ws:
                    e.wait_ge(h, v)
            self.prog[eng].append(thunk)


def build_nc(debug=False):
    nc = bass.Bass("TRN2", target_bir_lowering=False)

    dt = {}

    def din(name, shape, dtype=F32):
        dt[name] = nc.dram_tensor(name, list(shape), dtype, kind="ExternalInput").ap()

    din("x_kv", (UC, 128, D))
    din("x_own", (NTC, 128, D))
    din("valid12", (128, UC))
    din("cosk", (128, UC * 256))
    din("sink", (128, UC * 256))
    din("cosq", (128, NTC * 256))
    din("sinq", (128, NTC * 256))
    din("cmask", (128, 4 * 512))
    din("cmask16", (128, 4 * 512), BF16)
    din("idf", (128, 128))
    din("idb", (128, 128), BF16)
    din("ones_col_b", (128, 1), BF16)
    din("ones_row_f", (1, 128))
    din("gate_pack", (128, IC * 9))
    din("wq_t", (4, 128, IC * 512))
    din("wk_t", (1, 128, IC * 512))
    din("wv_t", (1, 128, IC * 512))
    din("wo_t", (4, 128, IC * 512))
    din("wg_t", (E * 8, 128, 2048))
    din("wu_t", (E * 8, 128, 2048))
    din("wd_t", (E * 4, 128, 4096))
    din("wsg_t", (32, 128, 2048))
    din("wsu_t", (32, 128, 2048))
    din("wsd_t", (16, 128, 4096))

    out_d = nc.dram_tensor("out", [NTC, 128, D], F32, kind="ExternalOutput").ap()
    ln2a_d = nc.dram_tensor("ln2a_scr", [NTC, 128, D], BF16, kind="Internal").ap()

    dbg = {}
    if debug:
        for name, shape, dty in [
            ("dbg_q16", (128, H * TOK), BF16),
            ("dbg_k16", (128, KVH * UT), BF16),
            ("dbg_v16", (128, UC * 512), BF16),
            ("dbg_y16", (128, H * TOK), BF16),
            ("dbg_ln2a", (NTC, 128, D), BF16),
            ("dbg_attn", (NTC, 128, D), BF16),
            ("dbg_comb", (128, NTC * 8), F32),
            ("dbg_sg", (128, NTC), F32),
            ("dbg_x3fm", (128, IC * TOK), BF16),
            ("dbg_he", (128, 8 * TOK), BF16),
            ("dbg_acc", (NTC, 128, D), BF16),
        ]:
            dbg[name] = nc.dram_tensor(name, list(shape), dty, kind="ExternalOutput").ap()

    w = Walk()
    es = contextlib.ExitStack()
    with es:
        ALLSEMS = []

        def mksem(name, dma):
            s = Sem(es.enter_context(nc.semaphore(name)), dma)
            ALLSEMS.append(s)
            return s

        qX = mksem("qX", True)
        qT = mksem("qT", True)
        tT = mksem("tT", False)
        tV = mksem("tV", False)
        tS = mksem("tS", False)
        tG = mksem("tG", False)

        def barrier():
            vals = [(s, s.n) for s in ALLSEMS]
            for eng in Walk.ENGINES:
                w.wait(eng, vals)

        _nuid = [0]

        def mksb(stack):
            def f(name, shape, dtype=F32):
                _nuid[0] += 1
                return stack.enter_context(
                    nc.sbuf_tensor(f"{name}_{_nuid[0]}", list(shape), dtype))
            return f

        sb = mksb(es)

        pA = es.enter_context(nc.psum_tensor("pA", [128, 512], F32))
        pB = es.enter_context(nc.psum_tensor("pB", [128, 512], F32))
        pC = es.enter_context(nc.psum_tensor("pC", [128, 512], F32))
        pD = es.enter_context(nc.psum_tensor("pD", [128, 512], F32))
        pS = es.enter_context(nc.psum_tensor("pS", [128, 512], F32))
        pR = es.enter_context(nc.psum_tensor("pR", [128, 512], F32))

        valid_sb = sb("valid_sb", (128, UC))

        idb_sb = sb("idb_sb", (128, 128), BF16)
        onesc_sb = sb("onesc_sb", (128, 1), BF16)
        onesr_sb = sb("onesr_sb", (1, 128))
        eps_ln = sb("eps_ln", (128, 1))
        eps_q = sb("eps_q", (128, 1))
        eps_k = sb("eps_k", (128, 1))
        stats2 = [sb("stats", (128, 24)) for _ in range(2)]
        mv2 = [sb("mv", (128, 2)) for _ in range(2)]
        rstd2 = [sb("rstd", (128, 4)) for _ in range(2)]
        ssq42 = [sb("ssq4", (128, 4)) for _ in range(2)]

        for name, ap_sb in [("valid12", valid_sb),
                            ("idb", idb_sb),
                            ("ones_col_b", onesc_sb), ("ones_row_f", onesr_sb)]:
            w.op("gpsimd", lambda e, o=ap_sb[:], i=dt[name]: e.dma_start(out=o, in_=i),
                 inc=qT)
        tbl = (qT, qT.n)
        w.op("vector", lambda e, o=eps_ln[:]: e.memset(o, 1e-5))
        w.op("vector", lambda e, o=eps_q[:]: e.memset(o, 1.28e-4))
        w.op("vector", lambda e, o=eps_k[:]: e.memset(o, 1e-6), inc=tV)
        eps_tk = (tV, tV.n)

        idb_ap = idb_sb[:]

        class Stream:

            _sid = [0]

            def __init__(self, salloc, name, fd, qd, cast_eng="gpsimd", nbuf=2):
                self.fd, self.qd, self.nq = fd, qd, fd // qd
                self.nbuf = nbuf
                self.stg = [salloc(f"{name}_s{i}", (128, qd)) for i in range(2)]
                self.w16 = [salloc(f"{name}_w{i}", (128, fd), BF16)
                            for i in range(nbuf)]
                Stream._sid[0] += 1
                self.qsem = [mksem(f"q{name}{Stream._sid[0]}_{i}", True)
                             for i in range(2)]
                self.stg_free = [NONE_T, NONE_T]
                self.w16_free = [NONE_T] * nbuf
                self.cnt = 0
                self.eng = cast_eng
                self.sem = {"vector": tV, "scalar": tS, "gpsimd": tG}[cast_eng]
                self.pending = []

            def load(self, dram_ap):
                bi = self.cnt % self.nbuf
                self.cnt += 1
                cv = 0
                for q in range(self.nq):
                    si = (self.cnt * self.nq + q) % 2
                    src = dram_ap[:, q * self.qd:(q + 1) * self.qd]
                    v = w.op("sync",
                             lambda e, o=self.stg[si][:], i=src: e.dma_start(out=o, in_=i),
                             inc=self.qsem[si], waits=[self.stg_free[si]])
                    o_ap = self.w16[bi][:, q * self.qd:(q + 1) * self.qd]
                    s_ap = self.stg[si][:]
                    if self.eng == "vector":
                        fn = lambda e, o=o_ap, s=s_ap: e.tensor_copy(o, s)
                    elif self.eng == "gpsimd":
                        fn = lambda e, o=o_ap, s=s_ap: e.tensor_copy(o, s)
                    else:
                        fn = lambda e, o=o_ap, s=s_ap: e.copy(o, s)
                    cv = w.op(self.eng, fn, inc=self.sem,
                              waits=[(self.qsem[si], v), self.w16_free[bi]])
                    self.stg_free[si] = (self.sem, cv)
                return bi, (self.sem, cv)

            def prefetch(self, dram_ap):
                self.pending.append(self.load(dram_ap))

            def take(self):
                return self.pending.pop(0)

            def used(self, bi, ticket):
                self.w16_free[bi] = ticket

        ln_setbusy = [NONE_T, NONE_T]

        def emit_ln(src, dst, src_ready, dst_free, eps_tile, ls=0):
            stats, mv, rstd = stats2[ls], mv2[ls], rstd2[ls]
            for s4 in range(4):
                w.op("vector",
                     lambda e, o=stats[:, s4 * 6:(s4 + 1) * 6],
                     i=src[:, s4 * 512:(s4 + 1) * 512]: e.bn_stats(o, i),
                     inc=(tV if s4 == 3 else None),
                     waits=list(src_ready) + ([ln_setbusy[ls]] if s4 == 0 else []))
            w.op("vector", lambda e, o=mv[:], i=stats[:]: e.bn_aggr(o, i),
                 inc=tV, waits=[(tV, tV.n), (tS, tS.n)])
            va = tV.n
            w.op("scalar",
                 lambda e, o=rstd[:, 0:1], i=mv[:, 1:2], b=eps_tile[:]:
                 e.activation(o, i, AF.Ln, bias=b, scale=1.0),
                 inc=tS, waits=[(tV, va), eps_tk])
            w.op("scalar",
                 lambda e, o=rstd[:, 0:1]: e.activation(o, o, AF.Exp, scale=-0.5),
                 inc=tS, waits=[(tS, tS.n)])
            vs = tS.n
            cv = w.op("vector",
                      lambda e, o=dst[:], i=src[:], m=mv[:, 0:1], r=rstd[:, 0:1]:
                      e.tensor_scalar(o, i, m, r, ALU.subtract, ALU.mult),
                      inc=tV, waits=[(tS, vs)] + list(dst_free))
            ln_setbusy[ls] = (tV, cv)
            return cv

        def emit_normrope(psum, qsc, qsq, qrot, cos_ap, sin_ap, eps_tile, scale,
                          mm_ticket, rot_cons, ls=0):
            rstd, ssq4 = rstd2[ls], ssq42[ls]
            cv = w.op("vector", lambda e, o=qsc[:], i=psum[:]: e.tensor_copy(o, i),
                      inc=tV, waits=[mm_ticket])
            psum_free = (tV, cv)
            w.op("vector", lambda e, o=qsq[:], i=qsc[:]: e.tensor_tensor(o, i, i, ALU.mult),
                 inc=tV, waits=[(tV, cv)])
            w.op("vector",
                 lambda e, o=ssq4[:], i=qsq[:].rearrange("p (h d) -> p h d", h=4):
                 e.tensor_reduce(o, i, axis=AX.X, op=ALU.add),
                 inc=tV, waits=[(tV, tV.n), (tS, tS.n)])
            va = tV.n
            w.op("scalar",
                 lambda e, o=rstd[:], i=ssq4[:], b=eps_tile[:], s=scale:
                 e.activation(o, i, AF.Ln, bias=b, scale=s),
                 inc=tS, waits=[(tV, va), eps_tk])
            w.op("scalar",
                 lambda e, o=rstd[:]: e.activation(o, o, AF.Exp, scale=-0.5),
                 inc=tS, waits=[(tS, tS.n)])
            vs = tS.n
            for hh in range(4):
                w.op("vector",
                     lambda e, o=qsc[:, hh * 128:(hh + 1) * 128], r=rstd[:, hh:hh + 1]:
                     e.tensor_scalar(o, o, r, None, ALU.mult),
                     inc=(tV if hh == 3 else None), waits=[(tS, vs)])
            sv = tV.n
            q3 = qsc[:].rearrange("p (h d) -> p h d", h=4)
            r3 = qrot[:].rearrange("p (h d) -> p h d", h=4)
            s3 = qsq[:].rearrange("p (h d) -> p h d", h=4)
            c3 = cos_ap.rearrange("p (h d) -> p h d", h=4)
            n3 = sin_ap.rearrange("p (h d) -> p h d", h=4)
            w.op("vector",
                 lambda e, o=r3[:, :, 0:64], a=q3[:, :, 0:64], b=c3:
                 e.tensor_tensor(o, a, b, ALU.mult),
                 waits=[(tV, sv), tbl] + list(rot_cons))
            w.op("vector",
                 lambda e, o=s3[:, :, 0:64], a=q3[:, :, 64:128], b=n3:
                 e.tensor_tensor(o, a, b, ALU.mult), inc=tV)
            w.op("vector",
                 lambda e, o=r3[:, :, 0:64], b=s3[:, :, 0:64]:
                 e.tensor_tensor(o, o, b, ALU.subtract),
                 inc=tV, waits=[(tV, tV.n)])
            w.op("vector",
                 lambda e, o=r3[:, :, 64:128], a=q3[:, :, 0:64], b=n3:
                 e.tensor_tensor(o, a, b, ALU.mult))
            w.op("vector",
                 lambda e, o=s3[:, :, 64:128], a=q3[:, :, 64:128], b=c3:
                 e.tensor_tensor(o, a, b, ALU.mult), inc=tV)
            w.op("vector",
                 lambda e, o=r3[:, :, 64:128], b=s3[:, :, 64:128]:
                 e.tensor_tensor(o, o, b, ALU.add),
                 inc=tV, waits=[(tV, tV.n)])
            return psum_free, tV.n

        ln2_stores = 0
        p1 = contextlib.ExitStack()
        with p1:
            sb1 = mksb(p1)
            q16 = sb1("q16", (128, H * TOK), BF16)
            k16 = sb1("k16", (128, KVH * UT), BF16)
            v16 = sb1("v16", (128, UC * 512), BF16)
            pab = contextlib.ExitStack()
            with pab:
                sb2 = mksb(pab)
                cosk_sl = sb2("cosk_sl", (128, UC * 256), BF16)
                sink_sl = sb2("sink_sl", (128, UC * 256), BF16)
                cosq_sb = sb2("cosq_sb", (128, NTC * 256), BF16)
                sinq_sb = sb2("sinq_sb", (128, NTC * 256), BF16)
                qTa = mksem("qTa", True)
                with nc.sbuf_tensor("ropestg", [128, UC * 256], F32) as ropestg:
                    stg_ap = ropestg[:]
                    prev = NONE_T
                    for nm_, dst_ in (("cosk", cosk_sl), ("sink", sink_sl),
                                      ("cosq", cosq_sb), ("sinq", sinq_sb)):
                        fdim = dst_[:].tensor.shape[1]
                        v_ = w.op("gpsimd",
                                  lambda e, o=ropestg[0:128, 0:fdim], i=dt[nm_]:
                                  e.dma_start(out=o, in_=i),
                                  inc=qTa, waits=[prev])
                        cv_ = w.op("vector",
                                   lambda e, o=dst_[:], i=ropestg[0:128, 0:fdim]:
                                   e.tensor_copy(o, i),
                                   inc=tV, waits=[(qTa, v_)])
                        prev = (tV, cv_)
                cs_ld = prev
                w.wait("gpsimd", [prev])
                xin = [sb2(f"xin{i}", (128, D)) for i in range(2)]
                xin_free = [NONE_T] * 2
                qxin = [mksem("qxin0", True), mksem("qxin1", True)]
                x1c = [sb2(f"x1c{i}", (128, D), BF16) for i in range(2)]
                x1c_free = [NONE_T] * 2
                x1fm = sb2("x1fm", (128, IC * UT), BF16)
                qscb = [sb2("qsc", (128, 512), BF16) for _ in range(2)]
                qsqb = [sb2("qsq", (128, 512), BF16) for _ in range(2)]
                qrotb = [sb2("qrot", (128, 512), BF16) for _ in range(2)]
                wstr = Stream(sb2, "wA", 4096, 1024, cast_eng="gpsimd", nbuf=2)
                pTb = [pab.enter_context(nc.psum_tensor("pT_0", [128, 1024], BF16)),
                       pab.enter_context(nc.psum_tensor("pT_1", [128, 1024], BF16))]

                x1_r = x1fm[:].rearrange("p (ic t) -> p ic t", ic=IC)
                pT_r = [pTb[0][:].rearrange("p (ii t) -> p ii t", ii=8),
                        pTb[1][:].rearrange("p (ii t) -> p ii t", ii=8)]
                tr_free = [NONE_T, NONE_T]
                nr_state = [dict(rot_cons=NONE_T) for _ in range(2)]
                bank_free2 = [NONE_T, NONE_T]
                unit_ctr = [0]
                x1_ck = [NONE_T] * UC
                tr_pending = []

                def flush_tr(keep=1):
                    while len(tr_pending) > keep:
                        tr_pending.pop(0)()

                def qkv_unit(bank, mmv, cos_ap, sin_ap, eps_tile, scale,
                             dst_sl, bi_bank):
                    u = unit_ctr[0] % 2
                    unit_ctr[0] += 1
                    st = nr_state[u]
                    pf, nv = emit_normrope(
                        bank, qscb[u], qsqb[u], qrotb[u], cos_ap, sin_ap,
                        eps_tile, scale, mmv, [st["rot_cons"]], ls=u)
                    bank_free2[bi_bank] = pf

                    def part2(u=u, nv=nv, dst_sl=dst_sl, st=st):
                        for hh in range(4):
                            w.op("tensor",
                                 lambda e, o=pTb[u][:, hh * 128:(hh + 1) * 128],
                                 i=qrotb[u][:, hh * 128:(hh + 1) * 128]:
                                 e.transpose(o, i, idb_ap),
                                 inc=(tT if hh == 3 else None),
                                 waits=[(tV, nv), tr_free[u]])
                        tt = (tT, tT.n)
                        st["rot_cons"] = tt
                        cv = w.op("vector",
                                  lambda e, o=dst_sl, i=pT_r[u][:, 0:4, :]:
                                  e.tensor_copy(o, i),
                                  inc=tV, waits=[tt])
                        tr_free[u] = (tV, cv)
                    tr_pending.append(part2)

                def emit_ln1_chunk(tc):
                    bi = tc % 2
                    v = w.op("gpsimd",
                             lambda e, o=xin[bi][:], i=dt["x_kv"][tc]:
                             e.dma_start(out=o, in_=i),
                             inc=qxin[bi], waits=[xin_free[bi]])
                    cv = emit_ln(xin[bi], x1c[bi], [(qxin[bi], v)],
                                 [x1c_free[bi]], eps_ln, ls=bi)
                    xin_free[bi] = (tV, cv)
                    for g in range(2):
                        for ii in range(8):
                            ic = g * 8 + ii
                            w.op("tensor",
                                 lambda e, o=pTb[g][:, ii * 128:(ii + 1) * 128],
                                 i=x1c[bi][:, ic * 128:(ic + 1) * 128]:
                                 e.transpose(o, i, idb_ap),
                                 inc=(tT if ii == 7 else None),
                                 waits=[(tV, cv), tr_free[g], tbl])
                        tt = tT.n
                        cpv = w.op("vector",
                                   lambda e,
                                   o=x1_r[:, g * 8:(g + 1) * 8, tc * 128:tc * 128 + 128],
                                   i=pT_r[g]:
                                   e.tensor_copy(o, i),
                                   inc=tV, waits=[(tT, tt)])
                        tr_free[g] = (tV, cpv)
                    x1c_free[bi] = tr_free[1]
                    x1_ck[tc] = tr_free[1]

                def slab_ap(mat, oc, hs):
                    return dt[mat + "_t"][oc][:, hs * 4096:(hs + 1) * 4096]

                def emit_qkv_tiles(mat, oc, wA, wB, rdyA, rdyB, tiles):
                    last_mm = 0
                    for tc in tiles:
                        bank = [pA, pB][tc % 2]
                        if mat == "wq":
                            loff = QOFF + tc * 128
                            xr = [x1_ck[7 + tc], x1_ck[8 + tc if 8 + tc < UC else UC - 1]]
                        else:
                            loff = tc * 128
                            xr = [x1_ck[tc]]
                        for ic in range(IC):
                            ws16 = wA if ic < 8 else wB
                            w.op("tensor",
                                 lambda e, o=bank[:],
                                 l=x1fm[:, ic * UT + loff:ic * UT + loff + 128],
                                 r=ws16[:, (ic % 8) * 512:(ic % 8 + 1) * 512],
                                 s=(ic == 0), p=(ic == IC - 1):
                                 e.matmul(o, l, r, start=s, stop=p),
                                 inc=(tT if ic == IC - 1 else None),
                                 waits=xr + [rdyA, rdyB, bank_free2[tc % 2]])
                        mmv = (tT, tT.n)
                        last_mm = tT.n
                        if mat == "wv":
                            cv = w.op("vector",
                                      lambda e,
                                      o=v16[:, tc * 512:(tc + 1) * 512],
                                      i=bank[:]: e.tensor_copy(o, i),
                                      inc=tV, waits=[mmv])
                            bank_free2[tc % 2] = (tV, cv)
                        elif mat == "wk":
                            w.wait("vector", [cs_ld])
                            k_r = k16[:].rearrange("p (h t) -> p h t", h=KVH)
                            qkv_unit(bank, mmv,
                                     cosk_sl[:, tc * 256:(tc + 1) * 256],
                                     sink_sl[:, tc * 256:(tc + 1) * 256],
                                     eps_k, 1.0 / 128.0,
                                     k_r[:, :, tc * 128:tc * 128 + 128],
                                     tc % 2)
                        else:
                            q_r = q16[:].rearrange("p (h t) -> p h t", h=H)
                            qkv_unit(bank, mmv,
                                     cosq_sb[:, tc * 256:(tc + 1) * 256],
                                     sinq_sb[:, tc * 256:(tc + 1) * 256],
                                     eps_q, 1.0,
                                     q_r[:, oc * 4:(oc + 1) * 4, tc * 128:tc * 128 + 128],
                                     tc % 2)
                        flush_tr(keep=1)
                    return last_mm

                wstr.prefetch(slab_ap("wk", 0, 0))
                wstr.prefetch(slab_ap("wk", 0, 1))
                emit_ln1_chunk(0)
                biA, rdyA = wstr.take()
                biB, rdyB = wstr.take()
                wkA, wkB = wstr.w16[biA], wstr.w16[biB]
                lm = 0
                for tc in range(1, UC):
                    emit_ln1_chunk(tc)
                    lm = emit_qkv_tiles("wk", 0, wkA, wkB, rdyA, rdyB, [tc - 1])
                lm = emit_qkv_tiles("wk", 0, wkA, wkB, rdyA, rdyB, [UC - 1])
                wstr.used(biA, (tT, lm))
                wstr.used(biB, (tT, lm))
                rest = [("wv", 0)] + [("wq", oc) for oc in range(4)]
                wstr.prefetch(slab_ap("wv", 0, 0))
                wstr.prefetch(slab_ap("wv", 0, 1))
                for si_, (mat, oc) in enumerate(rest):
                    biA, rdyA = wstr.take()
                    biB, rdyB = wstr.take()
                    wA, wB = wstr.w16[biA], wstr.w16[biB]
                    tiles = range(8) if mat == "wq" else range(UC)
                    lm = emit_qkv_tiles(mat, oc, wA, wB, rdyA, rdyB, tiles)
                    wstr.used(biA, (tT, lm))
                    wstr.used(biB, (tT, lm))
                    if si_ + 1 < len(rest):
                        m2_, o2_ = rest[si_ + 1]
                        wstr.prefetch(slab_ap(m2_, o2_, 0))
                        wstr.prefetch(slab_ap(m2_, o2_, 1))
                flush_tr(keep=0)
            qkv_ready = (tV, tV.n)
            barrier()

            if debug:
                for nm, src_ in [("dbg_q16", q16), ("dbg_k16", k16), ("dbg_v16", v16)]:
                    w.op("gpsimd", lambda e, o=dbg[nm], i=src_[:]:
                         e.dma_start(out=o, in_=i), inc=qX, waits=[qkv_ready])

            pcd = contextlib.ExitStack()
            with pcd:
                sb3 = mksb(pcd)
                exp16 = [sb3(f"exp16_{i}", (128, 512), BF16) for i in range(6)]
                rcp = [sb3(f"rcp{i}", (1, 512)) for i in range(2)]
                bcf = [sb3(f"bcf{i}", (128, 512)) for i in range(2)]
                y16 = sb3("y16", (128, H * TOK), BF16)
                attn16 = [sb3(f"attn16_{i}", (128, D), BF16) for i in range(8)]
                ln2c = [sb3(f"ln2c{i}", (128, D), BF16) for i in range(2)]
                ln2c_free = [NONE_T, NONE_T]
                qL = [mksem("qL0", True), mksem("qL1", True)]
                wstr2 = Stream(sb3, "wB", 4096, 2048, cast_eng="vector", nbuf=3)
                pSu = [pcd.enter_context(nc.psum_tensor(f"pSu{i}", [128, 512], F32))
                       for i in range(2)]
                cmask16_sb = sb3("cmask16_sb", (128, 4 * 512), BF16)
                qTb = mksem("qTb", True)
                w.op("gpsimd", lambda e, o=cmask16_sb[:], i=dt["cmask16"]:
                     e.dma_start(out=o, in_=i), inc=qTb)
                cm_ld = (qTb, qTb.n)
                w.wait("tensor", [cm_ld])

                scb = [[pA, pB], [pC, pD]]
                avb = [pS, pR]
                av_free = [NONE_T, NONE_T]
                sc_free = [[NONE_T, NONE_T], [NONE_T, NONE_T]]
                exp_free = [NONE_T] * 6
                sum_free = [NONE_T, NONE_T]
                pending = [None, None]

                def emit_tail(u):
                    h, j, rcpv = pending[u]
                    pending[u] = None
                    sbank = scb[u][1]
                    w.op("tensor",
                         lambda e, o=sbank[:], l=onesr_sb[:], r=rcp[u][:]:
                         e.matmul(o, l, r, start=True, stop=True),
                         inc=tT, waits=[rcpv, sc_free[u][1]])
                    bct = (tT, tT.n)
                    cv2 = w.op("vector",
                               lambda e, o=bcf[u][:], i=sbank[:]:
                               e.tensor_copy(o, i),
                               inc=tV, waits=[bct])
                    sc_free[u][1] = (tV, cv2)
                    yv = w.op("vector",
                              lambda e, o=y16[:, h * TOK + j * CH:h * TOK + (j + 1) * CH],
                              b2=bcf[u][:]:
                              e.tensor_tensor(o, o, b2, ALU.mult),
                              inc=tV, waits=[(tV, cv2)])
                    return yv

                for j in range(NSLOT):
                    for hp in range(H // 2):
                        exvq = {}
                        for kc in range(WC + 1):
                            if kc < WC:
                                uck = 4 * j + kc
                                for u in range(2):
                                    h = 2 * hp + u
                                    kvh = h // 4
                                    ei = u * 3 + kc % 3
                                    bank = scb[u][kc % 2]
                                    w.op("tensor",
                                         lambda e, o=bank[:],
                                         l=k16[:, kvh * UT + uck * 128:kvh * UT + uck * 128 + 128],
                                         r=q16[:, h * TOK + j * CH:h * TOK + (j + 1) * CH],
                                         p=(kc < 8):
                                         e.matmul(o, l, r, start=True, stop=p),
                                         inc=(tT if kc < 8 else None),
                                         waits=[qkv_ready, sc_free[u][kc % 2], tbl])
                                    if kc >= 8:
                                        w.op("tensor",
                                             lambda e, o=bank[:], l=idb_ap,
                                             r=cmask16_sb[:, (kc - 8) * 512:(kc - 7) * 512]:
                                             e.matmul(o, l, r, start=False, stop=True),
                                             inc=tT)
                                    mmv = (tT, tT.n)
                                    ev = w.op("scalar",
                                              lambda e, o=exp16[ei][:], i=bank[:],
                                              b=valid_sb[:, uck:uck + 1]:
                                              e.activation(o, i, AF.Exp, bias=b, scale=1.0),
                                              inc=tS, waits=[mmv, exp_free[ei], tbl])
                                    exvq[(u, kc)] = (tS, ev)
                                    sc_free[u][kc % 2] = (tS, ev)
                            if kc == 2 and pending[0] is not None:
                                emit_tail(0)
                                emit_tail(1)
                            if kc >= 1:
                                kd = kc - 1
                                uckd = 4 * j + kd
                                for u in range(2):
                                    h = 2 * hp + u
                                    kvh = h // 4
                                    ei = u * 3 + kd % 3
                                    w.op("tensor",
                                         lambda e, o=pSu[u][0:1, :], l=onesc_sb[:],
                                         r=exp16[ei][:],
                                         s=(kd == 0), p=(kd == WC - 1):
                                         e.matmul(o, l, r, start=s, stop=p),
                                         inc=tT,
                                         waits=[exvq.pop((u, kd))] + ([sum_free[u]] if kd == 0 else []))
                                    w.op("tensor",
                                         lambda e, o=avb[u][:],
                                         l=v16[:, uckd * 512 + kvh * 128:uckd * 512 + kvh * 128 + 128],
                                         r=exp16[ei][:],
                                         s=(kd == 0), p=(kd == WC - 1):
                                         e.matmul(o, l, r, start=s, stop=p),
                                         inc=tT,
                                         waits=([av_free[u]] if kd == 0 else []))
                                    exp_free[ei] = (tT, tT.n)
                        av_done = (tT, tT.n)
                        for u in range(2):
                            h = 2 * hp + u
                            cv = w.op("vector",
                                      lambda e, o=y16[:, h * TOK + j * CH:h * TOK + (j + 1) * CH],
                                      i=avb[u][:]: e.tensor_copy(o, i),
                                      inc=tV, waits=[av_done])
                            av_free[u] = (tV, cv)
                            lv = w.op("scalar",
                                      lambda e, o=rcp[u][:], i=pSu[u][0:1, :]:
                                      e.activation(o, i, AF.Ln, scale=1.0),
                                      inc=tS, waits=[av_done])
                            sum_free[u] = (tS, lv)
                            rv = w.op("scalar",
                                      lambda e, o=rcp[u][:]:
                                      e.activation(o, o, AF.Exp, scale=-1.0),
                                      inc=tS, waits=[(tS, lv)])
                            pending[u] = (h, j, (tS, rv))
                emit_tail(0)
                emit_tail(1)
                y_ready = (tV, tV.n)

                if debug:
                    w.op("gpsimd", lambda e, o=dbg["dbg_y16"], i=y16[:]:
                         e.dma_start(out=o, in_=i), inc=qX, waits=[y_ready])

                attn_parts = [0] * 8
                ob_free = [NONE_T, NONE_T]
                wstr2.prefetch(dt["wo_t"][0][:, 0:4096])
                wstr2.prefetch(dt["wo_t"][0][:, 4096:8192])
                for oc in range(4):
                    biA2, rdyA2 = wstr2.take()
                    biB2, rdyB2 = wstr2.take()
                    woA, woB = wstr2.w16[biA2], wstr2.w16[biB2]
                    last_mm = 0
                    for tq in range(8):
                        bank = [pA, pB][tq % 2]
                        for ic in range(IC):
                            ws16 = woA if ic < 8 else woB
                            w.op("tensor",
                                 lambda e, o=bank[:],
                                 l=y16[:, ic * TOK + tq * 128:ic * TOK + tq * 128 + 128],
                                 r=ws16[:, (ic % 8) * 512:(ic % 8 + 1) * 512],
                                 s=(ic == 0), p=(ic == IC - 1):
                                 e.matmul(o, l, r, start=s, stop=p),
                                 inc=(tT if ic == IC - 1 else None),
                                 waits=[y_ready, rdyA2, rdyB2, ob_free[tq % 2]])
                        mmv = (tT, tT.n)
                        last_mm = tT.n
                        cv = w.op("vector",
                                  lambda e, o=attn16[tq][:, oc * 512:(oc + 1) * 512],
                                  i=bank[:]: e.tensor_copy(o, i),
                                  inc=tV, waits=[mmv])
                        ob_free[tq % 2] = (tV, cv)
                        attn_parts[tq] = cv
                    wstr2.used(biA2, (tT, last_mm))
                    wstr2.used(biB2, (tT, last_mm))
                    if oc + 1 < 4:
                        wstr2.prefetch(dt["wo_t"][oc + 1][:, 0:4096])
                        wstr2.prefetch(dt["wo_t"][oc + 1][:, 4096:8192])
                for tq in range(8):
                    bi = tq % 2
                    if debug:
                        w.op("gpsimd",
                             lambda e, o=dbg["dbg_attn"][tq], i=attn16[tq][:]:
                             e.dma_start(out=o, in_=i),
                             inc=qX, waits=[(tV, attn_parts[tq])])
                    cv = emit_ln(attn16[tq], ln2c[bi], [(tV, attn_parts[tq])],
                                 [ln2c_free[bi]], eps_ln, ls=bi)
                    sv = w.op("gpsimd",
                              lambda e, o=ln2a_d[tq], i=ln2c[bi][:]:
                              e.dma_start(out=o, in_=i),
                              inc=qL[bi], waits=[(tV, cv)])
                    ln2c_free[bi] = (qL[bi], sv)
                    ln2_stores += 1
                    if debug:
                        w.op("gpsimd",
                             lambda e, o=dbg["dbg_ln2a"][tq], i=ln2c[bi][:]:
                             e.dma_start(out=o, in_=i), inc=qX)
        ln2_done2 = [(qL[0], qL[0].n), (qL[1], qL[1].n)]

        p2 = contextlib.ExitStack()
        with p2:
            sbf = mksb(p2)
            idf_sb = sbf("idf_sb", (128, 128))
            gate_sb = sbf("gate_sb", (128, IC * 9))
            qTc = mksem("qTc", True)
            w.op("gpsimd", lambda e, o=idf_sb[:], i=dt["idf"]:
                 e.dma_start(out=o, in_=i), inc=qTc)
            w.op("gpsimd", lambda e, o=gate_sb[:], i=dt["gate_pack"]:
                 e.dma_start(out=o, in_=i), inc=qTc)
            p2tbl = (qTc, qTc.n)
            w.wait("tensor", [p2tbl])
            w.wait("vector", [p2tbl])
            idf_ap = idf_sb[:]
            x3fm = sbf("x3fm", (128, IC * TOK), BF16)
            moe = sbf("moe", (128, NTC * D), BF16)
            comb = sbf("comb", (128, NTC * 8))
            sgv = sbf("sgv", (128, NTC))
            x3_r = x3fm[:].rearrange("p (ic t) -> p ic t", ic=IC)

            barrier()
            p2a = contextlib.ExitStack()
            with p2a:
                sba = mksb(p2a)
                xin2 = [sba(f"xin2_{i}", (128, D)) for i in range(2)]
                xin2_free = [NONE_T, NONE_T]
                l2c = [sba(f"l2c{i}", (128, D), BF16) for i in range(2)]
                l2c_free = [NONE_T, NONE_T]
                qxin2 = [mksem("qxin2_0", True), mksem("qxin2_1", True)]
                ql2c = [mksem("ql2c_0", True), mksem("ql2c_1", True)]
                x2c2 = [sba("x2c", (128, D)) for _ in range(2)]
                x3c2 = [sba("x3c", (128, D)) for _ in range(2)]
                xf322 = [sba("xf32", (128, D)) for _ in range(2)]
                r9 = sba("r9", (9, 128))
                pr9 = sba("pr9", (128, 9))
                prb = sba("prb", (128, 8))
                m1 = sba("m1", (128, 1))
                m2 = sba("m2", (128, 1))
                sm = sba("sm", (128, 1))
                eqb = sba("eqb", (128, 8))

                x2c_free = [NONE_T, NONE_T]
                x3c_free = [NONE_T, NONE_T]
                xf32_free = [NONE_T, NONE_T]
                pR_free = NONE_T
                pDt_free = NONE_T
                r9_cons = NONE_T
                pr9_cons = NONE_T
                trb_free = [NONE_T, NONE_T]
                for tc in range(NTC):
                    bi = tc % 2
                    x2c, x3c, xf32 = x2c2[bi], x3c2[bi], xf322[bi]
                    v1 = w.op("gpsimd",
                              lambda e, o=xin2[bi][:], i=dt["x_own"][tc]:
                              e.dma_start(out=o, in_=i),
                              inc=qxin2[bi], waits=[xin2_free[bi]] + ln2_done2)
                    v2 = w.op("gpsimd",
                              lambda e, o=l2c[bi][:], i=ln2a_d[tc]:
                              e.dma_start(out=o, in_=i),
                              inc=ql2c[bi], waits=[l2c_free[bi]])
                    cv = w.op("vector",
                              lambda e, o=x2c[:], a=xin2[bi][:], b=l2c[bi][:]:
                              e.tensor_tensor(o, a, b, ALU.add),
                              inc=tV, waits=[(qxin2[bi], v1), (ql2c[bi], v2), x2c_free[bi]])
                    xin2_free[bi] = (tV, cv)
                    l2c_free[bi] = (tV, cv)
                    lv = emit_ln(x2c, x3c, [(tV, cv)], [x3c_free[bi]], eps_ln, ls=bi)
                    x2c_free[bi] = (tV, lv)
                    grp_cp = 0
                    for g in range(4):
                        bank = [pA, pB][g % 2]
                        for ii in range(4):
                            ic = g * 4 + ii
                            w.op("tensor",
                                 lambda e, o=bank[:, ii * 128:(ii + 1) * 128],
                                 i=x3c[:, ic * 128:(ic + 1) * 128]:
                                 e.transpose(o, i, idf_ap),
                                 inc=(tT if ii == 3 else None),
                                 waits=[(tV, lv), trb_free[g % 2], tbl])
                        tt = (tT, tT.n)
                        w.op("vector",
                             lambda e,
                             o=x3_r[:, g * 4:(g + 1) * 4, tc * 128:tc * 128 + 128],
                             i=bank[:].rearrange("p (ii t) -> p ii t", ii=4):
                             e.tensor_copy(o, i),
                             waits=[tt])
                        cv2 = w.op("vector",
                                   lambda e, o=xf32[:, g * 512:(g + 1) * 512], i=bank[:]:
                                   e.tensor_copy(o, i),
                                   inc=tV, waits=[tt, xf32_free[bi]])
                        trb_free[g % 2] = (tV, cv2)
                        grp_cp = cv2
                    x3c_free[bi] = (tV, grp_cp)
                    xf_rdy = (tV, grp_cp)
                    for ic in range(IC):
                        w.op("tensor",
                             lambda e, o=pR[0:9, 0:128],
                             l=gate_sb[:, ic * 9:(ic + 1) * 9],
                             r=xf32[:, ic * 128:(ic + 1) * 128],
                             s=(ic == 0), p=(ic == IC - 1):
                             e.matmul(o, l, r, start=s, stop=p),
                             inc=(tT if ic == IC - 1 else None),
                             waits=[xf_rdy, tbl, pR_free])
                    rmm = (tT, tT.n)
                    xf32_free[bi] = rmm
                    cvr = w.op("vector", lambda e, o=r9[:], i=pR[0:9, 0:128]:
                               e.tensor_copy(o, i),
                               inc=tV, waits=[rmm, r9_cons])
                    pR_free = (tV, cvr)
                    w.op("tensor",
                         lambda e, o=pD[0:128, 0:9], i=r9[:], d9=idf_sb[0:9, 0:9]:
                         e.transpose(o, i, d9),
                         inc=tT, waits=[(tV, cvr), pDt_free])
                    r9_cons = (tT, tT.n)
                    cvp = w.op("vector", lambda e, o=pr9[:], i=pD[0:128, 0:9]:
                               e.tensor_copy(o, i),
                               inc=tV, waits=[r9_cons, pr9_cons])
                    pDt_free = (tV, cvp)
                    w.op("vector", lambda e, o=m1[:], i=pr9[:, 0:8]:
                         e.reduce_max(o, i, axis=AX.X),
                         inc=tV, waits=[(tV, cvp), (tS, tS.n)])
                    w.op("vector", lambda e, o=prb[:], i=pr9[:, 0:8], s=m1[:]:
                         e.tensor_scalar(o, i, s, None, ALU.subtract),
                         inc=tV, waits=[(tV, tV.n)])
                    av_ = tV.n
                    w.op("scalar", lambda e, o=prb[:]:
                         e.activation(o, o, AF.Exp, scale=1.0),
                         inc=tS, waits=[(tV, av_)])
                    ev = (tS, tS.n)
                    w.op("vector", lambda e, o=sm[:], i=prb[:]:
                         e.reduce_sum(o, i, axis=AX.X),
                         inc=tV, waits=[ev])
                    w.op("vector", lambda e, o=sm[:]: e.reciprocal(o, o),
                         inc=tV, waits=[(tV, tV.n)])
                    w.op("vector", lambda e, o=prb[:], s=sm[:]:
                         e.tensor_scalar(o, o, s, None, ALU.mult),
                         inc=tV, waits=[(tV, tV.n)])
                    w.op("vector", lambda e, o=m1[:], i=prb[:]:
                         e.reduce_max(o, i, axis=AX.X),
                         inc=tV, waits=[(tV, tV.n)])
                    w.op("vector", lambda e, o=eqb[:], i=prb[:], s=m1[:]:
                         e.tensor_scalar(o, i, s, None, ALU.is_equal),
                         inc=tV, waits=[(tV, tV.n)])
                    w.op("vector", lambda e, o=eqb[:], i=prb[:]:
                         e.scalar_tensor_tensor(o, o, 1e30, i, ALU.mult, ALU.subtract),
                         inc=tV, waits=[(tV, tV.n)])
                    w.op("vector", lambda e, o=m2[:], i=eqb[:]:
                         e.tensor_reduce(o, i, axis=AX.X, op=ALU.min, negate=True),
                         inc=tV, waits=[(tV, tV.n)])
                    w.op("vector", lambda e, o=eqb[:], i=prb[:], s=m2[:]:
                         e.tensor_scalar(o, i, s, None, ALU.is_ge),
                         inc=tV, waits=[(tV, tV.n)])
                    w.op("vector",
                         lambda e, o=comb[:, tc * 8:(tc + 1) * 8], a=prb[:], b=eqb[:]:
                         e.tensor_tensor(o, a, b, ALU.mult),
                         inc=tV, waits=[(tV, tV.n)])
                    w.op("scalar",
                         lambda e, o=m1[:], i=pr9[:, 8:9]:
                         e.activation(o, i, AF.Tanh, scale=0.5),
                         inc=tS, waits=[(tV, tV.n)])
                    pr9_cons = (tS, tS.n)
                    w.op("vector",
                         lambda e, o=sgv[:, tc:tc + 1], i=m1[:]:
                         e.tensor_scalar(o, i, 0.5, 0.5, ALU.mult, ALU.add),
                         inc=tV, waits=[(tS, tS.n)])
                x3_ready = (tV, tV.n)
                comb_ready = (tV, tV.n)

            if debug:
                for nm, src in [("dbg_x3fm", x3fm), ("dbg_comb", comb), ("dbg_sg", sgv)]:
                    w.op("gpsimd", lambda e, o=dbg[nm], i=src[:]:
                         e.dma_start(out=o, in_=i), inc=qX, waits=[x3_ready])

            barrier()
            p2b = contextlib.ExitStack()
            with p2b:
                sbb = mksb(p2b)
                he2 = [sbb("he", (128, 8 * TOK), BF16) for _ in range(2)]
                silu_f = [sbb(f"silu{i}", (128, 512)) for i in range(2)]
                silu_free = [NONE_T, NONE_T]
                gS = Stream(sbb, "wg", 2048, 2048, cast_eng="vector")
                uS = Stream(sbb, "wu", 2048, 2048, cast_eng="scalar")
                dS = Stream(sbb, "wd", 4096, 2048, cast_eng="scalar")

                gb = [pA, pB]
                ub = [pC, pD]
                db = [pS, pR]
                gb_free = [NONE_T, NONE_T]
                ub_free = [NONE_T, NONE_T]
                db_free = [NONE_T, NONE_T]
                he_free = [NONE_T, NONE_T]
                moe_parts = NONE_T
                for pf in range(2):
                    gS.prefetch(dt["wg_t"][pf])
                    uS.prefetch(dt["wu_t"][pf])
                dS.prefetch(dt["wd_t"][0])
                dS.prefetch(dt["wd_t"][1])
                gu_next = 2
                d_next = 2
                for ee in range(E):
                    he = he2[ee % 2]
                    he_done = 0
                    for ici in range(8):
                        bg, grdy = gS.take()
                        bu, urdy = uS.take()
                        g16, u16 = gS.w16[bg], uS.w16[bu]
                        lastg = lastu = 0
                        for t in range(2):
                            gi = (ici * 2 + t) % 2
                            for ic in range(IC):
                                w.op("tensor",
                                     lambda e, o=gb[gi][:],
                                     l=g16[:, ic * 128:(ic + 1) * 128],
                                     r=x3fm[:, ic * TOK + t * 512:ic * TOK + (t + 1) * 512],
                                     s=(ic == 0), p=(ic == IC - 1):
                                     e.matmul(o, l, r, start=s, stop=p),
                                     inc=(tT if ic == IC - 1 else None),
                                     waits=[x3_ready, grdy, gb_free[gi]])
                            gmm = (tT, tT.n)
                            lastg = tT.n
                            for ic in range(IC):
                                w.op("tensor",
                                     lambda e, o=ub[gi][:],
                                     l=u16[:, ic * 128:(ic + 1) * 128],
                                     r=x3fm[:, ic * TOK + t * 512:ic * TOK + (t + 1) * 512],
                                     s=(ic == 0), p=(ic == IC - 1):
                                     e.matmul(o, l, r, start=s, stop=p),
                                     inc=(tT if ic == IC - 1 else None),
                                     waits=[urdy, ub_free[gi]])
                            umm = (tT, tT.n)
                            lastu = tT.n
                            si = gi
                            sv = w.op("scalar",
                                      lambda e, o=silu_f[si][:], i=gb[gi][:]:
                                      e.activation(o, i, AF.Silu, scale=1.0),
                                      inc=tS, waits=[gmm, silu_free[si]])
                            gb_free[gi] = (tS, sv)
                            hv = w.op("vector",
                                      lambda e,
                                      o=he[:, ici * 1024 + t * 512:ici * 1024 + (t + 1) * 512],
                                      a=silu_f[si][:], b=ub[gi][:]:
                                      e.tensor_tensor(o, a, b, ALU.mult),
                                      inc=tV, waits=[(tS, sv), umm, he_free[ee % 2]])
                            silu_free[si] = (tV, hv)
                            ub_free[gi] = (tV, hv)
                            he_done = hv
                        gS.used(bg, (tT, lastg))
                        uS.used(bu, (tT, lastu))
                        if gu_next < E * 8:
                            gS.prefetch(dt["wg_t"][gu_next])
                            uS.prefetch(dt["wu_t"][gu_next])
                            gu_next += 1
                    he_ready = (tV, he_done)
                    if debug and ee == 0:
                        w.op("gpsimd", lambda e, o=dbg["dbg_he"], i=he[:]:
                             e.dma_start(out=o, in_=i), inc=qX, waits=[he_ready])
                    lastd_all = 0
                    for oc in range(4):
                        bd, drdy = dS.take()
                        d16 = dS.w16[bd]
                        lastd = 0
                        for tc in range(NTC):
                            di = tc % 2
                            for ici in range(8):
                                w.op("tensor",
                                     lambda e, o=db[di][:],
                                     l=he[:, ici * 1024 + tc * 128:ici * 1024 + tc * 128 + 128],
                                     r=d16[:, ici * 512:(ici + 1) * 512],
                                     s=(ici == 0), p=(ici == 7):
                                     e.matmul(o, l, r, start=s, stop=p),
                                     inc=(tT if ici == 7 else None),
                                     waits=[he_ready, drdy, db_free[di]])
                            mmv = (tT, tT.n)
                            lastd = tT.n
                            dst = moe[:, tc * D + oc * 512:tc * D + (oc + 1) * 512]
                            if ee == 0:
                                cv = w.op("vector",
                                          lambda e, o=dst, i=db[di][:],
                                          s=comb[:, tc * 8 + ee:tc * 8 + ee + 1]:
                                          e.tensor_scalar(o, i, s, None, ALU.mult),
                                          inc=tV, waits=[mmv, comb_ready])
                            else:
                                cv = w.op("vector",
                                          lambda e, o=dst, i=db[di][:],
                                          s=comb[:, tc * 8 + ee:tc * 8 + ee + 1]:
                                          e.scalar_tensor_tensor(o, i, s, o,
                                                                 ALU.mult, ALU.add),
                                          inc=tV, waits=[mmv, comb_ready, moe_parts])
                            db_free[di] = (tV, cv)
                        dS.used(bd, (tT, lastd))
                        if d_next < E * 4:
                            dS.prefetch(dt["wd_t"][d_next])
                            d_next += 1
                        lastd_all = lastd
                    moe_parts = (tV, tV.n)
                    he_free[ee % 2] = (tT, lastd_all)

            barrier()
            p2c = contextlib.ExitStack()
            with p2c:
                sbc = mksb(p2c)
                sh2 = [sbc("sh", (128, 32 * 512), BF16) for _ in range(2)]
                silu_f = [sbc(f"silu2_{i}", (128, 512)) for i in range(2)]
                silu_free = [NONE_T, NONE_T]
                gS = Stream(sbc, "wsg", 2048, 1024, cast_eng="vector")
                uS = Stream(sbc, "wsu", 2048, 1024, cast_eng="scalar")
                dS = Stream(sbc, "wsd", 4096, 1024, cast_eng="scalar")

                pA_free = NONE_T
                pB_free = NONE_T
                banks4 = [pC, pD, pS, pR]
                psum4_free = [NONE_T] * 4
                sh_cons = NONE_T
                moe_sh = moe_parts
                for t in range(2):
                    sh = sh2[t % 2]
                    sh_done = 0
                    for ici in range(32):
                        bg, grdy = gS.load(dt["wsg_t"][ici])
                        bu, urdy = uS.load(dt["wsu_t"][ici])
                        g16, u16 = gS.w16[bg], uS.w16[bu]
                        for ic in range(IC):
                            w.op("tensor",
                                 lambda e, o=pA[:],
                                 l=g16[:, ic * 128:(ic + 1) * 128],
                                 r=x3fm[:, ic * TOK + t * 512:ic * TOK + (t + 1) * 512],
                                 s=(ic == 0), p=(ic == IC - 1):
                                 e.matmul(o, l, r, start=s, stop=p),
                                 inc=(tT if ic == IC - 1 else None),
                                 waits=[x3_ready, grdy, pA_free])
                        gmm = (tT, tT.n)
                        lastg = tT.n
                        for ic in range(IC):
                            w.op("tensor",
                                 lambda e, o=pB[:],
                                 l=u16[:, ic * 128:(ic + 1) * 128],
                                 r=x3fm[:, ic * TOK + t * 512:ic * TOK + (t + 1) * 512],
                                 s=(ic == 0), p=(ic == IC - 1):
                                 e.matmul(o, l, r, start=s, stop=p),
                                 inc=(tT if ic == IC - 1 else None),
                                 waits=[urdy, pB_free])
                        umm = (tT, tT.n)
                        si = ici % 2
                        sv = w.op("scalar",
                                  lambda e, o=silu_f[si][:], i=pA[:]:
                                  e.activation(o, i, AF.Silu, scale=1.0),
                                  inc=tS, waits=[gmm, silu_free[si]])
                        pA_free = (tS, sv)
                        hv = w.op("vector",
                                  lambda e, o=sh[:, ici * 512:(ici + 1) * 512],
                                  a=silu_f[si][:], b=pB[:]:
                                  e.tensor_tensor(o, a, b, ALU.mult),
                                  inc=tV, waits=[(tS, sv), umm, sh_cons])
                        silu_free[si] = (tV, hv)
                        pB_free = (tV, hv)
                        sh_done = hv
                        gS.used(bg, (tT, lastg))
                        uS.used(bu, (tT, tT.n))
                    sh_ready = (tV, sh_done)
                    for oc in range(4):
                        mm_last = [0, 0, 0, 0]
                        for icg in range(4):
                            bd, drdy = dS.load(dt["wsd_t"][oc * 4 + icg])
                            d16 = dS.w16[bd]
                            for tl in range(4):
                                pb = banks4[tl]
                                for icl in range(8):
                                    ici = icg * 8 + icl
                                    w.op("tensor",
                                         lambda e, o=pb[:],
                                         l=sh[:, ici * 512 + tl * 128:ici * 512 + tl * 128 + 128],
                                         r=d16[:, icl * 512:(icl + 1) * 512],
                                         s=(icg == 0 and icl == 0),
                                         p=(icg == 3 and icl == 7):
                                         e.matmul(o, l, r, start=s, stop=p),
                                         inc=(tT if icl == 7 else None),
                                         waits=[sh_ready, drdy, psum4_free[tl]])
                                mm_last[tl] = tT.n
                            dS.used(bd, (tT, max(mm_last)))
                        for tl in range(4):
                            tc = t * 4 + tl
                            dst = moe[:, tc * D + oc * 512:tc * D + (oc + 1) * 512]
                            cv = w.op("vector",
                                      lambda e, o=dst, i=banks4[tl][:],
                                      s=sgv[:, tc:tc + 1]:
                                      e.scalar_tensor_tensor(o, i, s, o,
                                                             ALU.mult, ALU.add),
                                      inc=tV,
                                      waits=[(tT, mm_last[tl]), moe_sh, comb_ready])
                            psum4_free[tl] = (tV, cv)
                    sh_cons = (tT, tT.n)
                moe_ready = (tV, tV.n)

            if debug:
                for tc in range(NTC):
                    w.op("gpsimd",
                         lambda e, o=dbg["dbg_acc"][tc], i=moe[:, tc * D:(tc + 1) * D]:
                         e.dma_start(out=o, in_=i),
                         inc=qX, waits=[moe_ready])

            barrier()
            p2d = contextlib.ExitStack()
            with p2d:
                sbd = mksb(p2d)
                xin3 = [sbd(f"xin3_{i}", (128, D)) for i in range(2)]
                xin3_free = [NONE_T, NONE_T]
                l2c2 = [sbd(f"l2c2_{i}", (128, D), BF16) for i in range(2)]
                l2c2_free = [NONE_T, NONE_T]
                qxin3 = [mksem("qxin3_0", True), mksem("qxin3_1", True)]
                ql2c2 = [mksem("ql2c2_0", True), mksem("ql2c2_1", True)]
                qP = [mksem("qP0", True), mksem("qP1", True)]
                t42 = [sbd(f"t4_{i}", (128, D)) for i in range(2)]
                outc = [sbd(f"outc{i}", (128, D)) for i in range(2)]
                outc_free = [NONE_T, NONE_T]
                t4_free = [NONE_T, NONE_T]

                for tc in range(NTC):
                    bi = tc % 2
                    v1 = w.op("gpsimd",
                              lambda e, o=xin3[bi][:], i=dt["x_own"][tc]:
                              e.dma_start(out=o, in_=i),
                              inc=qxin3[bi], waits=[xin3_free[bi]])
                    v2 = w.op("gpsimd",
                              lambda e, o=l2c2[bi][:], i=ln2a_d[tc]:
                              e.dma_start(out=o, in_=i),
                              inc=ql2c2[bi], waits=[l2c2_free[bi]])
                    t4 = t42[bi]
                    lv = emit_ln(moe[:, tc * D:(tc + 1) * D], t4, [moe_ready],
                                 [t4_free[bi]], eps_ln, ls=tc % 2)
                    ov = w.op("vector",
                              lambda e, o=outc[bi][:], a=xin3[bi][:], b=l2c2[bi][:]:
                              e.tensor_tensor(o, a, b, ALU.add),
                              inc=tV, waits=[(qxin3[bi], v1), (ql2c2[bi], v2), outc_free[bi]])
                    ov2 = w.op("vector",
                               lambda e, o=outc[bi][:], b=t4[:]:
                               e.tensor_tensor(o, o, b, ALU.add),
                               inc=tV, waits=[(tV, lv), (tV, ov)])
                    t4_free[bi] = (tV, ov2)
                    xin3_free[bi] = (tV, ov2)
                    l2c2_free[bi] = (tV, ov2)
                    sv = w.op("gpsimd",
                              lambda e, o=out_d[tc], i=outc[bi][:]:
                              e.dma_start(out=o, in_=i),
                              inc=qP[bi], waits=[(tV, ov2)])
                    outc_free[bi] = (qP[bi], sv)

        w.wait("gpsimd", [(s, s.n) for s in ALLSEMS if s.dma])

        with nc.Block() as blk:
            @blk.sync
            def _(eng):
                for f in w.prog["sync"]:
                    f(eng)

            @blk.tensor
            def _(eng):
                for f in w.prog["tensor"]:
                    f(eng)

            @blk.vector
            def _(eng):
                for f in w.prog["vector"]:
                    f(eng)

            @blk.scalar
            def _(eng):
                for f in w.prog["scalar"]:
                    f(eng)

            @blk.gpsimd
            def _(eng):
                for f in w.prog["gpsimd"]:
                    f(eng)

    return nc


def kernel(**inputs):
    in_maps = host_prepare(inputs)
    nc = build_nc()
    res = run_bass_kernel_spmd(nc, in_maps, core_ids=list(range(NCORE)))
    out = np.empty((B, T, D), dtype=np.float32)
    for core in range(NCORE):
        b, half = core // 2, core % 2
        out[b, half * TOK:(half + 1) * TOK] = res.results[core]["out"].reshape(TOK, D)
    return out


# version 19
# speedup vs baseline: 1.0408x; 1.0408x over previous
import contextlib

import numpy as np
import ml_dtypes

import concourse.bass as bass
import concourse.mybir as mybir
from concourse.bass_utils import run_bass_kernel_spmd

F32 = mybir.dt.float32
BF16 = mybir.dt.bfloat16
AF = mybir.ActivationFunctionType
ALU = mybir.AluOpType
AX = mybir.AxisListType
BF = ml_dtypes.bfloat16

B, T, D = 4, 2048, 2048
H, KVH, DH = 16, 4, 128
IC = 16
CH = 512
WT = 1536
WC = 12
QOFF = 1023
NSLOT = 2
HT = 768
HC = 6
E, IEXP, ISH = 8, 1024, 4096
TOK, NTC = 1024, 8
UT, UC = 2048, 16
NCORE = 8
THETA = 10000.0


_PERM = np.empty(128, dtype=np.int64)
_PERM[:64] = np.arange(64) * 2
_PERM[64:] = np.arange(64) * 2 + 1


def _perm_heads(w, nheads):
    return w.reshape(nheads, 128, D)[:, _PERM, :].reshape(nheads * 128, D)


def _tile_rhs_out512(wT):
    nin, nout = wT.shape
    a = wT.reshape(nin // 128, 128, nout // 512, 512)
    return np.ascontiguousarray(
        a.transpose(2, 1, 0, 3).reshape(nout // 512, 128, (nin // 128) * 512))


def _tile_gu(w):
    I_, Din = w.shape
    a = w.reshape(I_ // 128, 128, Din // 128, 128)
    return np.ascontiguousarray(
        a.transpose(0, 3, 2, 1).reshape(I_ // 128, 128, (Din // 128) * 128))


_ROPE_CACHE = {}


def _rope_base():
    if "cs" in _ROPE_CACHE:
        return _ROPE_CACHE["cs"]
    import ctypes
    libm = ctypes.CDLL("libm.so.6")
    libm.cosf.restype = ctypes.c_float
    libm.cosf.argtypes = [ctypes.c_float]
    libm.sinf.restype = ctypes.c_float
    libm.sinf.argtypes = [ctypes.c_float]
    af = (np.arange(0, DH, 2, dtype=np.float32) / np.float32(DH)).astype(np.float32)
    p = np.power(np.float64(THETA), af.astype(np.float64)).astype(np.float32)
    inv = (np.float32(1.0) / p).astype(np.float32)
    ang = (np.arange(T, dtype=np.float32)[:, None] * inv[None, :]).astype(np.float32)
    flat = ang.ravel()
    cos = np.fromiter((libm.cosf(float(v)) for v in flat), np.float32,
                      count=flat.size).reshape(ang.shape)
    sin = np.fromiter((libm.sinf(float(v)) for v in flat), np.float32,
                      count=flat.size).reshape(ang.shape)
    _ROPE_CACHE["cs"] = (cos, sin)
    return cos, sin


def _rope_tables(positions, reps):
    cos_t, sin_t = _rope_base()
    idx = np.clip(positions.astype(np.int64), 0, T - 1)
    return (np.tile(cos_t[idx], (1, reps)), np.tile(sin_t[idx], (1, reps)))


def host_prepare(inputs):
    x = np.asarray(inputs["L0_attn_x"], dtype=np.float32)
    wq = np.asarray(inputs["L0_attn_wq"], dtype=np.float32)
    wk = np.asarray(inputs["L0_attn_wk"], dtype=np.float32)
    wv = np.asarray(inputs["L0_attn_wv"], dtype=np.float32)
    wo = np.asarray(inputs["L0_attn_wo"], dtype=np.float32)
    gw = np.asarray(inputs["L0_ffn_gate_w"], dtype=np.float32)
    egw = np.asarray(inputs["L0_ffn_expert_gate_w"], dtype=np.float32)
    euw = np.asarray(inputs["L0_ffn_expert_up_w"], dtype=np.float32)
    edw = np.asarray(inputs["L0_ffn_expert_down_w"], dtype=np.float32)
    sgw = np.asarray(inputs["L0_ffn_shared_gate_w"], dtype=np.float32)
    suw = np.asarray(inputs["L0_ffn_shared_up_w"], dtype=np.float32)
    sdw = np.asarray(inputs["L0_ffn_shared_down_w"], dtype=np.float32)
    seg = np.asarray(inputs["L0_ffn_shared_expert_gate_w"], dtype=np.float32)

    shared = {}
    shared["wq_t"] = _tile_rhs_out512(np.ascontiguousarray(_perm_heads(wq, H).T))
    shared["wk_t"] = _tile_rhs_out512(np.ascontiguousarray(_perm_heads(wk, KVH).T))
    shared["wv_t"] = _tile_rhs_out512(np.ascontiguousarray(wv.T))
    shared["wo_t"] = _tile_rhs_out512(np.ascontiguousarray(wo.T))
    gp = np.concatenate([gw.T, seg.T], axis=1)
    shared["gate_pack"] = np.ascontiguousarray(
        gp.reshape(IC, 128, 9).transpose(1, 0, 2).reshape(128, IC * 9))
    shared["wg_t"] = np.concatenate([_tile_gu(egw[e]) for e in range(E)])
    shared["wu_t"] = np.concatenate([_tile_gu(euw[e]) for e in range(E)])
    shared["wd_t"] = np.concatenate(
        [_tile_rhs_out512(np.ascontiguousarray(edw[e].T)) for e in range(E)])
    shared["wsg_t"] = _tile_gu(sgw)
    shared["wsu_t"] = _tile_gu(suw)
    wsd = _tile_rhs_out512(np.ascontiguousarray(sdw.T))
    shared["wsd_t"] = np.ascontiguousarray(
        wsd.reshape(4, 128, 4, 8 * 512).transpose(0, 2, 1, 3).reshape(16, 128, 4096))

    shared["idf"] = np.eye(128, dtype=np.float32)
    shared["idb"] = np.eye(128).astype(BF)
    shared["ones_col_b"] = np.ones((128, 1), dtype=BF)
    shared["ones_row_f"] = np.ones((1, 128), dtype=np.float32)
    d_ = np.arange(4)[:, None, None]
    p_ = np.arange(128)[None, :, None]
    i_ = np.arange(512)[None, None, :]
    cmask = np.where(i_ <= d_ * 128 + p_, np.float32(-1e30), np.float32(0.0))
    shared["cmask"] = np.ascontiguousarray(
        cmask.transpose(1, 0, 2).reshape(128, 4 * 512))
    shared["cmask16"] = shared["cmask"].astype(BF)

    in_maps = []
    for core in range(NCORE):
        b, half = core // 2, core % 2
        chunks = (2 * half, 2 * half + 1)
        m = dict(shared)
        ubase = half * 1024 - QOFF
        kpos_u = ubase + np.arange(UT)
        okmask = (kpos_u >= 0) & (kpos_u < T)
        xkv = np.zeros((UT, D), dtype=np.float32)
        xkv[okmask] = x[b, kpos_u[okmask]]
        val = np.where(okmask, np.float32(-12.0), np.float32(-1e9))
        m["x_kv"] = np.ascontiguousarray(xkv.reshape(UC, 128, D))
        m["x_own"] = np.ascontiguousarray(
            x[b, half * TOK:(half + 1) * TOK].reshape(NTC, 128, D))
        m["valid12"] = np.ascontiguousarray(val.reshape(UC, 128).T)
        cosk, sink = _rope_tables(np.maximum(kpos_u, 0).astype(np.float64), KVH)
        m["cosk"] = np.ascontiguousarray(
            cosk.reshape(UC, 128, 256).transpose(1, 0, 2)
            .reshape(128, UC * 256))
        m["sink"] = np.ascontiguousarray(
            sink.reshape(UC, 128, 256).transpose(1, 0, 2)
            .reshape(128, UC * 256))
        qpos = (half * TOK + np.arange(TOK)).astype(np.float64)
        cosq, sinq = _rope_tables(qpos, 4)
        m["cosq"] = np.ascontiguousarray(
            cosq.reshape(NTC, 128, 256).transpose(1, 0, 2).reshape(128, NTC * 256))
        m["sinq"] = np.ascontiguousarray(
            sinq.reshape(NTC, 128, 256).transpose(1, 0, 2).reshape(128, NTC * 256))
        in_maps.append(m)
    return in_maps



class Sem:
    __slots__ = ("h", "n", "dma")

    def __init__(self, h, dma):
        self.h, self.n, self.dma = h, 0, dma


NONE_T = (None, 0)


class Walk:
    ENGINES = ("sync", "tensor", "vector", "scalar", "gpsimd")

    def __init__(self):
        self.prog = {e: [] for e in self.ENGINES}
        self.hw = {e: {} for e in self.ENGINES}

    def _filter(self, eng, waits):
        hw = self.hw[eng]
        ws = []
        for s, v in waits:
            if s is None or v is None or v <= 0:
                continue
            if hw.get(id(s), -1) >= v:
                continue
            hw[id(s)] = v
            ws.append((s.h, v))
        return tuple(ws)

    def op(self, eng, fn, inc=None, waits=()):
        ws = self._filter(eng, waits)
        amt = 16 if (inc is not None and inc.dma) else 1
        if inc is not None:
            inc.n += amt
        semh = inc.h if inc is not None else None

        def thunk(e, fn=fn, ws=ws, semh=semh, amt=amt):
            for (h, v) in ws:
                e.wait_ge(h, v)
            ins = fn(e)
            if semh is not None:
                ins.then_inc(semh, amt)
        self.prog[eng].append(thunk)
        return inc.n if inc is not None else None

    def wait(self, eng, waits):
        ws = self._filter(eng, waits)
        if ws:
            def thunk(e, ws=ws):
                for (h, v) in ws:
                    e.wait_ge(h, v)
            self.prog[eng].append(thunk)


def build_nc(debug=False):
    nc = bass.Bass("TRN2", target_bir_lowering=False)

    dt = {}

    def din(name, shape, dtype=F32):
        dt[name] = nc.dram_tensor(name, list(shape), dtype, kind="ExternalInput").ap()

    din("x_kv", (UC, 128, D))
    din("x_own", (NTC, 128, D))
    din("valid12", (128, UC))
    din("cosk", (128, UC * 256))
    din("sink", (128, UC * 256))
    din("cosq", (128, NTC * 256))
    din("sinq", (128, NTC * 256))
    din("cmask", (128, 4 * 512))
    din("cmask16", (128, 4 * 512), BF16)
    din("idf", (128, 128))
    din("idb", (128, 128), BF16)
    din("ones_col_b", (128, 1), BF16)
    din("ones_row_f", (1, 128))
    din("gate_pack", (128, IC * 9))
    din("wq_t", (4, 128, IC * 512))
    din("wk_t", (1, 128, IC * 512))
    din("wv_t", (1, 128, IC * 512))
    din("wo_t", (4, 128, IC * 512))
    din("wg_t", (E * 8, 128, 2048))
    din("wu_t", (E * 8, 128, 2048))
    din("wd_t", (E * 4, 128, 4096))
    din("wsg_t", (32, 128, 2048))
    din("wsu_t", (32, 128, 2048))
    din("wsd_t", (16, 128, 4096))

    out_d = nc.dram_tensor("out", [NTC, 128, D], F32, kind="ExternalOutput").ap()
    ln2a_d = nc.dram_tensor("ln2a_scr", [NTC, 128, D], BF16, kind="Internal").ap()

    dbg = {}
    if debug:
        for name, shape, dty in [
            ("dbg_q16", (128, H * TOK), BF16),
            ("dbg_k16", (128, KVH * UT), BF16),
            ("dbg_v16", (128, UC * 512), BF16),
            ("dbg_y16", (128, H * TOK), BF16),
            ("dbg_ln2a", (NTC, 128, D), BF16),
            ("dbg_attn", (NTC, 128, D), BF16),
            ("dbg_comb", (128, NTC * 8), F32),
            ("dbg_sg", (128, NTC), F32),
            ("dbg_x3fm", (128, IC * TOK), BF16),
            ("dbg_he", (128, 8 * TOK), BF16),
            ("dbg_acc", (NTC, 128, D), BF16),
        ]:
            dbg[name] = nc.dram_tensor(name, list(shape), dty, kind="ExternalOutput").ap()

    w = Walk()
    es = contextlib.ExitStack()
    with es:
        ALLSEMS = []

        def mksem(name, dma):
            s = Sem(es.enter_context(nc.semaphore(name)), dma)
            ALLSEMS.append(s)
            return s

        qX = mksem("qX", True)
        qT = mksem("qT", True)
        tT = mksem("tT", False)
        tV = mksem("tV", False)
        tS = mksem("tS", False)
        tG = mksem("tG", False)

        def barrier():
            vals = [(s, s.n) for s in ALLSEMS]
            for eng in Walk.ENGINES:
                w.wait(eng, vals)

        _nuid = [0]

        def mksb(stack):
            def f(name, shape, dtype=F32):
                _nuid[0] += 1
                return stack.enter_context(
                    nc.sbuf_tensor(f"{name}_{_nuid[0]}", list(shape), dtype))
            return f

        sb = mksb(es)

        pA = es.enter_context(nc.psum_tensor("pA", [128, 512], F32))
        pB = es.enter_context(nc.psum_tensor("pB", [128, 512], F32))
        pC = es.enter_context(nc.psum_tensor("pC", [128, 512], F32))
        pD = es.enter_context(nc.psum_tensor("pD", [128, 512], F32))
        pS = es.enter_context(nc.psum_tensor("pS", [128, 512], F32))
        pR = es.enter_context(nc.psum_tensor("pR", [128, 512], F32))

        valid_sb = sb("valid_sb", (128, UC))

        idb_sb = sb("idb_sb", (128, 128), BF16)
        onesc_sb = sb("onesc_sb", (128, 1), BF16)
        onesr_sb = sb("onesr_sb", (1, 128))
        eps_ln = sb("eps_ln", (128, 1))
        eps_q = sb("eps_q", (128, 1))
        eps_k = sb("eps_k", (128, 1))
        stats2 = [sb("stats", (128, 24)) for _ in range(2)]
        mv2 = [sb("mv", (128, 2)) for _ in range(2)]
        rstd2 = [sb("rstd", (128, 4)) for _ in range(2)]
        ssq42 = [sb("ssq4", (128, 4)) for _ in range(2)]

        for name, ap_sb in [("valid12", valid_sb),
                            ("idb", idb_sb),
                            ("ones_col_b", onesc_sb), ("ones_row_f", onesr_sb)]:
            w.op("gpsimd", lambda e, o=ap_sb[:], i=dt[name]: e.dma_start(out=o, in_=i),
                 inc=qT)
        tbl = (qT, qT.n)
        w.op("vector", lambda e, o=eps_ln[:]: e.memset(o, 1e-5))
        w.op("vector", lambda e, o=eps_q[:]: e.memset(o, 1.28e-4))
        w.op("vector", lambda e, o=eps_k[:]: e.memset(o, 1e-6), inc=tV)
        eps_tk = (tV, tV.n)

        idb_ap = idb_sb[:]

        class Stream:

            _sid = [0]

            def __init__(self, salloc, name, fd, qd, cast_eng="gpsimd", nbuf=2):
                self.fd, self.qd, self.nq = fd, qd, fd // qd
                self.nbuf = nbuf
                self.stg = [salloc(f"{name}_s{i}", (128, qd)) for i in range(2)]
                self.w16 = [salloc(f"{name}_w{i}", (128, fd), BF16)
                            for i in range(nbuf)]
                Stream._sid[0] += 1
                self.qsem = [mksem(f"q{name}{Stream._sid[0]}_{i}", True)
                             for i in range(2)]
                self.stg_free = [NONE_T, NONE_T]
                self.w16_free = [NONE_T] * nbuf
                self.cnt = 0
                self.eng = cast_eng
                self.sem = {"vector": tV, "scalar": tS, "gpsimd": tG}[cast_eng]
                self.pending = []

            def load(self, dram_ap):
                bi = self.cnt % self.nbuf
                self.cnt += 1
                cv = 0
                for q in range(self.nq):
                    si = (self.cnt * self.nq + q) % 2
                    src = dram_ap[:, q * self.qd:(q + 1) * self.qd]
                    v = w.op("sync",
                             lambda e, o=self.stg[si][:], i=src: e.dma_start(out=o, in_=i),
                             inc=self.qsem[si], waits=[self.stg_free[si]])
                    o_ap = self.w16[bi][:, q * self.qd:(q + 1) * self.qd]
                    s_ap = self.stg[si][:]
                    if self.eng == "vector":
                        fn = lambda e, o=o_ap, s=s_ap: e.tensor_copy(o, s)
                    elif self.eng == "gpsimd":
                        fn = lambda e, o=o_ap, s=s_ap: e.tensor_copy(o, s)
                    else:
                        fn = lambda e, o=o_ap, s=s_ap: e.copy(o, s)
                    cv = w.op(self.eng, fn, inc=self.sem,
                              waits=[(self.qsem[si], v), self.w16_free[bi]])
                    self.stg_free[si] = (self.sem, cv)
                return bi, (self.sem, cv)

            def prefetch(self, dram_ap):
                self.pending.append(self.load(dram_ap))

            def take(self):
                return self.pending.pop(0)

            def used(self, bi, ticket):
                self.w16_free[bi] = ticket

        ln_setbusy = [NONE_T, NONE_T]

        def emit_ln(src, dst, src_ready, dst_free, eps_tile, ls=0):
            stats, mv, rstd = stats2[ls], mv2[ls], rstd2[ls]
            for s4 in range(4):
                w.op("vector",
                     lambda e, o=stats[:, s4 * 6:(s4 + 1) * 6],
                     i=src[:, s4 * 512:(s4 + 1) * 512]: e.bn_stats(o, i),
                     inc=(tV if s4 == 3 else None),
                     waits=list(src_ready) + ([ln_setbusy[ls]] if s4 == 0 else []))
            w.op("vector", lambda e, o=mv[:], i=stats[:]: e.bn_aggr(o, i),
                 inc=tV, waits=[(tV, tV.n), (tS, tS.n)])
            va = tV.n
            w.op("scalar",
                 lambda e, o=rstd[:, 0:1], i=mv[:, 1:2], b=eps_tile[:]:
                 e.activation(o, i, AF.Ln, bias=b, scale=1.0),
                 inc=tS, waits=[(tV, va), eps_tk])
            w.op("scalar",
                 lambda e, o=rstd[:, 0:1]: e.activation(o, o, AF.Exp, scale=-0.5),
                 inc=tS, waits=[(tS, tS.n)])
            vs = tS.n
            cv = w.op("vector",
                      lambda e, o=dst[:], i=src[:], m=mv[:, 0:1], r=rstd[:, 0:1]:
                      e.tensor_scalar(o, i, m, r, ALU.subtract, ALU.mult),
                      inc=tV, waits=[(tS, vs)] + list(dst_free))
            ln_setbusy[ls] = (tV, cv)
            return cv

        def emit_normrope(psum, qsc, qsq, qrot, cos_ap, sin_ap, eps_tile, scale,
                          mm_ticket, rot_cons, ls=0):
            rstd, ssq4 = rstd2[ls], ssq42[ls]
            cv = w.op("vector", lambda e, o=qsc[:], i=psum[:]: e.tensor_copy(o, i),
                      inc=tV, waits=[mm_ticket])
            psum_free = (tV, cv)
            w.op("vector", lambda e, o=qsq[:], i=qsc[:]: e.tensor_tensor(o, i, i, ALU.mult),
                 inc=tV, waits=[(tV, cv)])
            w.op("vector",
                 lambda e, o=ssq4[:], i=qsq[:].rearrange("p (h d) -> p h d", h=4):
                 e.tensor_reduce(o, i, axis=AX.X, op=ALU.add),
                 inc=tV, waits=[(tV, tV.n), (tS, tS.n)])
            va = tV.n
            w.op("scalar",
                 lambda e, o=rstd[:], i=ssq4[:], b=eps_tile[:], s=scale:
                 e.activation(o, i, AF.Ln, bias=b, scale=s),
                 inc=tS, waits=[(tV, va), eps_tk])
            w.op("scalar",
                 lambda e, o=rstd[:]: e.activation(o, o, AF.Exp, scale=-0.5),
                 inc=tS, waits=[(tS, tS.n)])
            vs = tS.n
            for hh in range(4):
                w.op("vector",
                     lambda e, o=qsc[:, hh * 128:(hh + 1) * 128], r=rstd[:, hh:hh + 1]:
                     e.tensor_scalar(o, o, r, None, ALU.mult),
                     inc=(tV if hh == 3 else None), waits=[(tS, vs)])
            sv = tV.n
            q3 = qsc[:].rearrange("p (h d) -> p h d", h=4)
            r3 = qrot[:].rearrange("p (h d) -> p h d", h=4)
            s3 = qsq[:].rearrange("p (h d) -> p h d", h=4)
            c3 = cos_ap.rearrange("p (h d) -> p h d", h=4)
            n3 = sin_ap.rearrange("p (h d) -> p h d", h=4)
            w.op("vector",
                 lambda e, o=r3[:, :, 0:64], a=q3[:, :, 0:64], b=c3:
                 e.tensor_tensor(o, a, b, ALU.mult),
                 waits=[(tV, sv), tbl] + list(rot_cons))
            w.op("vector",
                 lambda e, o=s3[:, :, 0:64], a=q3[:, :, 64:128], b=n3:
                 e.tensor_tensor(o, a, b, ALU.mult), inc=tV)
            w.op("vector",
                 lambda e, o=r3[:, :, 0:64], b=s3[:, :, 0:64]:
                 e.tensor_tensor(o, o, b, ALU.subtract),
                 inc=tV, waits=[(tV, tV.n)])
            w.op("vector",
                 lambda e, o=r3[:, :, 64:128], a=q3[:, :, 0:64], b=n3:
                 e.tensor_tensor(o, a, b, ALU.mult))
            w.op("vector",
                 lambda e, o=s3[:, :, 64:128], a=q3[:, :, 64:128], b=c3:
                 e.tensor_tensor(o, a, b, ALU.mult), inc=tV)
            w.op("vector",
                 lambda e, o=r3[:, :, 64:128], b=s3[:, :, 64:128]:
                 e.tensor_tensor(o, o, b, ALU.add),
                 inc=tV, waits=[(tV, tV.n)])
            return psum_free, tV.n

        ln2_stores = 0
        p1 = contextlib.ExitStack()
        with p1:
            sb1 = mksb(p1)
            q16 = sb1("q16", (128, H * TOK), BF16)
            k16 = sb1("k16", (128, KVH * UT), BF16)
            v16 = sb1("v16", (128, UC * 512), BF16)
            pab = contextlib.ExitStack()
            with pab:
                sb2 = mksb(pab)
                cosk_sl = sb2("cosk_sl", (128, UC * 256), BF16)
                sink_sl = sb2("sink_sl", (128, UC * 256), BF16)
                cosq_sb = sb2("cosq_sb", (128, NTC * 256), BF16)
                sinq_sb = sb2("sinq_sb", (128, NTC * 256), BF16)
                qTa = mksem("qTa", True)
                with nc.sbuf_tensor("ropestg", [128, UC * 256], F32) as ropestg:
                    stg_ap = ropestg[:]
                    prev = NONE_T
                    for nm_, dst_ in (("cosk", cosk_sl), ("sink", sink_sl),
                                      ("cosq", cosq_sb), ("sinq", sinq_sb)):
                        fdim = dst_[:].tensor.shape[1]
                        v_ = w.op("gpsimd",
                                  lambda e, o=ropestg[0:128, 0:fdim], i=dt[nm_]:
                                  e.dma_start(out=o, in_=i),
                                  inc=qTa, waits=[prev])
                        cv_ = w.op("vector",
                                   lambda e, o=dst_[:], i=ropestg[0:128, 0:fdim]:
                                   e.tensor_copy(o, i),
                                   inc=tV, waits=[(qTa, v_)])
                        prev = (tV, cv_)
                cs_ld = prev
                w.wait("gpsimd", [prev])
                xin = [sb2(f"xin{i}", (128, D)) for i in range(2)]
                xin_free = [NONE_T] * 2
                qxin = [mksem("qxin0", True), mksem("qxin1", True)]
                x1c = [sb2(f"x1c{i}", (128, D), BF16) for i in range(2)]
                x1c_free = [NONE_T] * 2
                x1fm = sb2("x1fm", (128, IC * UT), BF16)
                qscb = [sb2("qsc", (128, 512), BF16) for _ in range(2)]
                qsqb = [sb2("qsq", (128, 512), BF16) for _ in range(2)]
                qrotb = [sb2("qrot", (128, 512), BF16) for _ in range(2)]
                wstr = Stream(sb2, "wA", 4096, 1024, cast_eng="gpsimd", nbuf=2)
                pTb = [pab.enter_context(nc.psum_tensor("pT_0", [128, 1024], BF16)),
                       pab.enter_context(nc.psum_tensor("pT_1", [128, 1024], BF16))]

                x1_r = x1fm[:].rearrange("p (ic t) -> p ic t", ic=IC)
                pT_r = [pTb[0][:].rearrange("p (ii t) -> p ii t", ii=8),
                        pTb[1][:].rearrange("p (ii t) -> p ii t", ii=8)]
                tr_free = [NONE_T, NONE_T]
                nr_state = [dict(rot_cons=NONE_T) for _ in range(2)]
                bank_free2 = [NONE_T, NONE_T]
                unit_ctr = [0]
                x1_ck = [NONE_T] * UC
                tr_pending = []

                def flush_tr(keep=1):
                    while len(tr_pending) > keep:
                        tr_pending.pop(0)()

                def qkv_unit(bank, mmv, cos_ap, sin_ap, eps_tile, scale,
                             dst_sl, bi_bank):
                    u = unit_ctr[0] % 2
                    unit_ctr[0] += 1
                    st = nr_state[u]
                    pf, nv = emit_normrope(
                        bank, qscb[u], qsqb[u], qrotb[u], cos_ap, sin_ap,
                        eps_tile, scale, mmv, [st["rot_cons"]], ls=u)
                    bank_free2[bi_bank] = pf

                    def part2(u=u, nv=nv, dst_sl=dst_sl, st=st):
                        for hh in range(4):
                            w.op("tensor",
                                 lambda e, o=pTb[u][:, hh * 128:(hh + 1) * 128],
                                 i=qrotb[u][:, hh * 128:(hh + 1) * 128]:
                                 e.transpose(o, i, idb_ap),
                                 inc=(tT if hh == 3 else None),
                                 waits=[(tV, nv), tr_free[u]])
                        tt = (tT, tT.n)
                        st["rot_cons"] = tt
                        cv = w.op("vector",
                                  lambda e, o=dst_sl, i=pT_r[u][:, 0:4, :]:
                                  e.tensor_copy(o, i),
                                  inc=tV, waits=[tt])
                        tr_free[u] = (tV, cv)
                    tr_pending.append(part2)

                def emit_ln1_chunk(tc):
                    bi = tc % 2
                    v = w.op("gpsimd",
                             lambda e, o=xin[bi][:], i=dt["x_kv"][tc]:
                             e.dma_start(out=o, in_=i),
                             inc=qxin[bi], waits=[xin_free[bi]])
                    cv = emit_ln(xin[bi], x1c[bi], [(qxin[bi], v)],
                                 [x1c_free[bi]], eps_ln, ls=bi)
                    xin_free[bi] = (tV, cv)
                    for g in range(2):
                        for ii in range(8):
                            ic = g * 8 + ii
                            w.op("tensor",
                                 lambda e, o=pTb[g][:, ii * 128:(ii + 1) * 128],
                                 i=x1c[bi][:, ic * 128:(ic + 1) * 128]:
                                 e.transpose(o, i, idb_ap),
                                 inc=(tT if ii == 7 else None),
                                 waits=[(tV, cv), tr_free[g], tbl])
                        tt = tT.n
                        cpv = w.op("vector",
                                   lambda e,
                                   o=x1_r[:, g * 8:(g + 1) * 8, tc * 128:tc * 128 + 128],
                                   i=pT_r[g]:
                                   e.tensor_copy(o, i),
                                   inc=tV, waits=[(tT, tt)])
                        tr_free[g] = (tV, cpv)
                    x1c_free[bi] = tr_free[1]
                    x1_ck[tc] = tr_free[1]

                def slab_ap(mat, oc, hs):
                    return dt[mat + "_t"][oc][:, hs * 4096:(hs + 1) * 4096]

                def emit_qkv_tiles(mat, oc, wA, wB, rdyA, rdyB, tiles):
                    last_mm = 0
                    for tc in tiles:
                        bank = [pA, pB][tc % 2]
                        if mat == "wq":
                            loff = QOFF + tc * 128
                            xr = [x1_ck[7 + tc], x1_ck[8 + tc if 8 + tc < UC else UC - 1]]
                        else:
                            loff = tc * 128
                            xr = [x1_ck[tc]]
                        for ic in range(IC):
                            ws16 = wA if ic < 8 else wB
                            w.op("tensor",
                                 lambda e, o=bank[:],
                                 l=x1fm[:, ic * UT + loff:ic * UT + loff + 128],
                                 r=ws16[:, (ic % 8) * 512:(ic % 8 + 1) * 512],
                                 s=(ic == 0), p=(ic == IC - 1):
                                 e.matmul(o, l, r, start=s, stop=p),
                                 inc=(tT if ic == IC - 1 else None),
                                 waits=xr + [rdyA, rdyB, bank_free2[tc % 2]])
                        mmv = (tT, tT.n)
                        last_mm = tT.n
                        if mat == "wv":
                            cv = w.op("vector",
                                      lambda e,
                                      o=v16[:, tc * 512:(tc + 1) * 512],
                                      i=bank[:]: e.tensor_copy(o, i),
                                      inc=tV, waits=[mmv])
                            bank_free2[tc % 2] = (tV, cv)
                        elif mat == "wk":
                            w.wait("vector", [cs_ld])
                            k_r = k16[:].rearrange("p (h t) -> p h t", h=KVH)
                            qkv_unit(bank, mmv,
                                     cosk_sl[:, tc * 256:(tc + 1) * 256],
                                     sink_sl[:, tc * 256:(tc + 1) * 256],
                                     eps_k, 1.0 / 128.0,
                                     k_r[:, :, tc * 128:tc * 128 + 128],
                                     tc % 2)
                        else:
                            q_r = q16[:].rearrange("p (h t) -> p h t", h=H)
                            qkv_unit(bank, mmv,
                                     cosq_sb[:, tc * 256:(tc + 1) * 256],
                                     sinq_sb[:, tc * 256:(tc + 1) * 256],
                                     eps_q, 1.0,
                                     q_r[:, oc * 4:(oc + 1) * 4, tc * 128:tc * 128 + 128],
                                     tc % 2)
                        flush_tr(keep=1)
                    return last_mm

                wstr.prefetch(slab_ap("wk", 0, 0))
                wstr.prefetch(slab_ap("wk", 0, 1))
                emit_ln1_chunk(0)
                biA, rdyA = wstr.take()
                biB, rdyB = wstr.take()
                wkA, wkB = wstr.w16[biA], wstr.w16[biB]
                lm = 0
                for tc in range(1, UC):
                    emit_ln1_chunk(tc)
                    lm = emit_qkv_tiles("wk", 0, wkA, wkB, rdyA, rdyB, [tc - 1])
                lm = emit_qkv_tiles("wk", 0, wkA, wkB, rdyA, rdyB, [UC - 1])
                wstr.used(biA, (tT, lm))
                wstr.used(biB, (tT, lm))
                rest = [("wv", 0)] + [("wq", oc) for oc in range(4)]
                wstr.prefetch(slab_ap("wv", 0, 0))
                wstr.prefetch(slab_ap("wv", 0, 1))
                for si_, (mat, oc) in enumerate(rest):
                    biA, rdyA = wstr.take()
                    biB, rdyB = wstr.take()
                    wA, wB = wstr.w16[biA], wstr.w16[biB]
                    tiles = range(8) if mat == "wq" else range(UC)
                    lm = emit_qkv_tiles(mat, oc, wA, wB, rdyA, rdyB, tiles)
                    wstr.used(biA, (tT, lm))
                    wstr.used(biB, (tT, lm))
                    if si_ + 1 < len(rest):
                        m2_, o2_ = rest[si_ + 1]
                        wstr.prefetch(slab_ap(m2_, o2_, 0))
                        wstr.prefetch(slab_ap(m2_, o2_, 1))
                flush_tr(keep=0)
            qkv_ready = (tV, tV.n)
            barrier()

            if debug:
                for nm, src_ in [("dbg_q16", q16), ("dbg_k16", k16), ("dbg_v16", v16)]:
                    w.op("gpsimd", lambda e, o=dbg[nm], i=src_[:]:
                         e.dma_start(out=o, in_=i), inc=qX, waits=[qkv_ready])

            pcd = contextlib.ExitStack()
            with pcd:
                sb3 = mksb(pcd)
                exp16 = [sb3(f"exp16_{i}", (128, 512), BF16) for i in range(6)]
                rcp = [sb3(f"rcp{i}", (1, 512)) for i in range(2)]
                bcf = [sb3(f"bcf{i}", (128, 512)) for i in range(2)]
                y16 = sb3("y16", (128, H * TOK), BF16)
                attn16 = [sb3(f"attn16_{i}", (128, D), BF16) for i in range(8)]
                ln2c = [sb3(f"ln2c{i}", (128, D), BF16) for i in range(2)]
                ln2c_free = [NONE_T, NONE_T]
                qL = [mksem("qL0", True), mksem("qL1", True)]
                wstr2 = Stream(sb3, "wB", 4096, 2048, cast_eng="vector", nbuf=3)
                pSu = [pcd.enter_context(nc.psum_tensor(f"pSu{i}", [128, 512], F32))
                       for i in range(2)]
                cmask16_sb = sb3("cmask16_sb", (128, 4 * 512), BF16)
                qTb = mksem("qTb", True)
                w.op("gpsimd", lambda e, o=cmask16_sb[:], i=dt["cmask16"]:
                     e.dma_start(out=o, in_=i), inc=qTb)
                cm_ld = (qTb, qTb.n)
                w.wait("tensor", [cm_ld])

                scb = [[pA, pB], [pC, pD]]
                avb = [pS, pR]
                av_free = [NONE_T, NONE_T]
                sc_free = [[NONE_T, NONE_T], [NONE_T, NONE_T]]
                exp_free = [NONE_T] * 6
                sum_free = [NONE_T, NONE_T]
                pending = [None, None]

                def emit_tail(u):
                    h, j, rcpv = pending[u]
                    pending[u] = None
                    sbank = scb[u][1]
                    w.op("tensor",
                         lambda e, o=sbank[:], l=onesr_sb[:], r=rcp[u][:]:
                         e.matmul(o, l, r, start=True, stop=True),
                         inc=tT, waits=[rcpv, sc_free[u][1]])
                    bct = (tT, tT.n)
                    cv2 = w.op("vector",
                               lambda e, o=bcf[u][:], i=sbank[:]:
                               e.tensor_copy(o, i),
                               inc=tV, waits=[bct])
                    sc_free[u][1] = (tV, cv2)
                    yv = w.op("vector",
                              lambda e, o=y16[:, h * TOK + j * CH:h * TOK + (j + 1) * CH],
                              b2=bcf[u][:]:
                              e.tensor_tensor(o, o, b2, ALU.mult),
                              inc=tV, waits=[(tV, cv2)])
                    return yv

                for j in range(NSLOT):
                    for hp in range(H // 2):
                        exvq = {}
                        for kc in range(WC + 1):
                            if kc < WC:
                                uck = 4 * j + kc
                                for u in range(2):
                                    h = 2 * hp + u
                                    kvh = h // 4
                                    ei = u * 3 + kc % 3
                                    bank = scb[u][kc % 2]
                                    w.op("tensor",
                                         lambda e, o=bank[:],
                                         l=k16[:, kvh * UT + uck * 128:kvh * UT + uck * 128 + 128],
                                         r=q16[:, h * TOK + j * CH:h * TOK + (j + 1) * CH],
                                         p=(kc < 8):
                                         e.matmul(o, l, r, start=True, stop=p),
                                         inc=(tT if kc < 8 else None),
                                         waits=[qkv_ready, sc_free[u][kc % 2], tbl])
                                    if kc >= 8:
                                        w.op("tensor",
                                             lambda e, o=bank[:], l=idb_ap,
                                             r=cmask16_sb[:, (kc - 8) * 512:(kc - 7) * 512]:
                                             e.matmul(o, l, r, start=False, stop=True),
                                             inc=tT)
                                    mmv = (tT, tT.n)
                                    ev = w.op("scalar",
                                              lambda e, o=exp16[ei][:], i=bank[:],
                                              b=valid_sb[:, uck:uck + 1]:
                                              e.activation(o, i, AF.Exp, bias=b, scale=1.0),
                                              inc=tS, waits=[mmv, exp_free[ei], tbl])
                                    exvq[(u, kc)] = (tS, ev)
                                    sc_free[u][kc % 2] = (tS, ev)
                            if kc == 2 and pending[0] is not None:
                                emit_tail(0)
                                emit_tail(1)
                            if kc >= 1:
                                kd = kc - 1
                                uckd = 4 * j + kd
                                for u in range(2):
                                    h = 2 * hp + u
                                    kvh = h // 4
                                    ei = u * 3 + kd % 3
                                    w.op("tensor",
                                         lambda e, o=pSu[u][0:1, :], l=onesc_sb[:],
                                         r=exp16[ei][:],
                                         s=(kd == 0), p=(kd == WC - 1):
                                         e.matmul(o, l, r, start=s, stop=p),
                                         inc=tT,
                                         waits=[exvq.pop((u, kd))] + ([sum_free[u]] if kd == 0 else []))
                                    w.op("tensor",
                                         lambda e, o=avb[u][:],
                                         l=v16[:, uckd * 512 + kvh * 128:uckd * 512 + kvh * 128 + 128],
                                         r=exp16[ei][:],
                                         s=(kd == 0), p=(kd == WC - 1):
                                         e.matmul(o, l, r, start=s, stop=p),
                                         inc=tT,
                                         waits=([av_free[u]] if kd == 0 else []))
                                    exp_free[ei] = (tT, tT.n)
                        av_done = (tT, tT.n)
                        for u in range(2):
                            h = 2 * hp + u
                            cv = w.op("vector",
                                      lambda e, o=y16[:, h * TOK + j * CH:h * TOK + (j + 1) * CH],
                                      i=avb[u][:]: e.tensor_copy(o, i),
                                      inc=tV, waits=[av_done])
                            av_free[u] = (tV, cv)
                            lv = w.op("scalar",
                                      lambda e, o=rcp[u][:], i=pSu[u][0:1, :]:
                                      e.activation(o, i, AF.Ln, scale=1.0),
                                      inc=tS, waits=[av_done])
                            sum_free[u] = (tS, lv)
                            rv = w.op("scalar",
                                      lambda e, o=rcp[u][:]:
                                      e.activation(o, o, AF.Exp, scale=-1.0),
                                      inc=tS, waits=[(tS, lv)])
                            pending[u] = (h, j, (tS, rv))
                emit_tail(0)
                emit_tail(1)
                y_ready = (tV, tV.n)

                if debug:
                    w.op("gpsimd", lambda e, o=dbg["dbg_y16"], i=y16[:]:
                         e.dma_start(out=o, in_=i), inc=qX, waits=[y_ready])

                attn_parts = [0] * 8
                ob_free = [NONE_T, NONE_T]
                wstr2.prefetch(dt["wo_t"][0][:, 0:4096])
                wstr2.prefetch(dt["wo_t"][0][:, 4096:8192])
                for oc in range(4):
                    biA2, rdyA2 = wstr2.take()
                    biB2, rdyB2 = wstr2.take()
                    woA, woB = wstr2.w16[biA2], wstr2.w16[biB2]
                    last_mm = 0
                    for tq in range(8):
                        bank = [pA, pB][tq % 2]
                        for ic in range(IC):
                            ws16 = woA if ic < 8 else woB
                            w.op("tensor",
                                 lambda e, o=bank[:],
                                 l=y16[:, ic * TOK + tq * 128:ic * TOK + tq * 128 + 128],
                                 r=ws16[:, (ic % 8) * 512:(ic % 8 + 1) * 512],
                                 s=(ic == 0), p=(ic == IC - 1):
                                 e.matmul(o, l, r, start=s, stop=p),
                                 inc=(tT if ic == IC - 1 else None),
                                 waits=[y_ready, rdyA2, rdyB2, ob_free[tq % 2]])
                        mmv = (tT, tT.n)
                        last_mm = tT.n
                        cv = w.op("vector",
                                  lambda e, o=attn16[tq][:, oc * 512:(oc + 1) * 512],
                                  i=bank[:]: e.tensor_copy(o, i),
                                  inc=tV, waits=[mmv])
                        ob_free[tq % 2] = (tV, cv)
                        attn_parts[tq] = cv
                    wstr2.used(biA2, (tT, last_mm))
                    wstr2.used(biB2, (tT, last_mm))
                    if oc + 1 < 4:
                        wstr2.prefetch(dt["wo_t"][oc + 1][:, 0:4096])
                        wstr2.prefetch(dt["wo_t"][oc + 1][:, 4096:8192])
                for tq in range(8):
                    bi = tq % 2
                    if debug:
                        w.op("gpsimd",
                             lambda e, o=dbg["dbg_attn"][tq], i=attn16[tq][:]:
                             e.dma_start(out=o, in_=i),
                             inc=qX, waits=[(tV, attn_parts[tq])])
                    cv = emit_ln(attn16[tq], ln2c[bi], [(tV, attn_parts[tq])],
                                 [ln2c_free[bi]], eps_ln, ls=bi)
                    sv = w.op("gpsimd",
                              lambda e, o=ln2a_d[tq], i=ln2c[bi][:]:
                              e.dma_start(out=o, in_=i),
                              inc=qL[bi], waits=[(tV, cv)])
                    ln2c_free[bi] = (qL[bi], sv)
                    ln2_stores += 1
                    if debug:
                        w.op("gpsimd",
                             lambda e, o=dbg["dbg_ln2a"][tq], i=ln2c[bi][:]:
                             e.dma_start(out=o, in_=i), inc=qX)
        ln2_done2 = [(qL[0], qL[0].n), (qL[1], qL[1].n)]

        p2 = contextlib.ExitStack()
        with p2:
            sbf = mksb(p2)
            idf_sb = sbf("idf_sb", (128, 128))
            gate_sb = sbf("gate_sb", (128, IC * 9))
            qTc = mksem("qTc", True)
            w.op("gpsimd", lambda e, o=idf_sb[:], i=dt["idf"]:
                 e.dma_start(out=o, in_=i), inc=qTc)
            w.op("gpsimd", lambda e, o=gate_sb[:], i=dt["gate_pack"]:
                 e.dma_start(out=o, in_=i), inc=qTc)
            p2tbl = (qTc, qTc.n)
            w.wait("tensor", [p2tbl])
            w.wait("vector", [p2tbl])
            idf_ap = idf_sb[:]
            x3fm = sbf("x3fm", (128, IC * TOK), BF16)
            moe = sbf("moe", (128, NTC * D), BF16)
            comb = sbf("comb", (128, NTC * 8))
            sgv = sbf("sgv", (128, NTC))
            x3_r = x3fm[:].rearrange("p (ic t) -> p ic t", ic=IC)

            barrier()
            p2a = contextlib.ExitStack()
            with p2a:
                sba = mksb(p2a)
                xin2 = [sba(f"xin2_{i}", (128, D)) for i in range(2)]
                xin2_free = [NONE_T, NONE_T]
                l2c = [sba(f"l2c{i}", (128, D), BF16) for i in range(2)]
                l2c_free = [NONE_T, NONE_T]
                qxin2 = [mksem("qxin2_0", True), mksem("qxin2_1", True)]
                ql2c = [mksem("ql2c_0", True), mksem("ql2c_1", True)]
                x2c2 = [sba("x2c", (128, D)) for _ in range(2)]
                x3c2 = [sba("x3c", (128, D)) for _ in range(2)]
                xf322 = [sba("xf32", (128, D)) for _ in range(2)]
                r9 = sba("r9", (9, 128))
                pr9 = sba("pr9", (128, 9))
                prb = sba("prb", (128, 8))
                m1 = sba("m1", (128, 1))
                m2 = sba("m2", (128, 1))
                sm = sba("sm", (128, 1))
                eqb = sba("eqb", (128, 8))

                x2c_free = [NONE_T, NONE_T]
                x3c_free = [NONE_T, NONE_T]
                xf32_free = [NONE_T, NONE_T]
                pR_free = NONE_T
                pDt_free = NONE_T
                r9_cons = NONE_T
                pr9_cons = NONE_T
                trb_free = [NONE_T, NONE_T]
                for tc in range(NTC):
                    bi = tc % 2
                    x2c, x3c, xf32 = x2c2[bi], x3c2[bi], xf322[bi]
                    v1 = w.op("gpsimd",
                              lambda e, o=xin2[bi][:], i=dt["x_own"][tc]:
                              e.dma_start(out=o, in_=i),
                              inc=qxin2[bi], waits=[xin2_free[bi]] + ln2_done2)
                    v2 = w.op("gpsimd",
                              lambda e, o=l2c[bi][:], i=ln2a_d[tc]:
                              e.dma_start(out=o, in_=i),
                              inc=ql2c[bi], waits=[l2c_free[bi]])
                    cv = w.op("vector",
                              lambda e, o=x2c[:], a=xin2[bi][:], b=l2c[bi][:]:
                              e.tensor_tensor(o, a, b, ALU.add),
                              inc=tV, waits=[(qxin2[bi], v1), (ql2c[bi], v2), x2c_free[bi]])
                    xin2_free[bi] = (tV, cv)
                    l2c_free[bi] = (tV, cv)
                    lv = emit_ln(x2c, x3c, [(tV, cv)], [x3c_free[bi]], eps_ln, ls=bi)
                    x2c_free[bi] = (tV, lv)
                    grp_cp = 0
                    for g in range(4):
                        bank = [pA, pB][g % 2]
                        for ii in range(4):
                            ic = g * 4 + ii
                            w.op("tensor",
                                 lambda e, o=bank[:, ii * 128:(ii + 1) * 128],
                                 i=x3c[:, ic * 128:(ic + 1) * 128]:
                                 e.transpose(o, i, idf_ap),
                                 inc=(tT if ii == 3 else None),
                                 waits=[(tV, lv), trb_free[g % 2], tbl])
                        tt = (tT, tT.n)
                        w.op("vector",
                             lambda e,
                             o=x3_r[:, g * 4:(g + 1) * 4, tc * 128:tc * 128 + 128],
                             i=bank[:].rearrange("p (ii t) -> p ii t", ii=4):
                             e.tensor_copy(o, i),
                             waits=[tt])
                        cv2 = w.op("vector",
                                   lambda e, o=xf32[:, g * 512:(g + 1) * 512], i=bank[:]:
                                   e.tensor_copy(o, i),
                                   inc=tV, waits=[tt, xf32_free[bi]])
                        trb_free[g % 2] = (tV, cv2)
                        grp_cp = cv2
                    x3c_free[bi] = (tV, grp_cp)
                    xf_rdy = (tV, grp_cp)
                    for ic in range(IC):
                        w.op("tensor",
                             lambda e, o=pR[0:9, 0:128],
                             l=gate_sb[:, ic * 9:(ic + 1) * 9],
                             r=xf32[:, ic * 128:(ic + 1) * 128],
                             s=(ic == 0), p=(ic == IC - 1):
                             e.matmul(o, l, r, start=s, stop=p),
                             inc=(tT if ic == IC - 1 else None),
                             waits=[xf_rdy, tbl, pR_free])
                    rmm = (tT, tT.n)
                    xf32_free[bi] = rmm
                    cvr = w.op("vector", lambda e, o=r9[:], i=pR[0:9, 0:128]:
                               e.tensor_copy(o, i),
                               inc=tV, waits=[rmm, r9_cons])
                    pR_free = (tV, cvr)
                    w.op("tensor",
                         lambda e, o=pD[0:128, 0:9], i=r9[:], d9=idf_sb[0:9, 0:9]:
                         e.transpose(o, i, d9),
                         inc=tT, waits=[(tV, cvr), pDt_free])
                    r9_cons = (tT, tT.n)
                    cvp = w.op("vector", lambda e, o=pr9[:], i=pD[0:128, 0:9]:
                               e.tensor_copy(o, i),
                               inc=tV, waits=[r9_cons, pr9_cons])
                    pDt_free = (tV, cvp)
                    w.op("vector", lambda e, o=m1[:], i=pr9[:, 0:8]:
                         e.reduce_max(o, i, axis=AX.X),
                         inc=tV, waits=[(tV, cvp), (tS, tS.n)])
                    w.op("vector", lambda e, o=prb[:], i=pr9[:, 0:8], s=m1[:]:
                         e.tensor_scalar(o, i, s, None, ALU.subtract),
                         inc=tV, waits=[(tV, tV.n)])
                    av_ = tV.n
                    w.op("scalar", lambda e, o=prb[:]:
                         e.activation(o, o, AF.Exp, scale=1.0),
                         inc=tS, waits=[(tV, av_)])
                    ev = (tS, tS.n)
                    w.op("vector", lambda e, o=sm[:], i=prb[:]:
                         e.reduce_sum(o, i, axis=AX.X),
                         inc=tV, waits=[ev])
                    w.op("vector", lambda e, o=sm[:]: e.reciprocal(o, o),
                         inc=tV, waits=[(tV, tV.n)])
                    w.op("vector", lambda e, o=prb[:], s=sm[:]:
                         e.tensor_scalar(o, o, s, None, ALU.mult),
                         inc=tV, waits=[(tV, tV.n)])
                    w.op("vector", lambda e, o=m1[:], i=prb[:]:
                         e.reduce_max(o, i, axis=AX.X),
                         inc=tV, waits=[(tV, tV.n)])
                    w.op("vector", lambda e, o=eqb[:], i=prb[:], s=m1[:]:
                         e.tensor_scalar(o, i, s, None, ALU.is_equal),
                         inc=tV, waits=[(tV, tV.n)])
                    w.op("vector", lambda e, o=eqb[:], i=prb[:]:
                         e.scalar_tensor_tensor(o, o, 1e30, i, ALU.mult, ALU.subtract),
                         inc=tV, waits=[(tV, tV.n)])
                    w.op("vector", lambda e, o=m2[:], i=eqb[:]:
                         e.tensor_reduce(o, i, axis=AX.X, op=ALU.min, negate=True),
                         inc=tV, waits=[(tV, tV.n)])
                    w.op("vector", lambda e, o=eqb[:], i=prb[:], s=m2[:]:
                         e.tensor_scalar(o, i, s, None, ALU.is_ge),
                         inc=tV, waits=[(tV, tV.n)])
                    w.op("vector",
                         lambda e, o=comb[:, tc * 8:(tc + 1) * 8], a=prb[:], b=eqb[:]:
                         e.tensor_tensor(o, a, b, ALU.mult),
                         inc=tV, waits=[(tV, tV.n)])
                    w.op("scalar",
                         lambda e, o=m1[:], i=pr9[:, 8:9]:
                         e.activation(o, i, AF.Tanh, scale=0.5),
                         inc=tS, waits=[(tV, tV.n)])
                    pr9_cons = (tS, tS.n)
                    w.op("vector",
                         lambda e, o=sgv[:, tc:tc + 1], i=m1[:]:
                         e.tensor_scalar(o, i, 0.5, 0.5, ALU.mult, ALU.add),
                         inc=tV, waits=[(tS, tS.n)])
                x3_ready = (tV, tV.n)
                comb_ready = (tV, tV.n)

            if debug:
                for nm, src in [("dbg_x3fm", x3fm), ("dbg_comb", comb), ("dbg_sg", sgv)]:
                    w.op("gpsimd", lambda e, o=dbg[nm], i=src[:]:
                         e.dma_start(out=o, in_=i), inc=qX, waits=[x3_ready])

            barrier()
            p2b = contextlib.ExitStack()
            with p2b:
                sbb = mksb(p2b)
                he2 = [sbb("he", (128, 8 * TOK), BF16) for _ in range(2)]
                silu_f = [sbb(f"silu{i}", (128, 512)) for i in range(2)]
                silu_free = [NONE_T, NONE_T]
                gS = Stream(sbb, "wg", 2048, 2048, cast_eng="vector")
                uS = Stream(sbb, "wu", 2048, 2048, cast_eng="scalar")
                dS = Stream(sbb, "wd", 4096, 2048, cast_eng="scalar")

                gb = [pA, pB]
                ub = [pC, pD]
                db = [pS, pR]
                gb_free = [NONE_T, NONE_T]
                ub_free = [NONE_T, NONE_T]
                db_free = [NONE_T, NONE_T]
                he_free = [NONE_T, NONE_T]
                moe_parts = NONE_T
                for pf in range(2):
                    gS.prefetch(dt["wg_t"][pf])
                    uS.prefetch(dt["wu_t"][pf])
                dS.prefetch(dt["wd_t"][0])
                dS.prefetch(dt["wd_t"][1])
                gu_next = 2
                d_next = 2
                for ee in range(E):
                    he = he2[ee % 2]
                    he_done = 0
                    for ici in range(8):
                        bg, grdy = gS.take()
                        bu, urdy = uS.take()
                        g16, u16 = gS.w16[bg], uS.w16[bu]
                        lastg = lastu = 0
                        for t in range(2):
                            gi = (ici * 2 + t) % 2
                            for ic in range(IC):
                                w.op("tensor",
                                     lambda e, o=gb[gi][:],
                                     l=g16[:, ic * 128:(ic + 1) * 128],
                                     r=x3fm[:, ic * TOK + t * 512:ic * TOK + (t + 1) * 512],
                                     s=(ic == 0), p=(ic == IC - 1):
                                     e.matmul(o, l, r, start=s, stop=p),
                                     inc=(tT if ic == IC - 1 else None),
                                     waits=[x3_ready, grdy, gb_free[gi]])
                            gmm = (tT, tT.n)
                            lastg = tT.n
                            for ic in range(IC):
                                w.op("tensor",
                                     lambda e, o=ub[gi][:],
                                     l=u16[:, ic * 128:(ic + 1) * 128],
                                     r=x3fm[:, ic * TOK + t * 512:ic * TOK + (t + 1) * 512],
                                     s=(ic == 0), p=(ic == IC - 1):
                                     e.matmul(o, l, r, start=s, stop=p),
                                     inc=(tT if ic == IC - 1 else None),
                                     waits=[urdy, ub_free[gi]])
                            umm = (tT, tT.n)
                            lastu = tT.n
                            si = gi
                            sv = w.op("scalar",
                                      lambda e, o=silu_f[si][:], i=gb[gi][:]:
                                      e.activation(o, i, AF.Silu, scale=1.0),
                                      inc=tS, waits=[gmm, silu_free[si]])
                            gb_free[gi] = (tS, sv)
                            hv = w.op("vector",
                                      lambda e,
                                      o=he[:, ici * 1024 + t * 512:ici * 1024 + (t + 1) * 512],
                                      a=silu_f[si][:], b=ub[gi][:]:
                                      e.tensor_tensor(o, a, b, ALU.mult),
                                      inc=tV, waits=[(tS, sv), umm, he_free[ee % 2]])
                            silu_free[si] = (tV, hv)
                            ub_free[gi] = (tV, hv)
                            he_done = hv
                        gS.used(bg, (tT, lastg))
                        uS.used(bu, (tT, lastu))
                        if gu_next < E * 8:
                            gS.prefetch(dt["wg_t"][gu_next])
                            uS.prefetch(dt["wu_t"][gu_next])
                            gu_next += 1
                    he_ready = (tV, he_done)
                    if debug and ee == 0:
                        w.op("gpsimd", lambda e, o=dbg["dbg_he"], i=he[:]:
                             e.dma_start(out=o, in_=i), inc=qX, waits=[he_ready])
                    lastd_all = 0
                    for oc in range(4):
                        bd, drdy = dS.take()
                        d16 = dS.w16[bd]
                        lastd = 0
                        for tc in range(NTC):
                            di = tc % 2
                            for ici in range(8):
                                w.op("tensor",
                                     lambda e, o=db[di][:],
                                     l=he[:, ici * 1024 + tc * 128:ici * 1024 + tc * 128 + 128],
                                     r=d16[:, ici * 512:(ici + 1) * 512],
                                     s=(ici == 0), p=(ici == 7):
                                     e.matmul(o, l, r, start=s, stop=p),
                                     inc=(tT if ici == 7 else None),
                                     waits=[he_ready, drdy, db_free[di]])
                            mmv = (tT, tT.n)
                            lastd = tT.n
                            dst = moe[:, tc * D + oc * 512:tc * D + (oc + 1) * 512]
                            if ee == 0:
                                cv = w.op("vector",
                                          lambda e, o=dst, i=db[di][:],
                                          s=comb[:, tc * 8 + ee:tc * 8 + ee + 1]:
                                          e.tensor_scalar(o, i, s, None, ALU.mult),
                                          inc=tV, waits=[mmv, comb_ready])
                            else:
                                cv = w.op("vector",
                                          lambda e, o=dst, i=db[di][:],
                                          s=comb[:, tc * 8 + ee:tc * 8 + ee + 1]:
                                          e.scalar_tensor_tensor(o, i, s, o,
                                                                 ALU.mult, ALU.add),
                                          inc=tV, waits=[mmv, comb_ready, moe_parts])
                            db_free[di] = (tV, cv)
                        dS.used(bd, (tT, lastd))
                        if d_next < E * 4:
                            dS.prefetch(dt["wd_t"][d_next])
                            d_next += 1
                        lastd_all = lastd
                    moe_parts = (tV, tV.n)
                    he_free[ee % 2] = (tT, lastd_all)

            barrier()
            p2c = contextlib.ExitStack()
            with p2c:
                sbc = mksb(p2c)
                sh2 = [sbc("sh", (128, 32 * 512), BF16)]
                silu_f = [sbc(f"silu2_{i}", (128, 512)) for i in range(2)]
                silu_free = [NONE_T, NONE_T]
                gS = Stream(sbc, "wsg", 2048, 2048, cast_eng="vector")
                uS = Stream(sbc, "wsu", 2048, 2048, cast_eng="scalar")
                dS = Stream(sbc, "wsd", 4096, 2048, cast_eng="scalar")

                pA_free = NONE_T
                pB_free = NONE_T
                banks4 = [pC, pD, pS, pR]
                psum4_free = [NONE_T] * 4
                sh_cons = NONE_T
                moe_sh = moe_parts
                for t in range(2):
                    sh = sh2[0]
                    sh_done = 0
                    for ici in range(32):
                        bg, grdy = gS.load(dt["wsg_t"][ici])
                        bu, urdy = uS.load(dt["wsu_t"][ici])
                        g16, u16 = gS.w16[bg], uS.w16[bu]
                        for ic in range(IC):
                            w.op("tensor",
                                 lambda e, o=pA[:],
                                 l=g16[:, ic * 128:(ic + 1) * 128],
                                 r=x3fm[:, ic * TOK + t * 512:ic * TOK + (t + 1) * 512],
                                 s=(ic == 0), p=(ic == IC - 1):
                                 e.matmul(o, l, r, start=s, stop=p),
                                 inc=(tT if ic == IC - 1 else None),
                                 waits=[x3_ready, grdy, pA_free])
                        gmm = (tT, tT.n)
                        lastg = tT.n
                        for ic in range(IC):
                            w.op("tensor",
                                 lambda e, o=pB[:],
                                 l=u16[:, ic * 128:(ic + 1) * 128],
                                 r=x3fm[:, ic * TOK + t * 512:ic * TOK + (t + 1) * 512],
                                 s=(ic == 0), p=(ic == IC - 1):
                                 e.matmul(o, l, r, start=s, stop=p),
                                 inc=(tT if ic == IC - 1 else None),
                                 waits=[urdy, pB_free])
                        umm = (tT, tT.n)
                        si = ici % 2
                        sv = w.op("scalar",
                                  lambda e, o=silu_f[si][:], i=pA[:]:
                                  e.activation(o, i, AF.Silu, scale=1.0),
                                  inc=tS, waits=[gmm, silu_free[si]])
                        pA_free = (tS, sv)
                        hv = w.op("vector",
                                  lambda e, o=sh[:, ici * 512:(ici + 1) * 512],
                                  a=silu_f[si][:], b=pB[:]:
                                  e.tensor_tensor(o, a, b, ALU.mult),
                                  inc=tV, waits=[(tS, sv), umm, sh_cons])
                        silu_free[si] = (tV, hv)
                        pB_free = (tV, hv)
                        sh_done = hv
                        gS.used(bg, (tT, lastg))
                        uS.used(bu, (tT, tT.n))
                    sh_ready = (tV, sh_done)
                    for oc in range(4):
                        mm_last = [0, 0, 0, 0]
                        for icg in range(4):
                            bd, drdy = dS.load(dt["wsd_t"][oc * 4 + icg])
                            d16 = dS.w16[bd]
                            for tl in range(4):
                                pb = banks4[tl]
                                for icl in range(8):
                                    ici = icg * 8 + icl
                                    w.op("tensor",
                                         lambda e, o=pb[:],
                                         l=sh[:, ici * 512 + tl * 128:ici * 512 + tl * 128 + 128],
                                         r=d16[:, icl * 512:(icl + 1) * 512],
                                         s=(icg == 0 and icl == 0),
                                         p=(icg == 3 and icl == 7):
                                         e.matmul(o, l, r, start=s, stop=p),
                                         inc=(tT if icl == 7 else None),
                                         waits=[sh_ready, drdy, psum4_free[tl]])
                                mm_last[tl] = tT.n
                            dS.used(bd, (tT, max(mm_last)))
                        for tl in range(4):
                            tc = t * 4 + tl
                            dst = moe[:, tc * D + oc * 512:tc * D + (oc + 1) * 512]
                            cv = w.op("vector",
                                      lambda e, o=dst, i=banks4[tl][:],
                                      s=sgv[:, tc:tc + 1]:
                                      e.scalar_tensor_tensor(o, i, s, o,
                                                             ALU.mult, ALU.add),
                                      inc=tV,
                                      waits=[(tT, mm_last[tl]), moe_sh, comb_ready])
                            psum4_free[tl] = (tV, cv)
                    sh_cons = (tT, tT.n)
                moe_ready = (tV, tV.n)

            if debug:
                for tc in range(NTC):
                    w.op("gpsimd",
                         lambda e, o=dbg["dbg_acc"][tc], i=moe[:, tc * D:(tc + 1) * D]:
                         e.dma_start(out=o, in_=i),
                         inc=qX, waits=[moe_ready])

            barrier()
            p2d = contextlib.ExitStack()
            with p2d:
                sbd = mksb(p2d)
                xin3 = [sbd(f"xin3_{i}", (128, D)) for i in range(2)]
                xin3_free = [NONE_T, NONE_T]
                l2c2 = [sbd(f"l2c2_{i}", (128, D), BF16) for i in range(2)]
                l2c2_free = [NONE_T, NONE_T]
                qxin3 = [mksem("qxin3_0", True), mksem("qxin3_1", True)]
                ql2c2 = [mksem("ql2c2_0", True), mksem("ql2c2_1", True)]
                qP = [mksem("qP0", True), mksem("qP1", True)]
                t42 = [sbd(f"t4_{i}", (128, D)) for i in range(2)]
                outc = [sbd(f"outc{i}", (128, D)) for i in range(2)]
                outc_free = [NONE_T, NONE_T]
                t4_free = [NONE_T, NONE_T]

                for tc in range(NTC):
                    bi = tc % 2
                    v1 = w.op("gpsimd",
                              lambda e, o=xin3[bi][:], i=dt["x_own"][tc]:
                              e.dma_start(out=o, in_=i),
                              inc=qxin3[bi], waits=[xin3_free[bi]])
                    v2 = w.op("gpsimd",
                              lambda e, o=l2c2[bi][:], i=ln2a_d[tc]:
                              e.dma_start(out=o, in_=i),
                              inc=ql2c2[bi], waits=[l2c2_free[bi]])
                    t4 = t42[bi]
                    lv = emit_ln(moe[:, tc * D:(tc + 1) * D], t4, [moe_ready],
                                 [t4_free[bi]], eps_ln, ls=tc % 2)
                    ov = w.op("vector",
                              lambda e, o=outc[bi][:], a=xin3[bi][:], b=l2c2[bi][:]:
                              e.tensor_tensor(o, a, b, ALU.add),
                              inc=tV, waits=[(qxin3[bi], v1), (ql2c2[bi], v2), outc_free[bi]])
                    ov2 = w.op("vector",
                               lambda e, o=outc[bi][:], b=t4[:]:
                               e.tensor_tensor(o, o, b, ALU.add),
                               inc=tV, waits=[(tV, lv), (tV, ov)])
                    t4_free[bi] = (tV, ov2)
                    xin3_free[bi] = (tV, ov2)
                    l2c2_free[bi] = (tV, ov2)
                    sv = w.op("gpsimd",
                              lambda e, o=out_d[tc], i=outc[bi][:]:
                              e.dma_start(out=o, in_=i),
                              inc=qP[bi], waits=[(tV, ov2)])
                    outc_free[bi] = (qP[bi], sv)

        w.wait("gpsimd", [(s, s.n) for s in ALLSEMS if s.dma])

        with nc.Block() as blk:
            @blk.sync
            def _(eng):
                for f in w.prog["sync"]:
                    f(eng)

            @blk.tensor
            def _(eng):
                for f in w.prog["tensor"]:
                    f(eng)

            @blk.vector
            def _(eng):
                for f in w.prog["vector"]:
                    f(eng)

            @blk.scalar
            def _(eng):
                for f in w.prog["scalar"]:
                    f(eng)

            @blk.gpsimd
            def _(eng):
                for f in w.prog["gpsimd"]:
                    f(eng)

    return nc


def kernel(**inputs):
    in_maps = host_prepare(inputs)
    nc = build_nc()
    res = run_bass_kernel_spmd(nc, in_maps, core_ids=list(range(NCORE)))
    out = np.empty((B, T, D), dtype=np.float32)
    for core in range(NCORE):
        b, half = core // 2, core % 2
        out[b, half * TOK:(half + 1) * TOK] = res.results[core]["out"].reshape(TOK, D)
    return out


# version 22
# speedup vs baseline: 1.0432x; 1.0023x over previous
import contextlib

import numpy as np
import ml_dtypes

import concourse.bass as bass
import concourse.mybir as mybir
from concourse.bass_utils import run_bass_kernel_spmd

F32 = mybir.dt.float32
BF16 = mybir.dt.bfloat16
AF = mybir.ActivationFunctionType
ALU = mybir.AluOpType
AX = mybir.AxisListType
BF = ml_dtypes.bfloat16

B, T, D = 4, 2048, 2048
H, KVH, DH = 16, 4, 128
IC = 16
CH = 512
WT = 1536
WC = 12
QOFF = 1023
NSLOT = 2
HT = 768
HC = 6
E, IEXP, ISH = 8, 1024, 4096
TOK, NTC = 1024, 8
UT, UC = 2048, 16
NCORE = 8
THETA = 10000.0


_PERM = np.empty(128, dtype=np.int64)
_PERM[:64] = np.arange(64) * 2
_PERM[64:] = np.arange(64) * 2 + 1


def _perm_heads(w, nheads):
    return w.reshape(nheads, 128, D)[:, _PERM, :].reshape(nheads * 128, D)


def _tile_rhs_out512(wT):
    nin, nout = wT.shape
    a = wT.reshape(nin // 128, 128, nout // 512, 512)
    return np.ascontiguousarray(
        a.transpose(2, 1, 0, 3).reshape(nout // 512, 128, (nin // 128) * 512))


def _tile_gu(w):
    I_, Din = w.shape
    a = w.reshape(I_ // 128, 128, Din // 128, 128)
    return np.ascontiguousarray(
        a.transpose(0, 3, 2, 1).reshape(I_ // 128, 128, (Din // 128) * 128))


_ROPE_CACHE = {}


def _rope_base():
    if "cs" in _ROPE_CACHE:
        return _ROPE_CACHE["cs"]
    import ctypes
    libm = ctypes.CDLL("libm.so.6")
    libm.cosf.restype = ctypes.c_float
    libm.cosf.argtypes = [ctypes.c_float]
    libm.sinf.restype = ctypes.c_float
    libm.sinf.argtypes = [ctypes.c_float]
    af = (np.arange(0, DH, 2, dtype=np.float32) / np.float32(DH)).astype(np.float32)
    p = np.power(np.float64(THETA), af.astype(np.float64)).astype(np.float32)
    inv = (np.float32(1.0) / p).astype(np.float32)
    ang = (np.arange(T, dtype=np.float32)[:, None] * inv[None, :]).astype(np.float32)
    flat = ang.ravel()
    cos = np.fromiter((libm.cosf(float(v)) for v in flat), np.float32,
                      count=flat.size).reshape(ang.shape)
    sin = np.fromiter((libm.sinf(float(v)) for v in flat), np.float32,
                      count=flat.size).reshape(ang.shape)
    _ROPE_CACHE["cs"] = (cos, sin)
    return cos, sin


def _rope_tables(positions, reps):
    cos_t, sin_t = _rope_base()
    idx = np.clip(positions.astype(np.int64), 0, T - 1)
    return (np.tile(cos_t[idx], (1, reps)), np.tile(sin_t[idx], (1, reps)))


def host_prepare(inputs):
    x = np.asarray(inputs["L0_attn_x"], dtype=np.float32)
    wq = np.asarray(inputs["L0_attn_wq"], dtype=np.float32)
    wk = np.asarray(inputs["L0_attn_wk"], dtype=np.float32)
    wv = np.asarray(inputs["L0_attn_wv"], dtype=np.float32)
    wo = np.asarray(inputs["L0_attn_wo"], dtype=np.float32)
    gw = np.asarray(inputs["L0_ffn_gate_w"], dtype=np.float32)
    egw = np.asarray(inputs["L0_ffn_expert_gate_w"], dtype=np.float32)
    euw = np.asarray(inputs["L0_ffn_expert_up_w"], dtype=np.float32)
    edw = np.asarray(inputs["L0_ffn_expert_down_w"], dtype=np.float32)
    sgw = np.asarray(inputs["L0_ffn_shared_gate_w"], dtype=np.float32)
    suw = np.asarray(inputs["L0_ffn_shared_up_w"], dtype=np.float32)
    sdw = np.asarray(inputs["L0_ffn_shared_down_w"], dtype=np.float32)
    seg = np.asarray(inputs["L0_ffn_shared_expert_gate_w"], dtype=np.float32)

    shared = {}
    shared["wq_t"] = _tile_rhs_out512(np.ascontiguousarray(_perm_heads(wq, H).T))
    shared["wk_t"] = _tile_rhs_out512(np.ascontiguousarray(_perm_heads(wk, KVH).T))
    shared["wv_t"] = _tile_rhs_out512(np.ascontiguousarray(wv.T))
    shared["wo_t"] = _tile_rhs_out512(np.ascontiguousarray(wo.T))
    gp = np.concatenate([gw.T, seg.T], axis=1)
    shared["gate_pack"] = np.ascontiguousarray(
        gp.reshape(IC, 128, 9).transpose(1, 0, 2).reshape(128, IC * 9))
    shared["wg_t"] = np.concatenate([_tile_gu(egw[e]) for e in range(E)])
    shared["wu_t"] = np.concatenate([_tile_gu(euw[e]) for e in range(E)])
    shared["wd_t"] = np.concatenate(
        [_tile_rhs_out512(np.ascontiguousarray(edw[e].T)) for e in range(E)])
    shared["wsg_t"] = _tile_gu(sgw)
    shared["wsu_t"] = _tile_gu(suw)
    wsd = _tile_rhs_out512(np.ascontiguousarray(sdw.T))
    shared["wsd_t"] = np.ascontiguousarray(
        wsd.reshape(4, 128, 4, 8 * 512).transpose(0, 2, 1, 3).reshape(16, 128, 4096))

    shared["idf"] = np.eye(128, dtype=np.float32)
    shared["idb"] = np.eye(128).astype(BF)
    shared["ones_col_b"] = np.ones((128, 1), dtype=BF)
    shared["ones_row_f"] = np.ones((1, 128), dtype=np.float32)
    d_ = np.arange(4)[:, None, None]
    p_ = np.arange(128)[None, :, None]
    i_ = np.arange(512)[None, None, :]
    cmask = np.where(i_ <= d_ * 128 + p_, np.float32(-1e30), np.float32(0.0))
    shared["cmask"] = np.ascontiguousarray(
        cmask.transpose(1, 0, 2).reshape(128, 4 * 512))
    shared["cmask16"] = shared["cmask"].astype(BF)

    in_maps = []
    for core in range(NCORE):
        b, half = core // 2, core % 2
        chunks = (2 * half, 2 * half + 1)
        m = dict(shared)
        ubase = half * 1024 - QOFF
        kpos_u = ubase + np.arange(UT)
        okmask = (kpos_u >= 0) & (kpos_u < T)
        xkv = np.zeros((UT, D), dtype=np.float32)
        xkv[okmask] = x[b, kpos_u[okmask]]
        val = np.where(okmask, np.float32(-12.0), np.float32(-1e9))
        m["x_kv"] = np.ascontiguousarray(xkv.reshape(UC, 128, D))
        m["x_own"] = np.ascontiguousarray(
            x[b, half * TOK:(half + 1) * TOK].reshape(NTC, 128, D))
        m["valid12"] = np.ascontiguousarray(val.reshape(UC, 128).T)
        cosk, sink = _rope_tables(np.maximum(kpos_u, 0).astype(np.float64), KVH)
        m["cosk"] = np.ascontiguousarray(
            cosk.reshape(UC, 128, 256).transpose(1, 0, 2)
            .reshape(128, UC * 256))
        m["sink"] = np.ascontiguousarray(
            sink.reshape(UC, 128, 256).transpose(1, 0, 2)
            .reshape(128, UC * 256))
        qpos = (half * TOK + np.arange(TOK)).astype(np.float64)
        cosq, sinq = _rope_tables(qpos, 4)
        m["cosq"] = np.ascontiguousarray(
            cosq.reshape(NTC, 128, 256).transpose(1, 0, 2).reshape(128, NTC * 256))
        m["sinq"] = np.ascontiguousarray(
            sinq.reshape(NTC, 128, 256).transpose(1, 0, 2).reshape(128, NTC * 256))
        in_maps.append(m)
    return in_maps



class Sem:
    __slots__ = ("h", "n", "dma")

    def __init__(self, h, dma):
        self.h, self.n, self.dma = h, 0, dma


NONE_T = (None, 0)


class Walk:
    ENGINES = ("sync", "tensor", "vector", "scalar", "gpsimd")

    def __init__(self):
        self.prog = {e: [] for e in self.ENGINES}
        self.hw = {e: {} for e in self.ENGINES}

    def _filter(self, eng, waits):
        hw = self.hw[eng]
        ws = []
        for s, v in waits:
            if s is None or v is None or v <= 0:
                continue
            if hw.get(id(s), -1) >= v:
                continue
            hw[id(s)] = v
            ws.append((s.h, v))
        return tuple(ws)

    def op(self, eng, fn, inc=None, waits=()):
        ws = self._filter(eng, waits)
        amt = 16 if (inc is not None and inc.dma) else 1
        if inc is not None:
            inc.n += amt
        semh = inc.h if inc is not None else None

        def thunk(e, fn=fn, ws=ws, semh=semh, amt=amt):
            for (h, v) in ws:
                e.wait_ge(h, v)
            ins = fn(e)
            if semh is not None:
                ins.then_inc(semh, amt)
        self.prog[eng].append(thunk)
        return inc.n if inc is not None else None

    def wait(self, eng, waits):
        ws = self._filter(eng, waits)
        if ws:
            def thunk(e, ws=ws):
                for (h, v) in ws:
                    e.wait_ge(h, v)
            self.prog[eng].append(thunk)


def build_nc(debug=False):
    nc = bass.Bass("TRN2", target_bir_lowering=False)

    dt = {}

    def din(name, shape, dtype=F32):
        dt[name] = nc.dram_tensor(name, list(shape), dtype, kind="ExternalInput").ap()

    din("x_kv", (UC, 128, D))
    din("x_own", (NTC, 128, D))
    din("valid12", (128, UC))
    din("cosk", (128, UC * 256))
    din("sink", (128, UC * 256))
    din("cosq", (128, NTC * 256))
    din("sinq", (128, NTC * 256))
    din("cmask", (128, 4 * 512))
    din("cmask16", (128, 4 * 512), BF16)
    din("idf", (128, 128))
    din("idb", (128, 128), BF16)
    din("ones_col_b", (128, 1), BF16)
    din("ones_row_f", (1, 128))
    din("gate_pack", (128, IC * 9))
    din("wq_t", (4, 128, IC * 512))
    din("wk_t", (1, 128, IC * 512))
    din("wv_t", (1, 128, IC * 512))
    din("wo_t", (4, 128, IC * 512))
    din("wg_t", (E * 8, 128, 2048))
    din("wu_t", (E * 8, 128, 2048))
    din("wd_t", (E * 4, 128, 4096))
    din("wsg_t", (32, 128, 2048))
    din("wsu_t", (32, 128, 2048))
    din("wsd_t", (16, 128, 4096))

    out_d = nc.dram_tensor("out", [NTC, 128, D], F32, kind="ExternalOutput").ap()
    ln2a_d = nc.dram_tensor("ln2a_scr", [NTC, 128, D], BF16, kind="Internal").ap()

    dbg = {}
    if debug:
        for name, shape, dty in [
            ("dbg_q16", (128, H * TOK), BF16),
            ("dbg_k16", (128, KVH * UT), BF16),
            ("dbg_v16", (128, UC * 512), BF16),
            ("dbg_y16", (128, H * TOK), BF16),
            ("dbg_ln2a", (NTC, 128, D), BF16),
            ("dbg_attn", (NTC, 128, D), BF16),
            ("dbg_comb", (128, NTC * 8), F32),
            ("dbg_sg", (128, NTC), F32),
            ("dbg_x3fm", (128, IC * TOK), BF16),
            ("dbg_he", (128, 8 * TOK), BF16),
            ("dbg_acc", (NTC, 128, D), BF16),
        ]:
            dbg[name] = nc.dram_tensor(name, list(shape), dty, kind="ExternalOutput").ap()

    w = Walk()
    es = contextlib.ExitStack()
    with es:
        ALLSEMS = []

        def mksem(name, dma):
            s = Sem(es.enter_context(nc.semaphore(name)), dma)
            ALLSEMS.append(s)
            return s

        qX = mksem("qX", True)
        qT = mksem("qT", True)
        tT = mksem("tT", False)
        tV = mksem("tV", False)
        tS = mksem("tS", False)
        tG = mksem("tG", False)

        def barrier():
            vals = [(s, s.n) for s in ALLSEMS]
            for eng in Walk.ENGINES:
                w.wait(eng, vals)

        _nuid = [0]

        def mksb(stack):
            def f(name, shape, dtype=F32):
                _nuid[0] += 1
                return stack.enter_context(
                    nc.sbuf_tensor(f"{name}_{_nuid[0]}", list(shape), dtype))
            return f

        sb = mksb(es)

        pA = es.enter_context(nc.psum_tensor("pA", [128, 512], F32))
        pB = es.enter_context(nc.psum_tensor("pB", [128, 512], F32))
        pC = es.enter_context(nc.psum_tensor("pC", [128, 512], F32))
        pD = es.enter_context(nc.psum_tensor("pD", [128, 512], F32))
        pS = es.enter_context(nc.psum_tensor("pS", [128, 512], F32))
        pR = es.enter_context(nc.psum_tensor("pR", [128, 512], F32))

        valid_sb = sb("valid_sb", (128, UC))

        idb_sb = sb("idb_sb", (128, 128), BF16)
        onesc_sb = sb("onesc_sb", (128, 1), BF16)
        onesr_sb = sb("onesr_sb", (1, 128))
        eps_ln = sb("eps_ln", (128, 1))
        eps_q = sb("eps_q", (128, 1))
        eps_k = sb("eps_k", (128, 1))
        stats2 = [sb("stats", (128, 24)) for _ in range(2)]
        mv2 = [sb("mv", (128, 2)) for _ in range(2)]
        rstd2 = [sb("rstd", (128, 4)) for _ in range(2)]
        ssq42 = [sb("ssq4", (128, 4)) for _ in range(2)]

        for name, ap_sb in [("valid12", valid_sb),
                            ("idb", idb_sb),
                            ("ones_col_b", onesc_sb), ("ones_row_f", onesr_sb)]:
            w.op("gpsimd", lambda e, o=ap_sb[:], i=dt[name]: e.dma_start(out=o, in_=i),
                 inc=qT)
        tbl = (qT, qT.n)
        w.op("vector", lambda e, o=eps_ln[:]: e.memset(o, 1e-5))
        w.op("vector", lambda e, o=eps_q[:]: e.memset(o, 1.28e-4))
        w.op("vector", lambda e, o=eps_k[:]: e.memset(o, 1e-6), inc=tV)
        eps_tk = (tV, tV.n)

        idb_ap = idb_sb[:]

        class Stream:

            _sid = [0]

            def __init__(self, salloc, name, fd, qd, cast_eng="gpsimd", nbuf=2):
                self.fd, self.qd, self.nq = fd, qd, fd // qd
                self.nbuf = nbuf
                self.stg = [salloc(f"{name}_s{i}", (128, qd)) for i in range(2)]
                self.w16 = [salloc(f"{name}_w{i}", (128, fd), BF16)
                            for i in range(nbuf)]
                Stream._sid[0] += 1
                self.qsem = [mksem(f"q{name}{Stream._sid[0]}_{i}", True)
                             for i in range(2)]
                self.stg_free = [NONE_T, NONE_T]
                self.w16_free = [NONE_T] * nbuf
                self.cnt = 0
                self.eng = cast_eng
                self.sem = {"vector": tV, "scalar": tS, "gpsimd": tG}[cast_eng]
                self.pending = []

            def load(self, dram_ap):
                bi = self.cnt % self.nbuf
                self.cnt += 1
                cv = 0
                for q in range(self.nq):
                    si = (self.cnt * self.nq + q) % 2
                    src = dram_ap[:, q * self.qd:(q + 1) * self.qd]
                    v = w.op("sync",
                             lambda e, o=self.stg[si][:], i=src: e.dma_start(out=o, in_=i),
                             inc=self.qsem[si], waits=[self.stg_free[si]])
                    o_ap = self.w16[bi][:, q * self.qd:(q + 1) * self.qd]
                    s_ap = self.stg[si][:]
                    if self.eng == "vector":
                        fn = lambda e, o=o_ap, s=s_ap: e.tensor_copy(o, s)
                    elif self.eng == "gpsimd":
                        fn = lambda e, o=o_ap, s=s_ap: e.tensor_copy(o, s)
                    else:
                        fn = lambda e, o=o_ap, s=s_ap: e.copy(o, s)
                    cv = w.op(self.eng, fn, inc=self.sem,
                              waits=[(self.qsem[si], v), self.w16_free[bi]])
                    self.stg_free[si] = (self.sem, cv)
                return bi, (self.sem, cv)

            def prefetch(self, dram_ap):
                self.pending.append(self.load(dram_ap))

            def take(self):
                return self.pending.pop(0)

            def used(self, bi, ticket):
                self.w16_free[bi] = ticket

        ln_setbusy = [NONE_T, NONE_T]

        def emit_ln(src, dst, src_ready, dst_free, eps_tile, ls=0):
            stats, mv, rstd = stats2[ls], mv2[ls], rstd2[ls]
            for s4 in range(4):
                w.op("vector",
                     lambda e, o=stats[:, s4 * 6:(s4 + 1) * 6],
                     i=src[:, s4 * 512:(s4 + 1) * 512]: e.bn_stats(o, i),
                     inc=(tV if s4 == 3 else None),
                     waits=list(src_ready) + ([ln_setbusy[ls]] if s4 == 0 else []))
            w.op("vector", lambda e, o=mv[:], i=stats[:]: e.bn_aggr(o, i),
                 inc=tV, waits=[(tV, tV.n), (tS, tS.n)])
            va = tV.n
            w.op("scalar",
                 lambda e, o=rstd[:, 0:1], i=mv[:, 1:2], b=eps_tile[:]:
                 e.activation(o, i, AF.Ln, bias=b, scale=1.0),
                 inc=tS, waits=[(tV, va), eps_tk])
            w.op("scalar",
                 lambda e, o=rstd[:, 0:1]: e.activation(o, o, AF.Exp, scale=-0.5),
                 inc=tS, waits=[(tS, tS.n)])
            vs = tS.n
            cv = w.op("vector",
                      lambda e, o=dst[:], i=src[:], m=mv[:, 0:1], r=rstd[:, 0:1]:
                      e.tensor_scalar(o, i, m, r, ALU.subtract, ALU.mult),
                      inc=tV, waits=[(tS, vs)] + list(dst_free))
            ln_setbusy[ls] = (tV, cv)
            return cv

        def emit_normrope(psum, qsc, qsq, qrot, cos_ap, sin_ap, eps_tile, scale,
                          mm_ticket, rot_cons, ls=0):
            rstd, ssq4 = rstd2[ls], ssq42[ls]
            cv = w.op("vector", lambda e, o=qsc[:], i=psum[:]: e.tensor_copy(o, i),
                      inc=tV, waits=[mm_ticket])
            psum_free = (tV, cv)
            w.op("vector", lambda e, o=qsq[:], i=qsc[:]: e.tensor_tensor(o, i, i, ALU.mult),
                 inc=tV, waits=[(tV, cv)])
            w.op("vector",
                 lambda e, o=ssq4[:], i=qsq[:].rearrange("p (h d) -> p h d", h=4):
                 e.tensor_reduce(o, i, axis=AX.X, op=ALU.add),
                 inc=tV, waits=[(tV, tV.n), (tS, tS.n)])
            va = tV.n
            w.op("scalar",
                 lambda e, o=rstd[:], i=ssq4[:], b=eps_tile[:], s=scale:
                 e.activation(o, i, AF.Ln, bias=b, scale=s),
                 inc=tS, waits=[(tV, va), eps_tk])
            w.op("scalar",
                 lambda e, o=rstd[:]: e.activation(o, o, AF.Exp, scale=-0.5),
                 inc=tS, waits=[(tS, tS.n)])
            vs = tS.n
            for hh in range(4):
                w.op("vector",
                     lambda e, o=qsc[:, hh * 128:(hh + 1) * 128], r=rstd[:, hh:hh + 1]:
                     e.tensor_scalar(o, o, r, None, ALU.mult),
                     inc=(tV if hh == 3 else None), waits=[(tS, vs)])
            sv = tV.n
            q3 = qsc[:].rearrange("p (h d) -> p h d", h=4)
            r3 = qrot[:].rearrange("p (h d) -> p h d", h=4)
            s3 = qsq[:].rearrange("p (h d) -> p h d", h=4)
            c3 = cos_ap.rearrange("p (h d) -> p h d", h=4)
            n3 = sin_ap.rearrange("p (h d) -> p h d", h=4)
            w.op("vector",
                 lambda e, o=r3[:, :, 0:64], a=q3[:, :, 0:64], b=c3:
                 e.tensor_tensor(o, a, b, ALU.mult),
                 waits=[(tV, sv), tbl] + list(rot_cons))
            w.op("vector",
                 lambda e, o=s3[:, :, 0:64], a=q3[:, :, 64:128], b=n3:
                 e.tensor_tensor(o, a, b, ALU.mult), inc=tV)
            w.op("vector",
                 lambda e, o=r3[:, :, 0:64], b=s3[:, :, 0:64]:
                 e.tensor_tensor(o, o, b, ALU.subtract),
                 inc=tV, waits=[(tV, tV.n)])
            w.op("vector",
                 lambda e, o=r3[:, :, 64:128], a=q3[:, :, 0:64], b=n3:
                 e.tensor_tensor(o, a, b, ALU.mult))
            w.op("vector",
                 lambda e, o=s3[:, :, 64:128], a=q3[:, :, 64:128], b=c3:
                 e.tensor_tensor(o, a, b, ALU.mult), inc=tV)
            w.op("vector",
                 lambda e, o=r3[:, :, 64:128], b=s3[:, :, 64:128]:
                 e.tensor_tensor(o, o, b, ALU.add),
                 inc=tV, waits=[(tV, tV.n)])
            return psum_free, tV.n

        ln2_stores = 0
        p1 = contextlib.ExitStack()
        with p1:
            sb1 = mksb(p1)
            q16 = sb1("q16", (128, H * TOK), BF16)
            k16 = sb1("k16", (128, KVH * UT), BF16)
            v16 = sb1("v16", (128, UC * 512), BF16)
            pab = contextlib.ExitStack()
            with pab:
                sb2 = mksb(pab)
                cosk_sl = sb2("cosk_sl", (128, UC * 256), BF16)
                sink_sl = sb2("sink_sl", (128, UC * 256), BF16)
                cosq_sb = sb2("cosq_sb", (128, NTC * 256), BF16)
                sinq_sb = sb2("sinq_sb", (128, NTC * 256), BF16)
                qTa = mksem("qTa", True)
                with nc.sbuf_tensor("ropestg", [128, UC * 256], F32) as ropestg:
                    stg_ap = ropestg[:]
                    prev = NONE_T
                    for nm_, dst_ in (("cosk", cosk_sl), ("sink", sink_sl),
                                      ("cosq", cosq_sb), ("sinq", sinq_sb)):
                        fdim = dst_[:].tensor.shape[1]
                        v_ = w.op("gpsimd",
                                  lambda e, o=ropestg[0:128, 0:fdim], i=dt[nm_]:
                                  e.dma_start(out=o, in_=i),
                                  inc=qTa, waits=[prev])
                        cv_ = w.op("vector",
                                   lambda e, o=dst_[:], i=ropestg[0:128, 0:fdim]:
                                   e.tensor_copy(o, i),
                                   inc=tV, waits=[(qTa, v_)])
                        prev = (tV, cv_)
                cs_ld = prev
                w.wait("gpsimd", [prev])
                xin = [sb2(f"xin{i}", (128, D)) for i in range(2)]
                xin_free = [NONE_T] * 2
                qxin = [mksem("qxin0", True), mksem("qxin1", True)]
                x1c = [sb2(f"x1c{i}", (128, D), BF16) for i in range(2)]
                x1c_free = [NONE_T] * 2
                x1fm = sb2("x1fm", (128, IC * UT), BF16)
                qscb = [sb2("qsc", (128, 512), BF16) for _ in range(2)]
                qsqb = [sb2("qsq", (128, 512), BF16) for _ in range(2)]
                qrotb = [sb2("qrot", (128, 512), BF16) for _ in range(2)]
                wstr = Stream(sb2, "wA", 4096, 1024, cast_eng="gpsimd", nbuf=2)
                pTb = [pab.enter_context(nc.psum_tensor("pT_0", [128, 1024], BF16)),
                       pab.enter_context(nc.psum_tensor("pT_1", [128, 1024], BF16))]

                x1_r = x1fm[:].rearrange("p (ic t) -> p ic t", ic=IC)
                pT_r = [pTb[0][:].rearrange("p (ii t) -> p ii t", ii=8),
                        pTb[1][:].rearrange("p (ii t) -> p ii t", ii=8)]
                tr_free = [NONE_T, NONE_T]
                nr_state = [dict(rot_cons=NONE_T) for _ in range(2)]
                bank_free2 = [NONE_T, NONE_T]
                unit_ctr = [0]
                x1_ck = [NONE_T] * UC
                tr_pending = []

                def flush_tr(keep=1):
                    while len(tr_pending) > keep:
                        tr_pending.pop(0)()

                def qkv_unit(bank, mmv, cos_ap, sin_ap, eps_tile, scale,
                             dst_sl, bi_bank):
                    u = unit_ctr[0] % 2
                    unit_ctr[0] += 1
                    st = nr_state[u]
                    pf, nv = emit_normrope(
                        bank, qscb[u], qsqb[u], qrotb[u], cos_ap, sin_ap,
                        eps_tile, scale, mmv, [st["rot_cons"]], ls=u)
                    bank_free2[bi_bank] = pf

                    def part2(u=u, nv=nv, dst_sl=dst_sl, st=st):
                        for hh in range(4):
                            w.op("tensor",
                                 lambda e, o=pTb[u][:, hh * 128:(hh + 1) * 128],
                                 i=qrotb[u][:, hh * 128:(hh + 1) * 128]:
                                 e.transpose(o, i, idb_ap),
                                 inc=(tT if hh == 3 else None),
                                 waits=[(tV, nv), tr_free[u]])
                        tt = (tT, tT.n)
                        st["rot_cons"] = tt
                        cv = w.op("vector",
                                  lambda e, o=dst_sl, i=pT_r[u][:, 0:4, :]:
                                  e.tensor_copy(o, i),
                                  inc=tV, waits=[tt])
                        tr_free[u] = (tV, cv)
                    tr_pending.append(part2)

                def emit_ln1_chunk(tc):
                    bi = tc % 2
                    v = w.op("gpsimd",
                             lambda e, o=xin[bi][:], i=dt["x_kv"][tc]:
                             e.dma_start(out=o, in_=i),
                             inc=qxin[bi], waits=[xin_free[bi]])
                    cv = emit_ln(xin[bi], x1c[bi], [(qxin[bi], v)],
                                 [x1c_free[bi]], eps_ln, ls=bi)
                    xin_free[bi] = (tV, cv)
                    for g in range(2):
                        for ii in range(8):
                            ic = g * 8 + ii
                            w.op("tensor",
                                 lambda e, o=pTb[g][:, ii * 128:(ii + 1) * 128],
                                 i=x1c[bi][:, ic * 128:(ic + 1) * 128]:
                                 e.transpose(o, i, idb_ap),
                                 inc=(tT if ii == 7 else None),
                                 waits=[(tV, cv), tr_free[g], tbl])
                        tt = tT.n
                        cpv = w.op("vector",
                                   lambda e,
                                   o=x1_r[:, g * 8:(g + 1) * 8, tc * 128:tc * 128 + 128],
                                   i=pT_r[g]:
                                   e.tensor_copy(o, i),
                                   inc=tV, waits=[(tT, tt)])
                        tr_free[g] = (tV, cpv)
                    x1c_free[bi] = tr_free[1]
                    x1_ck[tc] = tr_free[1]

                def slab_ap(mat, oc, hs):
                    return dt[mat + "_t"][oc][:, hs * 4096:(hs + 1) * 4096]

                def emit_qkv_tiles(mat, oc, wA, wB, rdyA, rdyB, tiles):
                    last_mm = 0
                    for tc in tiles:
                        bank = [pA, pB][tc % 2]
                        if mat == "wq":
                            loff = QOFF + tc * 128
                            xr = [x1_ck[7 + tc], x1_ck[8 + tc if 8 + tc < UC else UC - 1]]
                        else:
                            loff = tc * 128
                            xr = [x1_ck[tc]]
                        for ic in range(IC):
                            ws16 = wA if ic < 8 else wB
                            w.op("tensor",
                                 lambda e, o=bank[:],
                                 l=x1fm[:, ic * UT + loff:ic * UT + loff + 128],
                                 r=ws16[:, (ic % 8) * 512:(ic % 8 + 1) * 512],
                                 s=(ic == 0), p=(ic == IC - 1):
                                 e.matmul(o, l, r, start=s, stop=p),
                                 inc=(tT if ic == IC - 1 else None),
                                 waits=xr + [rdyA, rdyB, bank_free2[tc % 2]])
                        mmv = (tT, tT.n)
                        last_mm = tT.n
                        if mat == "wv":
                            cv = w.op("vector",
                                      lambda e,
                                      o=v16[:, tc * 512:(tc + 1) * 512],
                                      i=bank[:]: e.tensor_copy(o, i),
                                      inc=tV, waits=[mmv])
                            bank_free2[tc % 2] = (tV, cv)
                        elif mat == "wk":
                            w.wait("vector", [cs_ld])
                            k_r = k16[:].rearrange("p (h t) -> p h t", h=KVH)
                            qkv_unit(bank, mmv,
                                     cosk_sl[:, tc * 256:(tc + 1) * 256],
                                     sink_sl[:, tc * 256:(tc + 1) * 256],
                                     eps_k, 1.0 / 128.0,
                                     k_r[:, :, tc * 128:tc * 128 + 128],
                                     tc % 2)
                        else:
                            q_r = q16[:].rearrange("p (h t) -> p h t", h=H)
                            qkv_unit(bank, mmv,
                                     cosq_sb[:, tc * 256:(tc + 1) * 256],
                                     sinq_sb[:, tc * 256:(tc + 1) * 256],
                                     eps_q, 1.0,
                                     q_r[:, oc * 4:(oc + 1) * 4, tc * 128:tc * 128 + 128],
                                     tc % 2)
                        flush_tr(keep=1)
                    return last_mm

                wstr.prefetch(slab_ap("wk", 0, 0))
                wstr.prefetch(slab_ap("wk", 0, 1))
                emit_ln1_chunk(0)
                biA, rdyA = wstr.take()
                biB, rdyB = wstr.take()
                wkA, wkB = wstr.w16[biA], wstr.w16[biB]
                lm = 0
                for tc in range(1, UC):
                    emit_ln1_chunk(tc)
                    lm = emit_qkv_tiles("wk", 0, wkA, wkB, rdyA, rdyB, [tc - 1])
                lm = emit_qkv_tiles("wk", 0, wkA, wkB, rdyA, rdyB, [UC - 1])
                wstr.used(biA, (tT, lm))
                wstr.used(biB, (tT, lm))
                rest = [("wv", 0)] + [("wq", oc) for oc in range(4)]
                wstr.prefetch(slab_ap("wv", 0, 0))
                wstr.prefetch(slab_ap("wv", 0, 1))
                for si_, (mat, oc) in enumerate(rest):
                    biA, rdyA = wstr.take()
                    biB, rdyB = wstr.take()
                    wA, wB = wstr.w16[biA], wstr.w16[biB]
                    tiles = range(8) if mat == "wq" else range(UC)
                    lm = emit_qkv_tiles(mat, oc, wA, wB, rdyA, rdyB, tiles)
                    wstr.used(biA, (tT, lm))
                    wstr.used(biB, (tT, lm))
                    if si_ + 1 < len(rest):
                        m2_, o2_ = rest[si_ + 1]
                        wstr.prefetch(slab_ap(m2_, o2_, 0))
                        wstr.prefetch(slab_ap(m2_, o2_, 1))
                flush_tr(keep=0)
            qkv_ready = (tV, tV.n)
            barrier()

            if debug:
                for nm, src_ in [("dbg_q16", q16), ("dbg_k16", k16), ("dbg_v16", v16)]:
                    w.op("gpsimd", lambda e, o=dbg[nm], i=src_[:]:
                         e.dma_start(out=o, in_=i), inc=qX, waits=[qkv_ready])

            pcd = contextlib.ExitStack()
            with pcd:
                sb3 = mksb(pcd)
                exp16 = [sb3(f"exp16_{i}", (128, 512), BF16) for i in range(6)]
                rcp = [sb3(f"rcp{i}", (1, 512)) for i in range(2)]
                bcf = [sb3(f"bcf{i}", (128, 512)) for i in range(2)]
                y16 = sb3("y16", (128, H * TOK), BF16)
                attn16 = [sb3(f"attn16_{i}", (128, D), BF16) for i in range(8)]
                ln2c = [sb3(f"ln2c{i}", (128, D), BF16) for i in range(2)]
                ln2c_free = [NONE_T, NONE_T]
                qL = [mksem("qL0", True), mksem("qL1", True)]
                wstr2 = Stream(sb3, "wB", 4096, 2048, cast_eng="vector", nbuf=3)
                pSu = [pcd.enter_context(nc.psum_tensor(f"pSu{i}", [128, 512], F32))
                       for i in range(2)]
                cmask16_sb = sb3("cmask16_sb", (128, 4 * 512), BF16)
                qTb = mksem("qTb", True)
                w.op("gpsimd", lambda e, o=cmask16_sb[:], i=dt["cmask16"]:
                     e.dma_start(out=o, in_=i), inc=qTb)
                cm_ld = (qTb, qTb.n)
                w.wait("tensor", [cm_ld])

                scb = [[pA, pB], [pC, pD]]
                avb = [pS, pR]
                av_free = [NONE_T, NONE_T]
                sc_free = [[NONE_T, NONE_T], [NONE_T, NONE_T]]
                exp_free = [NONE_T] * 6
                sum_free = [NONE_T, NONE_T]
                pending = [None, None]

                def emit_tail(u):
                    h, j, rcpv = pending[u]
                    pending[u] = None
                    sbank = scb[u][1]
                    w.op("tensor",
                         lambda e, o=sbank[:], l=onesr_sb[:], r=rcp[u][:]:
                         e.matmul(o, l, r, start=True, stop=True),
                         inc=tT, waits=[rcpv, sc_free[u][1]])
                    bct = (tT, tT.n)
                    cv2 = w.op("vector",
                               lambda e, o=bcf[u][:], i=sbank[:]:
                               e.tensor_copy(o, i),
                               inc=tV, waits=[bct])
                    sc_free[u][1] = (tV, cv2)
                    yv = w.op("vector",
                              lambda e, o=y16[:, h * TOK + j * CH:h * TOK + (j + 1) * CH],
                              b2=bcf[u][:]:
                              e.tensor_tensor(o, o, b2, ALU.mult),
                              inc=tV, waits=[(tV, cv2)])
                    return yv

                for j in range(NSLOT):
                    for hp in range(H // 2):
                        exvq = {}
                        for kc in range(WC + 1):
                            if kc < WC:
                                uck = 4 * j + kc
                                for u in range(2):
                                    h = 2 * hp + u
                                    kvh = h // 4
                                    ei = u * 3 + kc % 3
                                    bank = scb[u][kc % 2]
                                    w.op("tensor",
                                         lambda e, o=bank[:],
                                         l=k16[:, kvh * UT + uck * 128:kvh * UT + uck * 128 + 128],
                                         r=q16[:, h * TOK + j * CH:h * TOK + (j + 1) * CH],
                                         p=(kc < 8):
                                         e.matmul(o, l, r, start=True, stop=p),
                                         inc=(tT if kc < 8 else None),
                                         waits=[qkv_ready, sc_free[u][kc % 2], tbl])
                                    if kc >= 8:
                                        w.op("tensor",
                                             lambda e, o=bank[:], l=idb_ap,
                                             r=cmask16_sb[:, (kc - 8) * 512:(kc - 7) * 512]:
                                             e.matmul(o, l, r, start=False, stop=True),
                                             inc=tT)
                                    mmv = (tT, tT.n)
                                    ev = w.op("scalar",
                                              lambda e, o=exp16[ei][:], i=bank[:],
                                              b=valid_sb[:, uck:uck + 1]:
                                              e.activation(o, i, AF.Exp, bias=b, scale=1.0),
                                              inc=tS, waits=[mmv, exp_free[ei], tbl])
                                    exvq[(u, kc)] = (tS, ev)
                                    sc_free[u][kc % 2] = (tS, ev)
                            if kc == 2 and pending[0] is not None:
                                emit_tail(0)
                                emit_tail(1)
                            if kc >= 1:
                                kd = kc - 1
                                uckd = 4 * j + kd
                                for u in range(2):
                                    h = 2 * hp + u
                                    kvh = h // 4
                                    ei = u * 3 + kd % 3
                                    w.op("tensor",
                                         lambda e, o=pSu[u][0:1, :], l=onesc_sb[:],
                                         r=exp16[ei][:],
                                         s=(kd == 0), p=(kd == WC - 1):
                                         e.matmul(o, l, r, start=s, stop=p),
                                         inc=tT,
                                         waits=[exvq.pop((u, kd))] + ([sum_free[u]] if kd == 0 else []))
                                    w.op("tensor",
                                         lambda e, o=avb[u][:],
                                         l=v16[:, uckd * 512 + kvh * 128:uckd * 512 + kvh * 128 + 128],
                                         r=exp16[ei][:],
                                         s=(kd == 0), p=(kd == WC - 1):
                                         e.matmul(o, l, r, start=s, stop=p),
                                         inc=tT,
                                         waits=([av_free[u]] if kd == 0 else []))
                                    exp_free[ei] = (tT, tT.n)
                        av_done = (tT, tT.n)
                        for u in range(2):
                            h = 2 * hp + u
                            cv = w.op("vector",
                                      lambda e, o=y16[:, h * TOK + j * CH:h * TOK + (j + 1) * CH],
                                      i=avb[u][:]: e.tensor_copy(o, i),
                                      inc=tV, waits=[av_done])
                            av_free[u] = (tV, cv)
                            lv = w.op("scalar",
                                      lambda e, o=rcp[u][:], i=pSu[u][0:1, :]:
                                      e.activation(o, i, AF.Ln, scale=1.0),
                                      inc=tS, waits=[av_done])
                            sum_free[u] = (tS, lv)
                            rv = w.op("scalar",
                                      lambda e, o=rcp[u][:]:
                                      e.activation(o, o, AF.Exp, scale=-1.0),
                                      inc=tS, waits=[(tS, lv)])
                            pending[u] = (h, j, (tS, rv))
                emit_tail(0)
                emit_tail(1)
                y_ready = (tV, tV.n)

                if debug:
                    w.op("gpsimd", lambda e, o=dbg["dbg_y16"], i=y16[:]:
                         e.dma_start(out=o, in_=i), inc=qX, waits=[y_ready])

                attn_parts = [0] * 8
                ob_free = [NONE_T, NONE_T]
                wstr2.prefetch(dt["wo_t"][0][:, 0:4096])
                wstr2.prefetch(dt["wo_t"][0][:, 4096:8192])
                for oc in range(4):
                    biA2, rdyA2 = wstr2.take()
                    biB2, rdyB2 = wstr2.take()
                    woA, woB = wstr2.w16[biA2], wstr2.w16[biB2]
                    last_mm = 0
                    for tq in range(8):
                        bank = [pA, pB][tq % 2]
                        for ic in range(IC):
                            ws16 = woA if ic < 8 else woB
                            w.op("tensor",
                                 lambda e, o=bank[:],
                                 l=y16[:, ic * TOK + tq * 128:ic * TOK + tq * 128 + 128],
                                 r=ws16[:, (ic % 8) * 512:(ic % 8 + 1) * 512],
                                 s=(ic == 0), p=(ic == IC - 1):
                                 e.matmul(o, l, r, start=s, stop=p),
                                 inc=(tT if ic == IC - 1 else None),
                                 waits=[y_ready, rdyA2, rdyB2, ob_free[tq % 2]])
                        mmv = (tT, tT.n)
                        last_mm = tT.n
                        cv = w.op("vector",
                                  lambda e, o=attn16[tq][:, oc * 512:(oc + 1) * 512],
                                  i=bank[:]: e.tensor_copy(o, i),
                                  inc=tV, waits=[mmv])
                        ob_free[tq % 2] = (tV, cv)
                        attn_parts[tq] = cv
                    wstr2.used(biA2, (tT, last_mm))
                    wstr2.used(biB2, (tT, last_mm))
                    if oc + 1 < 4:
                        wstr2.prefetch(dt["wo_t"][oc + 1][:, 0:4096])
                        wstr2.prefetch(dt["wo_t"][oc + 1][:, 4096:8192])
                for tq in range(8):
                    bi = tq % 2
                    if debug:
                        w.op("gpsimd",
                             lambda e, o=dbg["dbg_attn"][tq], i=attn16[tq][:]:
                             e.dma_start(out=o, in_=i),
                             inc=qX, waits=[(tV, attn_parts[tq])])
                    cv = emit_ln(attn16[tq], ln2c[bi], [(tV, attn_parts[tq])],
                                 [ln2c_free[bi]], eps_ln, ls=bi)
                    sv = w.op("gpsimd",
                              lambda e, o=ln2a_d[tq], i=ln2c[bi][:]:
                              e.dma_start(out=o, in_=i),
                              inc=qL[bi], waits=[(tV, cv)])
                    ln2c_free[bi] = (qL[bi], sv)
                    ln2_stores += 1
                    if debug:
                        w.op("gpsimd",
                             lambda e, o=dbg["dbg_ln2a"][tq], i=ln2c[bi][:]:
                             e.dma_start(out=o, in_=i), inc=qX)
        ln2_done2 = [(qL[0], qL[0].n), (qL[1], qL[1].n)]

        p2 = contextlib.ExitStack()
        with p2:
            sbf = mksb(p2)
            idf_sb = sbf("idf_sb", (128, 128))
            gate_sb = sbf("gate_sb", (128, IC * 9))
            qTc = mksem("qTc", True)
            w.op("gpsimd", lambda e, o=idf_sb[:], i=dt["idf"]:
                 e.dma_start(out=o, in_=i), inc=qTc)
            w.op("gpsimd", lambda e, o=gate_sb[:], i=dt["gate_pack"]:
                 e.dma_start(out=o, in_=i), inc=qTc)
            p2tbl = (qTc, qTc.n)
            w.wait("tensor", [p2tbl])
            w.wait("vector", [p2tbl])
            idf_ap = idf_sb[:]
            x3fm = sbf("x3fm", (128, IC * TOK), BF16)
            moe = sbf("moe", (128, NTC * D), BF16)
            comb = sbf("comb", (128, NTC * 8))
            sgv = sbf("sgv", (128, NTC))
            x3_r = x3fm[:].rearrange("p (ic t) -> p ic t", ic=IC)

            barrier()
            p2a = contextlib.ExitStack()
            with p2a:
                sba = mksb(p2a)
                xin2 = [sba(f"xin2_{i}", (128, D)) for i in range(2)]
                xin2_free = [NONE_T, NONE_T]
                l2c = [sba(f"l2c{i}", (128, D), BF16) for i in range(2)]
                l2c_free = [NONE_T, NONE_T]
                qxin2 = [mksem("qxin2_0", True), mksem("qxin2_1", True)]
                ql2c = [mksem("ql2c_0", True), mksem("ql2c_1", True)]
                x2c2 = [sba("x2c", (128, D)) for _ in range(2)]
                x3c2 = [sba("x3c", (128, D)) for _ in range(2)]
                xf322 = [sba("xf32", (128, D)) for _ in range(2)]
                r9 = sba("r9", (9, 128))
                pr9 = sba("pr9", (128, 9))
                prb = sba("prb", (128, 8))
                m1 = sba("m1", (128, 1))
                m2 = sba("m2", (128, 1))
                sm = sba("sm", (128, 1))
                eqb = sba("eqb", (128, 8))

                x2c_free = [NONE_T, NONE_T]
                x3c_free = [NONE_T, NONE_T]
                xf32_free = [NONE_T, NONE_T]
                x3_ck = [NONE_T] * NTC
                comb_ck = [NONE_T] * NTC
                pR_free = NONE_T
                pDt_free = NONE_T
                r9_cons = NONE_T
                pr9_cons = NONE_T
                trb_free = [NONE_T, NONE_T]
                for tc in range(NTC):
                    bi = tc % 2
                    x2c, x3c, xf32 = x2c2[bi], x3c2[bi], xf322[bi]
                    v1 = w.op("gpsimd",
                              lambda e, o=xin2[bi][:], i=dt["x_own"][tc]:
                              e.dma_start(out=o, in_=i),
                              inc=qxin2[bi], waits=[xin2_free[bi]] + ln2_done2)
                    v2 = w.op("gpsimd",
                              lambda e, o=l2c[bi][:], i=ln2a_d[tc]:
                              e.dma_start(out=o, in_=i),
                              inc=ql2c[bi], waits=[l2c_free[bi]])
                    cv = w.op("vector",
                              lambda e, o=x2c[:], a=xin2[bi][:], b=l2c[bi][:]:
                              e.tensor_tensor(o, a, b, ALU.add),
                              inc=tV, waits=[(qxin2[bi], v1), (ql2c[bi], v2), x2c_free[bi]])
                    xin2_free[bi] = (tV, cv)
                    l2c_free[bi] = (tV, cv)
                    lv = emit_ln(x2c, x3c, [(tV, cv)], [x3c_free[bi]], eps_ln, ls=bi)
                    x2c_free[bi] = (tV, lv)
                    grp_cp = 0
                    for g in range(4):
                        bank = [pA, pB][g % 2]
                        for ii in range(4):
                            ic = g * 4 + ii
                            w.op("tensor",
                                 lambda e, o=bank[:, ii * 128:(ii + 1) * 128],
                                 i=x3c[:, ic * 128:(ic + 1) * 128]:
                                 e.transpose(o, i, idf_ap),
                                 inc=(tT if ii == 3 else None),
                                 waits=[(tV, lv), trb_free[g % 2], tbl])
                        tt = (tT, tT.n)
                        w.op("vector",
                             lambda e,
                             o=x3_r[:, g * 4:(g + 1) * 4, tc * 128:tc * 128 + 128],
                             i=bank[:].rearrange("p (ii t) -> p ii t", ii=4):
                             e.tensor_copy(o, i),
                             waits=[tt])
                        cv2 = w.op("vector",
                                   lambda e, o=xf32[:, g * 512:(g + 1) * 512], i=bank[:]:
                                   e.tensor_copy(o, i),
                                   inc=tV, waits=[tt, xf32_free[bi]])
                        trb_free[g % 2] = (tV, cv2)
                        grp_cp = cv2
                    x3c_free[bi] = (tV, grp_cp)
                    x3_ck[tc] = (tV, grp_cp)
                    xf_rdy = (tV, grp_cp)
                    for ic in range(IC):
                        w.op("tensor",
                             lambda e, o=pR[0:9, 0:128],
                             l=gate_sb[:, ic * 9:(ic + 1) * 9],
                             r=xf32[:, ic * 128:(ic + 1) * 128],
                             s=(ic == 0), p=(ic == IC - 1):
                             e.matmul(o, l, r, start=s, stop=p),
                             inc=(tT if ic == IC - 1 else None),
                             waits=[xf_rdy, tbl, pR_free])
                    rmm = (tT, tT.n)
                    xf32_free[bi] = rmm
                    cvr = w.op("vector", lambda e, o=r9[:], i=pR[0:9, 0:128]:
                               e.tensor_copy(o, i),
                               inc=tV, waits=[rmm, r9_cons])
                    pR_free = (tV, cvr)
                    w.op("tensor",
                         lambda e, o=pD[0:128, 0:9], i=r9[:], d9=idf_sb[0:9, 0:9]:
                         e.transpose(o, i, d9),
                         inc=tT, waits=[(tV, cvr), pDt_free])
                    r9_cons = (tT, tT.n)
                    cvp = w.op("vector", lambda e, o=pr9[:], i=pD[0:128, 0:9]:
                               e.tensor_copy(o, i),
                               inc=tV, waits=[r9_cons, pr9_cons])
                    pDt_free = (tV, cvp)
                    w.op("vector", lambda e, o=m1[:], i=pr9[:, 0:8]:
                         e.reduce_max(o, i, axis=AX.X),
                         inc=tV, waits=[(tV, cvp), (tS, tS.n)])
                    w.op("vector", lambda e, o=prb[:], i=pr9[:, 0:8], s=m1[:]:
                         e.tensor_scalar(o, i, s, None, ALU.subtract),
                         inc=tV, waits=[(tV, tV.n)])
                    av_ = tV.n
                    w.op("scalar", lambda e, o=prb[:]:
                         e.activation(o, o, AF.Exp, scale=1.0),
                         inc=tS, waits=[(tV, av_)])
                    ev = (tS, tS.n)
                    w.op("vector", lambda e, o=sm[:], i=prb[:]:
                         e.reduce_sum(o, i, axis=AX.X),
                         inc=tV, waits=[ev])
                    w.op("vector", lambda e, o=sm[:]: e.reciprocal(o, o),
                         inc=tV, waits=[(tV, tV.n)])
                    w.op("vector", lambda e, o=prb[:], s=sm[:]:
                         e.tensor_scalar(o, o, s, None, ALU.mult),
                         inc=tV, waits=[(tV, tV.n)])
                    w.op("vector", lambda e, o=m1[:], i=prb[:]:
                         e.reduce_max(o, i, axis=AX.X),
                         inc=tV, waits=[(tV, tV.n)])
                    w.op("vector", lambda e, o=eqb[:], i=prb[:], s=m1[:]:
                         e.tensor_scalar(o, i, s, None, ALU.is_equal),
                         inc=tV, waits=[(tV, tV.n)])
                    w.op("vector", lambda e, o=eqb[:], i=prb[:]:
                         e.scalar_tensor_tensor(o, o, 1e30, i, ALU.mult, ALU.subtract),
                         inc=tV, waits=[(tV, tV.n)])
                    w.op("vector", lambda e, o=m2[:], i=eqb[:]:
                         e.tensor_reduce(o, i, axis=AX.X, op=ALU.min, negate=True),
                         inc=tV, waits=[(tV, tV.n)])
                    w.op("vector", lambda e, o=eqb[:], i=prb[:], s=m2[:]:
                         e.tensor_scalar(o, i, s, None, ALU.is_ge),
                         inc=tV, waits=[(tV, tV.n)])
                    w.op("vector",
                         lambda e, o=comb[:, tc * 8:(tc + 1) * 8], a=prb[:], b=eqb[:]:
                         e.tensor_tensor(o, a, b, ALU.mult),
                         inc=tV, waits=[(tV, tV.n)])
                    w.op("scalar",
                         lambda e, o=m1[:], i=pr9[:, 8:9]:
                         e.activation(o, i, AF.Tanh, scale=0.5),
                         inc=tS, waits=[(tV, tV.n)])
                    pr9_cons = (tS, tS.n)
                    w.op("vector",
                         lambda e, o=sgv[:, tc:tc + 1], i=m1[:]:
                         e.tensor_scalar(o, i, 0.5, 0.5, ALU.mult, ALU.add),
                         inc=tV, waits=[(tS, tS.n)])
                    comb_ck[tc] = (tV, tV.n)
                x3_ready = (tV, tV.n)
                comb_ready = (tV, tV.n)

            if debug:
                for nm, src in [("dbg_x3fm", x3fm), ("dbg_comb", comb), ("dbg_sg", sgv)]:
                    w.op("gpsimd", lambda e, o=dbg[nm], i=src[:]:
                         e.dma_start(out=o, in_=i), inc=qX, waits=[x3_ready])

            barrier()
            p2b = contextlib.ExitStack()
            with p2b:
                sbb = mksb(p2b)
                he2 = [sbb("he", (128, 8 * TOK), BF16) for _ in range(2)]
                silu_f = [sbb(f"silu{i}", (128, 512)) for i in range(2)]
                silu_free = [NONE_T, NONE_T]
                gS = Stream(sbb, "wg", 2048, 2048, cast_eng="vector")
                uS = Stream(sbb, "wu", 2048, 2048, cast_eng="scalar")
                dS = Stream(sbb, "wd", 4096, 2048, cast_eng="scalar")

                gb = [pA, pB]
                ub = [pC, pD]
                db = [pS, pR]
                gb_free = [NONE_T, NONE_T]
                ub_free = [NONE_T, NONE_T]
                db_free = [NONE_T, NONE_T]
                he_free = [NONE_T, NONE_T]
                moe_parts = NONE_T
                for pf in range(2):
                    gS.prefetch(dt["wg_t"][pf])
                    uS.prefetch(dt["wu_t"][pf])
                dS.prefetch(dt["wd_t"][0])
                dS.prefetch(dt["wd_t"][1])
                gu_next = 2
                d_next = 2
                for ee in range(E):
                    he = he2[ee % 2]
                    he_done = 0
                    for ici in range(8):
                        bg, grdy = gS.take()
                        bu, urdy = uS.take()
                        g16, u16 = gS.w16[bg], uS.w16[bu]
                        lastg = lastu = 0
                        for t in range(2):
                            gi = (ici * 2 + t) % 2
                            for ic in range(IC):
                                w.op("tensor",
                                     lambda e, o=gb[gi][:],
                                     l=g16[:, ic * 128:(ic + 1) * 128],
                                     r=x3fm[:, ic * TOK + t * 512:ic * TOK + (t + 1) * 512],
                                     s=(ic == 0), p=(ic == IC - 1):
                                     e.matmul(o, l, r, start=s, stop=p),
                                     inc=(tT if ic == IC - 1 else None),
                                     waits=[x3_ck[4 * t], x3_ck[4 * t + 1],
                                            x3_ck[4 * t + 2], x3_ck[4 * t + 3],
                                            grdy, gb_free[gi]])
                            gmm = (tT, tT.n)
                            lastg = tT.n
                            for ic in range(IC):
                                w.op("tensor",
                                     lambda e, o=ub[gi][:],
                                     l=u16[:, ic * 128:(ic + 1) * 128],
                                     r=x3fm[:, ic * TOK + t * 512:ic * TOK + (t + 1) * 512],
                                     s=(ic == 0), p=(ic == IC - 1):
                                     e.matmul(o, l, r, start=s, stop=p),
                                     inc=(tT if ic == IC - 1 else None),
                                     waits=[urdy, ub_free[gi]])
                            umm = (tT, tT.n)
                            lastu = tT.n
                            si = gi
                            sv = w.op("scalar",
                                      lambda e, o=silu_f[si][:], i=gb[gi][:]:
                                      e.activation(o, i, AF.Silu, scale=1.0),
                                      inc=tS, waits=[gmm, silu_free[si]])
                            gb_free[gi] = (tS, sv)
                            hv = w.op("vector",
                                      lambda e,
                                      o=he[:, ici * 1024 + t * 512:ici * 1024 + (t + 1) * 512],
                                      a=silu_f[si][:], b=ub[gi][:]:
                                      e.tensor_tensor(o, a, b, ALU.mult),
                                      inc=tV, waits=[(tS, sv), umm, he_free[ee % 2]])
                            silu_free[si] = (tV, hv)
                            ub_free[gi] = (tV, hv)
                            he_done = hv
                        gS.used(bg, (tT, lastg))
                        uS.used(bu, (tT, lastu))
                        if gu_next < E * 8:
                            gS.prefetch(dt["wg_t"][gu_next])
                            uS.prefetch(dt["wu_t"][gu_next])
                            gu_next += 1
                    he_ready = (tV, he_done)
                    if debug and ee == 0:
                        w.op("gpsimd", lambda e, o=dbg["dbg_he"], i=he[:]:
                             e.dma_start(out=o, in_=i), inc=qX, waits=[he_ready])
                    lastd_all = 0
                    for oc in range(4):
                        bd, drdy = dS.take()
                        d16 = dS.w16[bd]
                        lastd = 0
                        for tc in range(NTC):
                            di = tc % 2
                            for ici in range(8):
                                w.op("tensor",
                                     lambda e, o=db[di][:],
                                     l=he[:, ici * 1024 + tc * 128:ici * 1024 + tc * 128 + 128],
                                     r=d16[:, ici * 512:(ici + 1) * 512],
                                     s=(ici == 0), p=(ici == 7):
                                     e.matmul(o, l, r, start=s, stop=p),
                                     inc=(tT if ici == 7 else None),
                                     waits=[he_ready, drdy, db_free[di]])
                            mmv = (tT, tT.n)
                            lastd = tT.n
                            dst = moe[:, tc * D + oc * 512:tc * D + (oc + 1) * 512]
                            if ee == 0:
                                cv = w.op("vector",
                                          lambda e, o=dst, i=db[di][:],
                                          s=comb[:, tc * 8 + ee:tc * 8 + ee + 1]:
                                          e.tensor_scalar(o, i, s, None, ALU.mult),
                                          inc=tV, waits=[mmv, comb_ck[tc]])
                            else:
                                cv = w.op("vector",
                                          lambda e, o=dst, i=db[di][:],
                                          s=comb[:, tc * 8 + ee:tc * 8 + ee + 1]:
                                          e.scalar_tensor_tensor(o, i, s, o,
                                                                 ALU.mult, ALU.add),
                                          inc=tV, waits=[mmv, comb_ready, moe_parts])
                            db_free[di] = (tV, cv)
                        dS.used(bd, (tT, lastd))
                        if d_next < E * 4:
                            dS.prefetch(dt["wd_t"][d_next])
                            d_next += 1
                        lastd_all = lastd
                    moe_parts = (tV, tV.n)
                    he_free[ee % 2] = (tT, lastd_all)

            barrier()
            p2c = contextlib.ExitStack()
            with p2c:
                sbc = mksb(p2c)
                sh2 = [sbc("sh", (128, 32 * 512), BF16)]
                silu_f = [sbc(f"silu2_{i}", (128, 512)) for i in range(2)]
                silu_free = [NONE_T, NONE_T]
                gS = Stream(sbc, "wsg", 2048, 1024, cast_eng="vector")
                uS = Stream(sbc, "wsu", 2048, 1024, cast_eng="scalar")
                dS = Stream(sbc, "wsd", 4096, 2048, cast_eng="scalar")

                pA_free = NONE_T
                pB_free = NONE_T
                banks4 = [pC, pD, pS, pR]
                psum4_free = [NONE_T] * 4
                moe_tc = [NONE_T] * NTC
                sh_cons = NONE_T
                moe_sh = moe_parts
                for t in range(2):
                    sh = sh2[0]
                    sh_done = 0
                    for ici in range(32):
                        bg, grdy = gS.load(dt["wsg_t"][ici])
                        bu, urdy = uS.load(dt["wsu_t"][ici])
                        g16, u16 = gS.w16[bg], uS.w16[bu]
                        for ic in range(IC):
                            w.op("tensor",
                                 lambda e, o=pA[:],
                                 l=g16[:, ic * 128:(ic + 1) * 128],
                                 r=x3fm[:, ic * TOK + t * 512:ic * TOK + (t + 1) * 512],
                                 s=(ic == 0), p=(ic == IC - 1):
                                 e.matmul(o, l, r, start=s, stop=p),
                                 inc=(tT if ic == IC - 1 else None),
                                 waits=[x3_ready, grdy, pA_free])
                        gmm = (tT, tT.n)
                        lastg = tT.n
                        for ic in range(IC):
                            w.op("tensor",
                                 lambda e, o=pB[:],
                                 l=u16[:, ic * 128:(ic + 1) * 128],
                                 r=x3fm[:, ic * TOK + t * 512:ic * TOK + (t + 1) * 512],
                                 s=(ic == 0), p=(ic == IC - 1):
                                 e.matmul(o, l, r, start=s, stop=p),
                                 inc=(tT if ic == IC - 1 else None),
                                 waits=[urdy, pB_free])
                        umm = (tT, tT.n)
                        si = ici % 2
                        sv = w.op("scalar",
                                  lambda e, o=silu_f[si][:], i=pA[:]:
                                  e.activation(o, i, AF.Silu, scale=1.0),
                                  inc=tS, waits=[gmm, silu_free[si]])
                        pA_free = (tS, sv)
                        hv = w.op("vector",
                                  lambda e, o=sh[:, ici * 512:(ici + 1) * 512],
                                  a=silu_f[si][:], b=pB[:]:
                                  e.tensor_tensor(o, a, b, ALU.mult),
                                  inc=tV, waits=[(tS, sv), umm, sh_cons])
                        silu_free[si] = (tV, hv)
                        pB_free = (tV, hv)
                        sh_done = hv
                        gS.used(bg, (tT, lastg))
                        uS.used(bu, (tT, tT.n))
                    sh_ready = (tV, sh_done)
                    for oc in range(4):
                        mm_last = [0, 0, 0, 0]
                        for icg in range(4):
                            bd, drdy = dS.load(dt["wsd_t"][oc * 4 + icg])
                            d16 = dS.w16[bd]
                            for tl in range(4):
                                pb = banks4[tl]
                                for icl in range(8):
                                    ici = icg * 8 + icl
                                    w.op("tensor",
                                         lambda e, o=pb[:],
                                         l=sh[:, ici * 512 + tl * 128:ici * 512 + tl * 128 + 128],
                                         r=d16[:, icl * 512:(icl + 1) * 512],
                                         s=(icg == 0 and icl == 0),
                                         p=(icg == 3 and icl == 7):
                                         e.matmul(o, l, r, start=s, stop=p),
                                         inc=(tT if icl == 7 else None),
                                         waits=[sh_ready, drdy, psum4_free[tl]])
                                mm_last[tl] = tT.n
                            dS.used(bd, (tT, max(mm_last)))
                        for tl in range(4):
                            tc = t * 4 + tl
                            dst = moe[:, tc * D + oc * 512:tc * D + (oc + 1) * 512]
                            cv = w.op("vector",
                                      lambda e, o=dst, i=banks4[tl][:],
                                      s=sgv[:, tc:tc + 1]:
                                      e.scalar_tensor_tensor(o, i, s, o,
                                                             ALU.mult, ALU.add),
                                      inc=tV,
                                      waits=[(tT, mm_last[tl]), moe_sh, comb_ready])
                            psum4_free[tl] = (tV, cv)
                            if oc == 3:
                                moe_tc[tc] = (tV, cv)
                    sh_cons = (tT, tT.n)
                moe_ready = (tV, tV.n)

                if debug:
                    for tc in range(NTC):
                        w.op("gpsimd",
                             lambda e, o=dbg["dbg_acc"][tc], i=moe[:, tc * D:(tc + 1) * D]:
                             e.dma_start(out=o, in_=i),
                             inc=qX, waits=[moe_ready])

                sbd = sbc
                xin3 = [sbd("xin3_0", (128, D))]
                xin3_free = [NONE_T]
                l2c2 = [sbd(f"l2c2_{i}", (128, D), BF16) for i in range(2)]
                l2c2_free = [NONE_T, NONE_T]
                qxin3 = [mksem("qxin3_0", True)]
                ql2c2 = [mksem("ql2c2_0", True), mksem("ql2c2_1", True)]
                qP = [mksem("qP0", True), mksem("qP1", True)]
                t42 = [sbd("t4_0", (128, D))]
                outc = [sbd(f"outc{i}", (128, D)) for i in range(2)]
                outc_free = [NONE_T, NONE_T]
                t4_free = [NONE_T]

                for tc in range(NTC):
                    bi = tc % 2
                    v1 = w.op("gpsimd",
                              lambda e, o=xin3[0][:], i=dt["x_own"][tc]:
                              e.dma_start(out=o, in_=i),
                              inc=qxin3[0], waits=[xin3_free[0]])
                    v2 = w.op("gpsimd",
                              lambda e, o=l2c2[bi][:], i=ln2a_d[tc]:
                              e.dma_start(out=o, in_=i),
                              inc=ql2c2[bi], waits=[l2c2_free[bi]])
                    t4 = t42[0]
                    lv = emit_ln(moe[:, tc * D:(tc + 1) * D], t4, [moe_tc[tc]],
                                 [t4_free[0]], eps_ln, ls=tc % 2)
                    ov = w.op("vector",
                              lambda e, o=outc[bi][:], a=xin3[0][:], b=l2c2[bi][:]:
                              e.tensor_tensor(o, a, b, ALU.add),
                              inc=tV, waits=[(qxin3[0], v1), (ql2c2[bi], v2), outc_free[bi]])
                    ov2 = w.op("vector",
                               lambda e, o=outc[bi][:], b=t4[:]:
                               e.tensor_tensor(o, o, b, ALU.add),
                               inc=tV, waits=[(tV, lv), (tV, ov)])
                    t4_free[0] = (tV, ov2)
                    xin3_free[0] = (tV, ov)
                    l2c2_free[bi] = (tV, ov2)
                    sv = w.op("gpsimd",
                              lambda e, o=out_d[tc], i=outc[bi][:]:
                              e.dma_start(out=o, in_=i),
                              inc=qP[bi], waits=[(tV, ov2)])
                    outc_free[bi] = (qP[bi], sv)

        w.wait("gpsimd", [(s, s.n) for s in ALLSEMS if s.dma])

        with nc.Block() as blk:
            @blk.sync
            def _(eng):
                for f in w.prog["sync"]:
                    f(eng)

            @blk.tensor
            def _(eng):
                for f in w.prog["tensor"]:
                    f(eng)

            @blk.vector
            def _(eng):
                for f in w.prog["vector"]:
                    f(eng)

            @blk.scalar
            def _(eng):
                for f in w.prog["scalar"]:
                    f(eng)

            @blk.gpsimd
            def _(eng):
                for f in w.prog["gpsimd"]:
                    f(eng)

    return nc


def kernel(**inputs):
    in_maps = host_prepare(inputs)
    nc = build_nc()
    res = run_bass_kernel_spmd(nc, in_maps, core_ids=list(range(NCORE)))
    out = np.empty((B, T, D), dtype=np.float32)
    for core in range(NCORE):
        b, half = core // 2, core % 2
        out[b, half * TOK:(half + 1) * TOK] = res.results[core]["out"].reshape(TOK, D)
    return out


# version 23
# speedup vs baseline: 1.0835x; 1.0386x over previous
import contextlib

import numpy as np
import ml_dtypes

import concourse.bass as bass
import concourse.mybir as mybir
from concourse.bass_utils import run_bass_kernel_spmd

F32 = mybir.dt.float32
BF16 = mybir.dt.bfloat16
AF = mybir.ActivationFunctionType
ALU = mybir.AluOpType
AX = mybir.AxisListType
BF = ml_dtypes.bfloat16

B, T, D = 4, 2048, 2048
H, KVH, DH = 16, 4, 128
IC = 16
CH = 512
WT = 1536
WC = 12
QOFF = 1023
NSLOT = 2
HT = 768
HC = 6
E, IEXP, ISH = 8, 1024, 4096
TOK, NTC = 1024, 8
UT, UC = 2048, 16
NCORE = 8
THETA = 10000.0


_PERM = np.empty(128, dtype=np.int64)
_PERM[:64] = np.arange(64) * 2
_PERM[64:] = np.arange(64) * 2 + 1


def _perm_heads(w, nheads):
    return w.reshape(nheads, 128, D)[:, _PERM, :].reshape(nheads * 128, D)


def _tile_rhs_out512(wT):
    nin, nout = wT.shape
    a = wT.reshape(nin // 128, 128, nout // 512, 512)
    return np.ascontiguousarray(
        a.transpose(2, 1, 0, 3).reshape(nout // 512, 128, (nin // 128) * 512))


def _tile_gu(w):
    I_, Din = w.shape
    a = w.reshape(I_ // 128, 128, Din // 128, 128)
    return np.ascontiguousarray(
        a.transpose(0, 3, 2, 1).reshape(I_ // 128, 128, (Din // 128) * 128))


_ROPE_CACHE = {}


def _rope_base():
    if "cs" in _ROPE_CACHE:
        return _ROPE_CACHE["cs"]
    import ctypes
    libm = ctypes.CDLL("libm.so.6")
    libm.cosf.restype = ctypes.c_float
    libm.cosf.argtypes = [ctypes.c_float]
    libm.sinf.restype = ctypes.c_float
    libm.sinf.argtypes = [ctypes.c_float]
    af = (np.arange(0, DH, 2, dtype=np.float32) / np.float32(DH)).astype(np.float32)
    p = np.power(np.float64(THETA), af.astype(np.float64)).astype(np.float32)
    inv = (np.float32(1.0) / p).astype(np.float32)
    ang = (np.arange(T, dtype=np.float32)[:, None] * inv[None, :]).astype(np.float32)
    flat = ang.ravel()
    cos = np.fromiter((libm.cosf(float(v)) for v in flat), np.float32,
                      count=flat.size).reshape(ang.shape)
    sin = np.fromiter((libm.sinf(float(v)) for v in flat), np.float32,
                      count=flat.size).reshape(ang.shape)
    _ROPE_CACHE["cs"] = (cos, sin)
    return cos, sin


def _rope_tables(positions, reps):
    cos_t, sin_t = _rope_base()
    idx = np.clip(positions.astype(np.int64), 0, T - 1)
    return (np.tile(cos_t[idx], (1, reps)), np.tile(sin_t[idx], (1, reps)))


def host_prepare(inputs):
    x = np.asarray(inputs["L0_attn_x"], dtype=np.float32)
    wq = np.asarray(inputs["L0_attn_wq"], dtype=np.float32)
    wk = np.asarray(inputs["L0_attn_wk"], dtype=np.float32)
    wv = np.asarray(inputs["L0_attn_wv"], dtype=np.float32)
    wo = np.asarray(inputs["L0_attn_wo"], dtype=np.float32)
    gw = np.asarray(inputs["L0_ffn_gate_w"], dtype=np.float32)
    egw = np.asarray(inputs["L0_ffn_expert_gate_w"], dtype=np.float32)
    euw = np.asarray(inputs["L0_ffn_expert_up_w"], dtype=np.float32)
    edw = np.asarray(inputs["L0_ffn_expert_down_w"], dtype=np.float32)
    sgw = np.asarray(inputs["L0_ffn_shared_gate_w"], dtype=np.float32)
    suw = np.asarray(inputs["L0_ffn_shared_up_w"], dtype=np.float32)
    sdw = np.asarray(inputs["L0_ffn_shared_down_w"], dtype=np.float32)
    seg = np.asarray(inputs["L0_ffn_shared_expert_gate_w"], dtype=np.float32)

    shared = {}
    shared["wq_t"] = _tile_rhs_out512(np.ascontiguousarray(_perm_heads(wq, H).T))
    shared["wk_t"] = _tile_rhs_out512(np.ascontiguousarray(_perm_heads(wk, KVH).T))
    shared["wv_t"] = _tile_rhs_out512(np.ascontiguousarray(wv.T))
    shared["wo_t"] = _tile_rhs_out512(np.ascontiguousarray(wo.T))
    gp = np.concatenate([gw.T, seg.T], axis=1)
    shared["gate_pack"] = np.ascontiguousarray(
        gp.reshape(IC, 128, 9).transpose(1, 0, 2).reshape(128, IC * 9))
    shared["wg_t"] = np.concatenate([_tile_gu(egw[e]) for e in range(E)])
    shared["wu_t"] = np.concatenate([_tile_gu(euw[e]) for e in range(E)])
    shared["wd_t"] = np.concatenate(
        [_tile_rhs_out512(np.ascontiguousarray(edw[e].T)) for e in range(E)])
    shared["wsg_t"] = _tile_gu(sgw)
    shared["wsu_t"] = _tile_gu(suw)
    wsd = _tile_rhs_out512(np.ascontiguousarray(sdw.T))
    shared["wsd_t"] = np.ascontiguousarray(
        wsd.reshape(4, 128, 4, 8 * 512).transpose(0, 2, 1, 3).reshape(16, 128, 4096))

    shared["idf"] = np.eye(128, dtype=np.float32)
    shared["idb"] = np.eye(128).astype(BF)
    shared["ones_col_b"] = np.ones((128, 1), dtype=BF)
    shared["ones_row_f"] = np.ones((1, 128), dtype=np.float32)
    d_ = np.arange(4)[:, None, None]
    p_ = np.arange(128)[None, :, None]
    i_ = np.arange(512)[None, None, :]
    cmask = np.where(i_ <= d_ * 128 + p_, np.float32(-1e30), np.float32(0.0))
    shared["cmask"] = np.ascontiguousarray(
        cmask.transpose(1, 0, 2).reshape(128, 4 * 512))
    shared["cmask16"] = shared["cmask"].astype(BF)

    in_maps = []
    for core in range(NCORE):
        b, half = core // 2, core % 2
        chunks = (2 * half, 2 * half + 1)
        m = dict(shared)
        ubase = half * 1024 - QOFF
        kpos_u = ubase + np.arange(UT)
        okmask = (kpos_u >= 0) & (kpos_u < T)
        xkv = np.zeros((UT, D), dtype=np.float32)
        xkv[okmask] = x[b, kpos_u[okmask]]
        val = np.where(okmask, np.float32(-12.0), np.float32(-1e9))
        m["x_kv"] = np.ascontiguousarray(xkv.reshape(UC, 128, D))
        m["x_own"] = np.ascontiguousarray(
            x[b, half * TOK:(half + 1) * TOK].reshape(NTC, 128, D))
        m["valid12"] = np.ascontiguousarray(val.reshape(UC, 128).T)
        cosk, sink = _rope_tables(np.maximum(kpos_u, 0).astype(np.float64), KVH)
        m["cosk"] = np.ascontiguousarray(
            cosk.reshape(UC, 128, 256).transpose(1, 0, 2)
            .reshape(128, UC * 256))
        m["sink"] = np.ascontiguousarray(
            sink.reshape(UC, 128, 256).transpose(1, 0, 2)
            .reshape(128, UC * 256))
        qpos = (half * TOK + np.arange(TOK)).astype(np.float64)
        cosq, sinq = _rope_tables(qpos, 4)
        m["cosq"] = np.ascontiguousarray(
            cosq.reshape(NTC, 128, 256).transpose(1, 0, 2).reshape(128, NTC * 256))
        m["sinq"] = np.ascontiguousarray(
            sinq.reshape(NTC, 128, 256).transpose(1, 0, 2).reshape(128, NTC * 256))
        in_maps.append(m)
    return in_maps



class Sem:
    __slots__ = ("h", "n", "dma")

    def __init__(self, h, dma):
        self.h, self.n, self.dma = h, 0, dma


NONE_T = (None, 0)


class Walk:
    ENGINES = ("sync", "tensor", "vector", "scalar", "gpsimd")

    def __init__(self):
        self.prog = {e: [] for e in self.ENGINES}
        self.hw = {e: {} for e in self.ENGINES}

    def _filter(self, eng, waits):
        hw = self.hw[eng]
        ws = []
        for s, v in waits:
            if s is None or v is None or v <= 0:
                continue
            if hw.get(id(s), -1) >= v:
                continue
            hw[id(s)] = v
            ws.append((s.h, v))
        return tuple(ws)

    def op(self, eng, fn, inc=None, waits=()):
        ws = self._filter(eng, waits)
        amt = 16 if (inc is not None and inc.dma) else 1
        if inc is not None:
            inc.n += amt
        semh = inc.h if inc is not None else None

        def thunk(e, fn=fn, ws=ws, semh=semh, amt=amt):
            for (h, v) in ws:
                e.wait_ge(h, v)
            ins = fn(e)
            if semh is not None:
                ins.then_inc(semh, amt)
        self.prog[eng].append(thunk)
        return inc.n if inc is not None else None

    def wait(self, eng, waits):
        ws = self._filter(eng, waits)
        if ws:
            def thunk(e, ws=ws):
                for (h, v) in ws:
                    e.wait_ge(h, v)
            self.prog[eng].append(thunk)


def build_nc(debug=False):
    nc = bass.Bass("TRN2", target_bir_lowering=False)

    dt = {}

    def din(name, shape, dtype=F32):
        dt[name] = nc.dram_tensor(name, list(shape), dtype, kind="ExternalInput").ap()

    din("x_kv", (UC, 128, D))
    din("x_own", (NTC, 128, D))
    din("valid12", (128, UC))
    din("cosk", (128, UC * 256))
    din("sink", (128, UC * 256))
    din("cosq", (128, NTC * 256))
    din("sinq", (128, NTC * 256))
    din("cmask", (128, 4 * 512))
    din("cmask16", (128, 4 * 512), BF16)
    din("idf", (128, 128))
    din("idb", (128, 128), BF16)
    din("ones_col_b", (128, 1), BF16)
    din("ones_row_f", (1, 128))
    din("gate_pack", (128, IC * 9))
    din("wq_t", (4, 128, IC * 512))
    din("wk_t", (1, 128, IC * 512))
    din("wv_t", (1, 128, IC * 512))
    din("wo_t", (4, 128, IC * 512))
    din("wg_t", (E * 8, 128, 2048))
    din("wu_t", (E * 8, 128, 2048))
    din("wd_t", (E * 4, 128, 4096))
    din("wsg_t", (32, 128, 2048))
    din("wsu_t", (32, 128, 2048))
    din("wsd_t", (16, 128, 4096))

    out_d = nc.dram_tensor("out", [NTC, 128, D], F32, kind="ExternalOutput").ap()
    ln2a_d = nc.dram_tensor("ln2a_scr", [NTC, 128, D], BF16, kind="Internal").ap()

    dbg = {}
    if debug:
        for name, shape, dty in [
            ("dbg_q16", (128, H * TOK), BF16),
            ("dbg_k16", (128, KVH * UT), BF16),
            ("dbg_v16", (128, UC * 512), BF16),
            ("dbg_y16", (128, H * TOK), BF16),
            ("dbg_ln2a", (NTC, 128, D), BF16),
            ("dbg_attn", (NTC, 128, D), BF16),
            ("dbg_comb", (128, NTC * 8), F32),
            ("dbg_sg", (128, NTC), F32),
            ("dbg_x3fm", (128, IC * TOK), BF16),
            ("dbg_he", (128, 8 * TOK), BF16),
            ("dbg_acc", (NTC, 128, D), BF16),
        ]:
            dbg[name] = nc.dram_tensor(name, list(shape), dty, kind="ExternalOutput").ap()

    w = Walk()
    es = contextlib.ExitStack()
    with es:
        ALLSEMS = []

        def mksem(name, dma):
            s = Sem(es.enter_context(nc.semaphore(name)), dma)
            ALLSEMS.append(s)
            return s

        qX = mksem("qX", True)
        qT = mksem("qT", True)
        tT = mksem("tT", False)
        tV = mksem("tV", False)
        tS = mksem("tS", False)
        tG = mksem("tG", False)

        def barrier():
            vals = [(s, s.n) for s in ALLSEMS]
            for eng in Walk.ENGINES:
                w.wait(eng, vals)

        _nuid = [0]

        def mksb(stack):
            def f(name, shape, dtype=F32):
                _nuid[0] += 1
                return stack.enter_context(
                    nc.sbuf_tensor(f"{name}_{_nuid[0]}", list(shape), dtype))
            return f

        sb = mksb(es)

        pA = es.enter_context(nc.psum_tensor("pA", [128, 512], F32))
        pB = es.enter_context(nc.psum_tensor("pB", [128, 512], F32))
        pC = es.enter_context(nc.psum_tensor("pC", [128, 512], F32))
        pD = es.enter_context(nc.psum_tensor("pD", [128, 512], F32))
        pS = es.enter_context(nc.psum_tensor("pS", [128, 512], F32))
        pR = es.enter_context(nc.psum_tensor("pR", [128, 512], F32))

        valid_sb = sb("valid_sb", (128, UC))

        idb_sb = sb("idb_sb", (128, 128), BF16)
        onesc_sb = sb("onesc_sb", (128, 1), BF16)
        onesr_sb = sb("onesr_sb", (1, 128))
        eps_ln = sb("eps_ln", (128, 1))
        eps_q = sb("eps_q", (128, 1))
        eps_k = sb("eps_k", (128, 1))
        stats2 = [sb("stats", (128, 24)) for _ in range(2)]
        mv2 = [sb("mv", (128, 2)) for _ in range(2)]
        rstd2 = [sb("rstd", (128, 4)) for _ in range(2)]
        ssq42 = [sb("ssq4", (128, 4)) for _ in range(2)]

        for name, ap_sb in [("valid12", valid_sb),
                            ("idb", idb_sb),
                            ("ones_col_b", onesc_sb), ("ones_row_f", onesr_sb)]:
            w.op("gpsimd", lambda e, o=ap_sb[:], i=dt[name]: e.dma_start(out=o, in_=i),
                 inc=qT)
        tbl = (qT, qT.n)
        w.op("vector", lambda e, o=eps_ln[:]: e.memset(o, 1e-5))
        w.op("vector", lambda e, o=eps_q[:]: e.memset(o, 1.28e-4))
        w.op("vector", lambda e, o=eps_k[:]: e.memset(o, 1e-6), inc=tV)
        eps_tk = (tV, tV.n)

        idb_ap = idb_sb[:]

        class Stream:

            _sid = [0]

            def __init__(self, salloc, name, fd, qd, cast_eng="gpsimd", nbuf=2):
                self.fd, self.qd, self.nq = fd, qd, fd // qd
                self.nbuf = nbuf
                self.stg = [salloc(f"{name}_s{i}", (128, qd)) for i in range(2)]
                self.w16 = [salloc(f"{name}_w{i}", (128, fd), BF16)
                            for i in range(nbuf)]
                Stream._sid[0] += 1
                self.qsem = [mksem(f"q{name}{Stream._sid[0]}_{i}", True)
                             for i in range(2)]
                self.stg_free = [NONE_T, NONE_T]
                self.w16_free = [NONE_T] * nbuf
                self.cnt = 0
                self.eng = cast_eng
                self.sem = {"vector": tV, "scalar": tS, "gpsimd": tG}[cast_eng]
                self.pending = []

            def load(self, dram_ap):
                bi = self.cnt % self.nbuf
                self.cnt += 1
                cv = 0
                for q in range(self.nq):
                    si = (self.cnt * self.nq + q) % 2
                    src = dram_ap[:, q * self.qd:(q + 1) * self.qd]
                    v = w.op("sync",
                             lambda e, o=self.stg[si][:], i=src: e.dma_start(out=o, in_=i),
                             inc=self.qsem[si], waits=[self.stg_free[si]])
                    o_ap = self.w16[bi][:, q * self.qd:(q + 1) * self.qd]
                    s_ap = self.stg[si][:]
                    if self.eng == "vector":
                        fn = lambda e, o=o_ap, s=s_ap: e.tensor_copy(o, s)
                    elif self.eng == "gpsimd":
                        fn = lambda e, o=o_ap, s=s_ap: e.tensor_copy(o, s)
                    else:
                        fn = lambda e, o=o_ap, s=s_ap: e.copy(o, s)
                    cv = w.op(self.eng, fn, inc=self.sem,
                              waits=[(self.qsem[si], v), self.w16_free[bi]])
                    self.stg_free[si] = (self.sem, cv)
                return bi, (self.sem, cv)

            def prefetch(self, dram_ap):
                self.pending.append(self.load(dram_ap))

            def take(self):
                return self.pending.pop(0)

            def used(self, bi, ticket):
                self.w16_free[bi] = ticket

        ln_setbusy = [NONE_T, NONE_T]

        def emit_ln(src, dst, src_ready, dst_free, eps_tile, ls=0):
            stats, mv, rstd = stats2[ls], mv2[ls], rstd2[ls]
            for s4 in range(4):
                w.op("vector",
                     lambda e, o=stats[:, s4 * 6:(s4 + 1) * 6],
                     i=src[:, s4 * 512:(s4 + 1) * 512]: e.bn_stats(o, i),
                     inc=(tV if s4 == 3 else None),
                     waits=list(src_ready) + ([ln_setbusy[ls]] if s4 == 0 else []))
            w.op("vector", lambda e, o=mv[:], i=stats[:]: e.bn_aggr(o, i),
                 inc=tV, waits=[(tV, tV.n), (tS, tS.n)])
            va = tV.n
            w.op("scalar",
                 lambda e, o=rstd[:, 0:1], i=mv[:, 1:2], b=eps_tile[:]:
                 e.activation(o, i, AF.Ln, bias=b, scale=1.0),
                 inc=tS, waits=[(tV, va), eps_tk])
            w.op("scalar",
                 lambda e, o=rstd[:, 0:1]: e.activation(o, o, AF.Exp, scale=-0.5),
                 inc=tS, waits=[(tS, tS.n)])
            vs = tS.n
            cv = w.op("vector",
                      lambda e, o=dst[:], i=src[:], m=mv[:, 0:1], r=rstd[:, 0:1]:
                      e.tensor_scalar(o, i, m, r, ALU.subtract, ALU.mult),
                      inc=tV, waits=[(tS, vs)] + list(dst_free))
            ln_setbusy[ls] = (tV, cv)
            return cv

        def emit_normrope(psum, qsc, qsq, qrot, cos_ap, sin_ap, eps_tile, scale,
                          mm_ticket, rot_cons, ls=0):
            rstd, ssq4 = rstd2[ls], ssq42[ls]
            cv = w.op("vector", lambda e, o=qsc[:], i=psum[:]: e.tensor_copy(o, i),
                      inc=tV, waits=[mm_ticket])
            psum_free = (tV, cv)
            w.op("vector", lambda e, o=qsq[:], i=qsc[:]: e.tensor_tensor(o, i, i, ALU.mult),
                 inc=tV, waits=[(tV, cv)])
            w.op("vector",
                 lambda e, o=ssq4[:], i=qsq[:].rearrange("p (h d) -> p h d", h=4):
                 e.tensor_reduce(o, i, axis=AX.X, op=ALU.add),
                 inc=tV, waits=[(tV, tV.n), (tS, tS.n)])
            va = tV.n
            w.op("scalar",
                 lambda e, o=rstd[:], i=ssq4[:], b=eps_tile[:], s=scale:
                 e.activation(o, i, AF.Ln, bias=b, scale=s),
                 inc=tS, waits=[(tV, va), eps_tk])
            w.op("scalar",
                 lambda e, o=rstd[:]: e.activation(o, o, AF.Exp, scale=-0.5),
                 inc=tS, waits=[(tS, tS.n)])
            vs = tS.n
            for hh in range(4):
                w.op("vector",
                     lambda e, o=qsc[:, hh * 128:(hh + 1) * 128], r=rstd[:, hh:hh + 1]:
                     e.tensor_scalar(o, o, r, None, ALU.mult),
                     inc=(tV if hh == 3 else None), waits=[(tS, vs)])
            sv = tV.n
            q3 = qsc[:].rearrange("p (h d) -> p h d", h=4)
            r3 = qrot[:].rearrange("p (h d) -> p h d", h=4)
            s3 = qsq[:].rearrange("p (h d) -> p h d", h=4)
            c3 = cos_ap.rearrange("p (h d) -> p h d", h=4)
            n3 = sin_ap.rearrange("p (h d) -> p h d", h=4)
            w.op("vector",
                 lambda e, o=r3[:, :, 0:64], a=q3[:, :, 0:64], b=c3:
                 e.tensor_tensor(o, a, b, ALU.mult),
                 waits=[(tV, sv), tbl] + list(rot_cons))
            w.op("vector",
                 lambda e, o=s3[:, :, 0:64], a=q3[:, :, 64:128], b=n3:
                 e.tensor_tensor(o, a, b, ALU.mult), inc=tV)
            w.op("vector",
                 lambda e, o=r3[:, :, 0:64], b=s3[:, :, 0:64]:
                 e.tensor_tensor(o, o, b, ALU.subtract),
                 inc=tV, waits=[(tV, tV.n)])
            w.op("vector",
                 lambda e, o=r3[:, :, 64:128], a=q3[:, :, 0:64], b=n3:
                 e.tensor_tensor(o, a, b, ALU.mult))
            w.op("vector",
                 lambda e, o=s3[:, :, 64:128], a=q3[:, :, 64:128], b=c3:
                 e.tensor_tensor(o, a, b, ALU.mult), inc=tV)
            w.op("vector",
                 lambda e, o=r3[:, :, 64:128], b=s3[:, :, 64:128]:
                 e.tensor_tensor(o, o, b, ALU.add),
                 inc=tV, waits=[(tV, tV.n)])
            return psum_free, tV.n

        ln2_stores = 0
        p1 = contextlib.ExitStack()
        with p1:
            sb1 = mksb(p1)
            q16 = sb1("q16", (128, H * TOK), BF16)
            k16 = sb1("k16", (128, KVH * UT), BF16)
            v16 = sb1("v16", (128, UC * 512), BF16)
            pab = contextlib.ExitStack()
            with pab:
                sb2 = mksb(pab)
                cosk_sl = sb2("cosk_sl", (128, UC * 256), BF16)
                sink_sl = sb2("sink_sl", (128, UC * 256), BF16)
                cosq_sb = sb2("cosq_sb", (128, NTC * 256), BF16)
                sinq_sb = sb2("sinq_sb", (128, NTC * 256), BF16)
                qTa = mksem("qTa", True)
                with nc.sbuf_tensor("ropestg", [128, UC * 256], F32) as ropestg:
                    stg_ap = ropestg[:]
                    prev = NONE_T
                    for nm_, dst_ in (("cosk", cosk_sl), ("sink", sink_sl),
                                      ("cosq", cosq_sb), ("sinq", sinq_sb)):
                        fdim = dst_[:].tensor.shape[1]
                        v_ = w.op("gpsimd",
                                  lambda e, o=ropestg[0:128, 0:fdim], i=dt[nm_]:
                                  e.dma_start(out=o, in_=i),
                                  inc=qTa, waits=[prev])
                        cv_ = w.op("vector",
                                   lambda e, o=dst_[:], i=ropestg[0:128, 0:fdim]:
                                   e.tensor_copy(o, i),
                                   inc=tV, waits=[(qTa, v_)])
                        prev = (tV, cv_)
                cs_ld = prev
                w.wait("gpsimd", [prev])
                xin = [sb2(f"xin{i}", (128, D)) for i in range(2)]
                xin_free = [NONE_T] * 2
                qxin = [mksem("qxin0", True), mksem("qxin1", True)]
                x1c = [sb2(f"x1c{i}", (128, D), BF16) for i in range(2)]
                x1c_free = [NONE_T] * 2
                x1fm = sb2("x1fm", (128, IC * UT), BF16)
                qscb = [sb2("qsc", (128, 512), BF16) for _ in range(2)]
                qsqb = [sb2("qsq", (128, 512), BF16) for _ in range(2)]
                qrotb = [sb2("qrot", (128, 512), BF16) for _ in range(2)]
                wstr = Stream(sb2, "wA", 4096, 1024, cast_eng="gpsimd", nbuf=2)
                pTb = [pab.enter_context(nc.psum_tensor("pT_0", [128, 1024], BF16)),
                       pab.enter_context(nc.psum_tensor("pT_1", [128, 1024], BF16))]

                x1_r = x1fm[:].rearrange("p (ic t) -> p ic t", ic=IC)
                pT_r = [pTb[0][:].rearrange("p (ii t) -> p ii t", ii=8),
                        pTb[1][:].rearrange("p (ii t) -> p ii t", ii=8)]
                tr_free = [NONE_T, NONE_T]
                nr_state = [dict(rot_cons=NONE_T) for _ in range(2)]
                bank_free2 = [NONE_T, NONE_T]
                unit_ctr = [0]
                x1_ck = [NONE_T] * UC
                tr_pending = []

                def flush_tr(keep=1):
                    while len(tr_pending) > keep:
                        tr_pending.pop(0)()

                def qkv_unit(bank, mmv, cos_ap, sin_ap, eps_tile, scale,
                             dst_sl, bi_bank):
                    u = unit_ctr[0] % 2
                    unit_ctr[0] += 1
                    st = nr_state[u]
                    pf, nv = emit_normrope(
                        bank, qscb[u], qsqb[u], qrotb[u], cos_ap, sin_ap,
                        eps_tile, scale, mmv, [st["rot_cons"]], ls=u)
                    bank_free2[bi_bank] = pf

                    def part2(u=u, nv=nv, dst_sl=dst_sl, st=st):
                        for hh in range(4):
                            w.op("tensor",
                                 lambda e, o=pTb[u][:, hh * 128:(hh + 1) * 128],
                                 i=qrotb[u][:, hh * 128:(hh + 1) * 128]:
                                 e.transpose(o, i, idb_ap),
                                 inc=(tT if hh == 3 else None),
                                 waits=[(tV, nv), tr_free[u]])
                        tt = (tT, tT.n)
                        st["rot_cons"] = tt
                        cv = w.op("vector",
                                  lambda e, o=dst_sl, i=pT_r[u][:, 0:4, :]:
                                  e.tensor_copy(o, i),
                                  inc=tV, waits=[tt])
                        tr_free[u] = (tV, cv)
                    tr_pending.append(part2)

                def emit_ln1_chunk(tc):
                    bi = tc % 2
                    v = w.op("gpsimd",
                             lambda e, o=xin[bi][:], i=dt["x_kv"][tc]:
                             e.dma_start(out=o, in_=i),
                             inc=qxin[bi], waits=[xin_free[bi]])
                    cv = emit_ln(xin[bi], x1c[bi], [(qxin[bi], v)],
                                 [x1c_free[bi]], eps_ln, ls=bi)
                    xin_free[bi] = (tV, cv)
                    for g in range(2):
                        for ii in range(8):
                            ic = g * 8 + ii
                            w.op("tensor",
                                 lambda e, o=pTb[g][:, ii * 128:(ii + 1) * 128],
                                 i=x1c[bi][:, ic * 128:(ic + 1) * 128]:
                                 e.transpose(o, i, idb_ap),
                                 inc=(tT if ii == 7 else None),
                                 waits=[(tV, cv), tr_free[g], tbl])
                        tt = tT.n
                        cpv = w.op("vector",
                                   lambda e,
                                   o=x1_r[:, g * 8:(g + 1) * 8, tc * 128:tc * 128 + 128],
                                   i=pT_r[g]:
                                   e.tensor_copy(o, i),
                                   inc=tV, waits=[(tT, tt)])
                        tr_free[g] = (tV, cpv)
                    x1c_free[bi] = tr_free[1]
                    x1_ck[tc] = tr_free[1]

                def slab_ap(mat, oc, hs):
                    return dt[mat + "_t"][oc][:, hs * 4096:(hs + 1) * 4096]

                def emit_qkv_tiles(mat, oc, wA, wB, rdyA, rdyB, tiles):
                    last_mm = 0
                    for tc in tiles:
                        bank = [pA, pB][tc % 2]
                        if mat == "wq":
                            loff = QOFF + tc * 128
                            xr = [x1_ck[7 + tc], x1_ck[8 + tc if 8 + tc < UC else UC - 1]]
                        else:
                            loff = tc * 128
                            xr = [x1_ck[tc]]
                        for ic in range(IC):
                            ws16 = wA if ic < 8 else wB
                            w.op("tensor",
                                 lambda e, o=bank[:],
                                 l=x1fm[:, ic * UT + loff:ic * UT + loff + 128],
                                 r=ws16[:, (ic % 8) * 512:(ic % 8 + 1) * 512],
                                 s=(ic == 0), p=(ic == IC - 1):
                                 e.matmul(o, l, r, start=s, stop=p),
                                 inc=(tT if ic == IC - 1 else None),
                                 waits=xr + [rdyA, rdyB, bank_free2[tc % 2]])
                        mmv = (tT, tT.n)
                        last_mm = tT.n
                        if mat == "wv":
                            cv = w.op("vector",
                                      lambda e,
                                      o=v16[:, tc * 512:(tc + 1) * 512],
                                      i=bank[:]: e.tensor_copy(o, i),
                                      inc=tV, waits=[mmv])
                            bank_free2[tc % 2] = (tV, cv)
                        elif mat == "wk":
                            w.wait("vector", [cs_ld])
                            k_r = k16[:].rearrange("p (h t) -> p h t", h=KVH)
                            qkv_unit(bank, mmv,
                                     cosk_sl[:, tc * 256:(tc + 1) * 256],
                                     sink_sl[:, tc * 256:(tc + 1) * 256],
                                     eps_k, 1.0 / 128.0,
                                     k_r[:, :, tc * 128:tc * 128 + 128],
                                     tc % 2)
                        else:
                            q_r = q16[:].rearrange("p (h t) -> p h t", h=H)
                            qkv_unit(bank, mmv,
                                     cosq_sb[:, tc * 256:(tc + 1) * 256],
                                     sinq_sb[:, tc * 256:(tc + 1) * 256],
                                     eps_q, 1.0,
                                     q_r[:, oc * 4:(oc + 1) * 4, tc * 128:tc * 128 + 128],
                                     tc % 2)
                        flush_tr(keep=1)
                    return last_mm

                wstr.prefetch(slab_ap("wk", 0, 0))
                wstr.prefetch(slab_ap("wk", 0, 1))
                emit_ln1_chunk(0)
                biA, rdyA = wstr.take()
                biB, rdyB = wstr.take()
                wkA, wkB = wstr.w16[biA], wstr.w16[biB]
                lm = 0
                for tc in range(1, UC):
                    emit_ln1_chunk(tc)
                    lm = emit_qkv_tiles("wk", 0, wkA, wkB, rdyA, rdyB, [tc - 1])
                lm = emit_qkv_tiles("wk", 0, wkA, wkB, rdyA, rdyB, [UC - 1])
                wstr.used(biA, (tT, lm))
                wstr.used(biB, (tT, lm))
                rest = [("wv", 0)] + [("wq", oc) for oc in range(4)]
                wstr.prefetch(slab_ap("wv", 0, 0))
                wstr.prefetch(slab_ap("wv", 0, 1))
                for si_, (mat, oc) in enumerate(rest):
                    biA, rdyA = wstr.take()
                    biB, rdyB = wstr.take()
                    wA, wB = wstr.w16[biA], wstr.w16[biB]
                    tiles = range(8) if mat == "wq" else range(UC)
                    lm = emit_qkv_tiles(mat, oc, wA, wB, rdyA, rdyB, tiles)
                    wstr.used(biA, (tT, lm))
                    wstr.used(biB, (tT, lm))
                    if si_ + 1 < len(rest):
                        m2_, o2_ = rest[si_ + 1]
                        wstr.prefetch(slab_ap(m2_, o2_, 0))
                        wstr.prefetch(slab_ap(m2_, o2_, 1))
                flush_tr(keep=0)
            qkv_ready = (tV, tV.n)
            barrier()

            if debug:
                for nm, src_ in [("dbg_q16", q16), ("dbg_k16", k16), ("dbg_v16", v16)]:
                    w.op("gpsimd", lambda e, o=dbg[nm], i=src_[:]:
                         e.dma_start(out=o, in_=i), inc=qX, waits=[qkv_ready])

            pcd = contextlib.ExitStack()
            with pcd:
                sb3 = mksb(pcd)
                exp16 = [sb3(f"exp16_{i}", (128, 512), BF16) for i in range(6)]
                rcp = [sb3(f"rcp{i}", (1, 512)) for i in range(2)]
                bcf = [sb3(f"bcf{i}", (128, 512)) for i in range(2)]
                y16 = sb3("y16", (128, H * TOK), BF16)
                attn16 = [sb3(f"attn16_{i}", (128, D), BF16) for i in range(8)]
                ln2c = [sb3(f"ln2c{i}", (128, D), BF16) for i in range(2)]
                ln2c_free = [NONE_T, NONE_T]
                qL = [mksem("qL0", True), mksem("qL1", True)]
                wstr2 = Stream(sb3, "wB", 4096, 2048, cast_eng="vector", nbuf=3)
                pSu = [pcd.enter_context(nc.psum_tensor(f"pSu{i}", [128, 512], F32))
                       for i in range(2)]
                cmask16_sb = sb3("cmask16_sb", (128, 4 * 512), BF16)
                qTb = mksem("qTb", True)
                w.op("gpsimd", lambda e, o=cmask16_sb[:], i=dt["cmask16"]:
                     e.dma_start(out=o, in_=i), inc=qTb)
                cm_ld = (qTb, qTb.n)
                w.wait("tensor", [cm_ld])

                scb = [[pA, pB], [pC, pD]]
                avb = [pS, pR]
                av_free = [NONE_T, NONE_T]
                sc_free = [[NONE_T, NONE_T], [NONE_T, NONE_T]]
                exp_free = [NONE_T] * 6
                sum_free = [NONE_T, NONE_T]
                pending = [None, None]

                def emit_tail(u):
                    h, j, rcpv = pending[u]
                    pending[u] = None
                    sbank = scb[u][1]
                    w.op("tensor",
                         lambda e, o=sbank[:], l=onesr_sb[:], r=rcp[u][:]:
                         e.matmul(o, l, r, start=True, stop=True),
                         inc=tT, waits=[rcpv, sc_free[u][1]])
                    bct = (tT, tT.n)
                    cv2 = w.op("vector",
                               lambda e, o=bcf[u][:], i=sbank[:]:
                               e.tensor_copy(o, i),
                               inc=tV, waits=[bct])
                    sc_free[u][1] = (tV, cv2)
                    yv = w.op("vector",
                              lambda e, o=y16[:, h * TOK + j * CH:h * TOK + (j + 1) * CH],
                              b2=bcf[u][:]:
                              e.tensor_tensor(o, o, b2, ALU.mult),
                              inc=tV, waits=[(tV, cv2)])
                    return yv

                for j in range(NSLOT):
                    for hp in range(H // 2):
                        exvq = {}
                        for kc in range(WC + 1):
                            if kc < WC:
                                uck = 4 * j + kc
                                for u in range(2):
                                    h = 2 * hp + u
                                    kvh = h // 4
                                    ei = u * 3 + kc % 3
                                    bank = scb[u][kc % 2]
                                    w.op("tensor",
                                         lambda e, o=bank[:],
                                         l=k16[:, kvh * UT + uck * 128:kvh * UT + uck * 128 + 128],
                                         r=q16[:, h * TOK + j * CH:h * TOK + (j + 1) * CH],
                                         p=(kc < 8):
                                         e.matmul(o, l, r, start=True, stop=p),
                                         inc=(tT if kc < 8 else None),
                                         waits=[qkv_ready, sc_free[u][kc % 2], tbl])
                                    if kc >= 8:
                                        w.op("tensor",
                                             lambda e, o=bank[:], l=idb_ap,
                                             r=cmask16_sb[:, (kc - 8) * 512:(kc - 7) * 512]:
                                             e.matmul(o, l, r, start=False, stop=True),
                                             inc=tT)
                                    mmv = (tT, tT.n)
                                    ev = w.op("scalar",
                                              lambda e, o=exp16[ei][:], i=bank[:],
                                              b=valid_sb[:, uck:uck + 1]:
                                              e.activation(o, i, AF.Exp, bias=b, scale=1.0),
                                              inc=tS, waits=[mmv, exp_free[ei], tbl])
                                    exvq[(u, kc)] = (tS, ev)
                                    sc_free[u][kc % 2] = (tS, ev)
                            if kc == 2 and pending[0] is not None:
                                emit_tail(0)
                                emit_tail(1)
                            if kc >= 1:
                                kd = kc - 1
                                uckd = 4 * j + kd
                                for u in range(2):
                                    h = 2 * hp + u
                                    kvh = h // 4
                                    ei = u * 3 + kd % 3
                                    w.op("tensor",
                                         lambda e, o=pSu[u][0:1, :], l=onesc_sb[:],
                                         r=exp16[ei][:],
                                         s=(kd == 0), p=(kd == WC - 1):
                                         e.matmul(o, l, r, start=s, stop=p),
                                         inc=tT,
                                         waits=[exvq.pop((u, kd))] + ([sum_free[u]] if kd == 0 else []))
                                    w.op("tensor",
                                         lambda e, o=avb[u][:],
                                         l=v16[:, uckd * 512 + kvh * 128:uckd * 512 + kvh * 128 + 128],
                                         r=exp16[ei][:],
                                         s=(kd == 0), p=(kd == WC - 1):
                                         e.matmul(o, l, r, start=s, stop=p),
                                         inc=tT,
                                         waits=([av_free[u]] if kd == 0 else []))
                                    exp_free[ei] = (tT, tT.n)
                        av_done = (tT, tT.n)
                        for u in range(2):
                            h = 2 * hp + u
                            cv = w.op("vector",
                                      lambda e, o=y16[:, h * TOK + j * CH:h * TOK + (j + 1) * CH],
                                      i=avb[u][:]: e.tensor_copy(o, i),
                                      inc=tV, waits=[av_done])
                            av_free[u] = (tV, cv)
                            lv = w.op("scalar",
                                      lambda e, o=rcp[u][:], i=pSu[u][0:1, :]:
                                      e.activation(o, i, AF.Ln, scale=1.0),
                                      inc=tS, waits=[av_done])
                            sum_free[u] = (tS, lv)
                            rv = w.op("scalar",
                                      lambda e, o=rcp[u][:]:
                                      e.activation(o, o, AF.Exp, scale=-1.0),
                                      inc=tS, waits=[(tS, lv)])
                            pending[u] = (h, j, (tS, rv))
                emit_tail(0)
                emit_tail(1)
                y_ready = (tV, tV.n)

                if debug:
                    w.op("gpsimd", lambda e, o=dbg["dbg_y16"], i=y16[:]:
                         e.dma_start(out=o, in_=i), inc=qX, waits=[y_ready])

                attn_parts = [0] * 8
                ob_free = [NONE_T, NONE_T]
                wstr2.prefetch(dt["wo_t"][0][:, 0:4096])
                wstr2.prefetch(dt["wo_t"][0][:, 4096:8192])
                for oc in range(4):
                    biA2, rdyA2 = wstr2.take()
                    biB2, rdyB2 = wstr2.take()
                    woA, woB = wstr2.w16[biA2], wstr2.w16[biB2]
                    last_mm = 0
                    for tq in range(8):
                        bank = [pA, pB][tq % 2]
                        for ic in range(IC):
                            ws16 = woA if ic < 8 else woB
                            w.op("tensor",
                                 lambda e, o=bank[:],
                                 l=y16[:, ic * TOK + tq * 128:ic * TOK + tq * 128 + 128],
                                 r=ws16[:, (ic % 8) * 512:(ic % 8 + 1) * 512],
                                 s=(ic == 0), p=(ic == IC - 1):
                                 e.matmul(o, l, r, start=s, stop=p),
                                 inc=(tT if ic == IC - 1 else None),
                                 waits=[y_ready, rdyA2, rdyB2, ob_free[tq % 2]])
                        mmv = (tT, tT.n)
                        last_mm = tT.n
                        cv = w.op("vector",
                                  lambda e, o=attn16[tq][:, oc * 512:(oc + 1) * 512],
                                  i=bank[:]: e.tensor_copy(o, i),
                                  inc=tV, waits=[mmv])
                        ob_free[tq % 2] = (tV, cv)
                        attn_parts[tq] = cv
                    wstr2.used(biA2, (tT, last_mm))
                    wstr2.used(biB2, (tT, last_mm))
                    if oc + 1 < 4:
                        wstr2.prefetch(dt["wo_t"][oc + 1][:, 0:4096])
                        wstr2.prefetch(dt["wo_t"][oc + 1][:, 4096:8192])
                for tq in range(8):
                    bi = tq % 2
                    if debug:
                        w.op("gpsimd",
                             lambda e, o=dbg["dbg_attn"][tq], i=attn16[tq][:]:
                             e.dma_start(out=o, in_=i),
                             inc=qX, waits=[(tV, attn_parts[tq])])
                    cv = emit_ln(attn16[tq], ln2c[bi], [(tV, attn_parts[tq])],
                                 [ln2c_free[bi]], eps_ln, ls=bi)
                    sv = w.op("gpsimd",
                              lambda e, o=ln2a_d[tq], i=ln2c[bi][:]:
                              e.dma_start(out=o, in_=i),
                              inc=qL[bi], waits=[(tV, cv)])
                    ln2c_free[bi] = (qL[bi], sv)
                    ln2_stores += 1
                    if debug:
                        w.op("gpsimd",
                             lambda e, o=dbg["dbg_ln2a"][tq], i=ln2c[bi][:]:
                             e.dma_start(out=o, in_=i), inc=qX)
        ln2_done2 = [(qL[0], qL[0].n), (qL[1], qL[1].n)]

        p2 = contextlib.ExitStack()
        with p2:
            sbf = mksb(p2)
            idf_sb = sbf("idf_sb", (128, 128))
            gate_sb = sbf("gate_sb", (128, IC * 9))
            qTc = mksem("qTc", True)
            w.op("gpsimd", lambda e, o=idf_sb[:], i=dt["idf"]:
                 e.dma_start(out=o, in_=i), inc=qTc)
            w.op("gpsimd", lambda e, o=gate_sb[:], i=dt["gate_pack"]:
                 e.dma_start(out=o, in_=i), inc=qTc)
            p2tbl = (qTc, qTc.n)
            w.wait("tensor", [p2tbl])
            w.wait("vector", [p2tbl])
            idf_ap = idf_sb[:]
            x3fm = sbf("x3fm", (128, IC * TOK), BF16)
            moe = sbf("moe", (128, NTC * D), BF16)
            comb = sbf("comb", (128, NTC * 8))
            sgv = sbf("sgv", (128, NTC))
            x3_r = x3fm[:].rearrange("p (ic t) -> p ic t", ic=IC)

            barrier()
            p2a = contextlib.ExitStack()
            with p2a:
                sba = mksb(p2a)
                xin2 = [sba(f"xin2_{i}", (128, D)) for i in range(2)]
                xin2_free = [NONE_T, NONE_T]
                l2c = [sba(f"l2c{i}", (128, D), BF16) for i in range(2)]
                l2c_free = [NONE_T, NONE_T]
                qxin2 = [mksem("qxin2_0", True), mksem("qxin2_1", True)]
                ql2c = [mksem("ql2c_0", True), mksem("ql2c_1", True)]
                x2c2 = [sba("x2c", (128, D)) for _ in range(2)]
                x3c2 = [sba("x3c", (128, D)) for _ in range(2)]
                xf322 = [sba("xf32", (128, D)) for _ in range(2)]
                r9 = sba("r9", (9, 128))
                pr9 = sba("pr9", (128, 9))
                prb = sba("prb", (128, 8))
                m1 = sba("m1", (128, 1))
                m2 = sba("m2", (128, 1))
                sm = sba("sm", (128, 1))
                eqb = sba("eqb", (128, 8))

                x2c_free = [NONE_T, NONE_T]
                x3c_free = [NONE_T, NONE_T]
                xf32_free = [NONE_T, NONE_T]
                x3_ck = [NONE_T] * NTC
                comb_ck = [NONE_T] * NTC
                pR_free = NONE_T
                pDt_free = NONE_T
                r9_cons = NONE_T
                pr9_cons = NONE_T
                trb_free = [NONE_T, NONE_T]
                for tc in range(NTC):
                    bi = tc % 2
                    x2c, x3c, xf32 = x2c2[bi], x3c2[bi], xf322[bi]
                    v1 = w.op("gpsimd",
                              lambda e, o=xin2[bi][:], i=dt["x_own"][tc]:
                              e.dma_start(out=o, in_=i),
                              inc=qxin2[bi], waits=[xin2_free[bi]] + ln2_done2)
                    v2 = w.op("gpsimd",
                              lambda e, o=l2c[bi][:], i=ln2a_d[tc]:
                              e.dma_start(out=o, in_=i),
                              inc=ql2c[bi], waits=[l2c_free[bi]])
                    cv = w.op("vector",
                              lambda e, o=x2c[:], a=xin2[bi][:], b=l2c[bi][:]:
                              e.tensor_tensor(o, a, b, ALU.add),
                              inc=tV, waits=[(qxin2[bi], v1), (ql2c[bi], v2), x2c_free[bi]])
                    xin2_free[bi] = (tV, cv)
                    l2c_free[bi] = (tV, cv)
                    lv = emit_ln(x2c, x3c, [(tV, cv)], [x3c_free[bi]], eps_ln, ls=bi)
                    x2c_free[bi] = (tV, lv)
                    grp_cp = 0
                    for g in range(4):
                        bank = [pA, pB][g % 2]
                        for ii in range(4):
                            ic = g * 4 + ii
                            w.op("tensor",
                                 lambda e, o=bank[:, ii * 128:(ii + 1) * 128],
                                 i=x3c[:, ic * 128:(ic + 1) * 128]:
                                 e.transpose(o, i, idf_ap),
                                 inc=(tT if ii == 3 else None),
                                 waits=[(tV, lv), trb_free[g % 2], tbl])
                        tt = (tT, tT.n)
                        w.op("vector",
                             lambda e,
                             o=x3_r[:, g * 4:(g + 1) * 4, tc * 128:tc * 128 + 128],
                             i=bank[:].rearrange("p (ii t) -> p ii t", ii=4):
                             e.tensor_copy(o, i),
                             waits=[tt])
                        cv2 = w.op("vector",
                                   lambda e, o=xf32[:, g * 512:(g + 1) * 512], i=bank[:]:
                                   e.tensor_copy(o, i),
                                   inc=tV, waits=[tt, xf32_free[bi]])
                        trb_free[g % 2] = (tV, cv2)
                        grp_cp = cv2
                    x3c_free[bi] = (tV, grp_cp)
                    x3_ck[tc] = (tV, grp_cp)
                    xf_rdy = (tV, grp_cp)
                    for ic in range(IC):
                        w.op("tensor",
                             lambda e, o=pR[0:9, 0:128],
                             l=gate_sb[:, ic * 9:(ic + 1) * 9],
                             r=xf32[:, ic * 128:(ic + 1) * 128],
                             s=(ic == 0), p=(ic == IC - 1):
                             e.matmul(o, l, r, start=s, stop=p),
                             inc=(tT if ic == IC - 1 else None),
                             waits=[xf_rdy, tbl, pR_free])
                    rmm = (tT, tT.n)
                    xf32_free[bi] = rmm
                    cvr = w.op("vector", lambda e, o=r9[:], i=pR[0:9, 0:128]:
                               e.tensor_copy(o, i),
                               inc=tV, waits=[rmm, r9_cons])
                    pR_free = (tV, cvr)
                    w.op("tensor",
                         lambda e, o=pD[0:128, 0:9], i=r9[:], d9=idf_sb[0:9, 0:9]:
                         e.transpose(o, i, d9),
                         inc=tT, waits=[(tV, cvr), pDt_free])
                    r9_cons = (tT, tT.n)
                    cvp = w.op("vector", lambda e, o=pr9[:], i=pD[0:128, 0:9]:
                               e.tensor_copy(o, i),
                               inc=tV, waits=[r9_cons, pr9_cons])
                    pDt_free = (tV, cvp)
                    w.op("vector", lambda e, o=m1[:], i=pr9[:, 0:8]:
                         e.reduce_max(o, i, axis=AX.X),
                         inc=tV, waits=[(tV, cvp), (tS, tS.n)])
                    w.op("vector", lambda e, o=prb[:], i=pr9[:, 0:8], s=m1[:]:
                         e.tensor_scalar(o, i, s, None, ALU.subtract),
                         inc=tV, waits=[(tV, tV.n)])
                    av_ = tV.n
                    w.op("scalar", lambda e, o=prb[:]:
                         e.activation(o, o, AF.Exp, scale=1.0),
                         inc=tS, waits=[(tV, av_)])
                    ev = (tS, tS.n)
                    w.op("vector", lambda e, o=sm[:], i=prb[:]:
                         e.reduce_sum(o, i, axis=AX.X),
                         inc=tV, waits=[ev])
                    w.op("vector", lambda e, o=sm[:]: e.reciprocal(o, o),
                         inc=tV, waits=[(tV, tV.n)])
                    w.op("vector", lambda e, o=prb[:], s=sm[:]:
                         e.tensor_scalar(o, o, s, None, ALU.mult),
                         inc=tV, waits=[(tV, tV.n)])
                    w.op("vector", lambda e, o=m1[:], i=prb[:]:
                         e.reduce_max(o, i, axis=AX.X),
                         inc=tV, waits=[(tV, tV.n)])
                    w.op("vector", lambda e, o=eqb[:], i=prb[:], s=m1[:]:
                         e.tensor_scalar(o, i, s, None, ALU.is_equal),
                         inc=tV, waits=[(tV, tV.n)])
                    w.op("vector", lambda e, o=eqb[:], i=prb[:]:
                         e.scalar_tensor_tensor(o, o, 1e30, i, ALU.mult, ALU.subtract),
                         inc=tV, waits=[(tV, tV.n)])
                    w.op("vector", lambda e, o=m2[:], i=eqb[:]:
                         e.tensor_reduce(o, i, axis=AX.X, op=ALU.min, negate=True),
                         inc=tV, waits=[(tV, tV.n)])
                    w.op("vector", lambda e, o=eqb[:], i=prb[:], s=m2[:]:
                         e.tensor_scalar(o, i, s, None, ALU.is_ge),
                         inc=tV, waits=[(tV, tV.n)])
                    w.op("vector",
                         lambda e, o=comb[:, tc * 8:(tc + 1) * 8], a=prb[:], b=eqb[:]:
                         e.tensor_tensor(o, a, b, ALU.mult),
                         inc=tV, waits=[(tV, tV.n)])
                    w.op("scalar",
                         lambda e, o=m1[:], i=pr9[:, 8:9]:
                         e.activation(o, i, AF.Tanh, scale=0.5),
                         inc=tS, waits=[(tV, tV.n)])
                    pr9_cons = (tS, tS.n)
                    w.op("vector",
                         lambda e, o=sgv[:, tc:tc + 1], i=m1[:]:
                         e.tensor_scalar(o, i, 0.5, 0.5, ALU.mult, ALU.add),
                         inc=tV, waits=[(tS, tS.n)])
                    comb_ck[tc] = (tV, tV.n)
                x3_ready = (tV, tV.n)
                comb_ready = (tV, tV.n)

            if debug:
                for nm, src in [("dbg_x3fm", x3fm), ("dbg_comb", comb), ("dbg_sg", sgv)]:
                    w.op("gpsimd", lambda e, o=dbg[nm], i=src[:]:
                         e.dma_start(out=o, in_=i), inc=qX, waits=[x3_ready])

            barrier()
            p2b = contextlib.ExitStack()
            with p2b:
                sbb = mksb(p2b)
                he2 = [sbb("he", (128, 8 * TOK), BF16) for _ in range(2)]
                silu_f = [sbb(f"silu{i}", (128, 512)) for i in range(2)]
                silu_free = [NONE_T, NONE_T]
                gS = Stream(sbb, "wg", 2048, 2048, cast_eng="vector")
                uS = Stream(sbb, "wu", 2048, 2048, cast_eng="scalar")
                dS = Stream(sbb, "wd", 4096, 2048, cast_eng="scalar")

                gb = [pA, pB]
                ub = [pC, pD]
                db = [pS, pR]
                gb_free = [NONE_T, NONE_T]
                ub_free = [NONE_T, NONE_T]
                db_free = [NONE_T, NONE_T]
                he_free = [NONE_T, NONE_T]
                moe_parts = NONE_T
                for pf in range(2):
                    gS.prefetch(dt["wg_t"][pf])
                    uS.prefetch(dt["wu_t"][pf])
                dS.prefetch(dt["wd_t"][0])
                dS.prefetch(dt["wd_t"][1])
                gu_next = 2
                d_next = 2
                for ee in range(E):
                    he = he2[ee % 2]
                    he_done = 0
                    for ici in range(8):
                        bg, grdy = gS.take()
                        bu, urdy = uS.take()
                        g16, u16 = gS.w16[bg], uS.w16[bu]
                        lastg = lastu = 0
                        for t in range(2):
                            gi = (ici * 2 + t) % 2
                            for ic in range(IC):
                                w.op("tensor",
                                     lambda e, o=gb[gi][:],
                                     l=g16[:, ic * 128:(ic + 1) * 128],
                                     r=x3fm[:, ic * TOK + t * 512:ic * TOK + (t + 1) * 512],
                                     s=(ic == 0), p=(ic == IC - 1):
                                     e.matmul(o, l, r, start=s, stop=p),
                                     inc=(tT if ic == IC - 1 else None),
                                     waits=[x3_ck[4 * t], x3_ck[4 * t + 1],
                                            x3_ck[4 * t + 2], x3_ck[4 * t + 3],
                                            grdy, gb_free[gi]])
                            gmm = (tT, tT.n)
                            lastg = tT.n
                            for ic in range(IC):
                                w.op("tensor",
                                     lambda e, o=ub[gi][:],
                                     l=u16[:, ic * 128:(ic + 1) * 128],
                                     r=x3fm[:, ic * TOK + t * 512:ic * TOK + (t + 1) * 512],
                                     s=(ic == 0), p=(ic == IC - 1):
                                     e.matmul(o, l, r, start=s, stop=p),
                                     inc=(tT if ic == IC - 1 else None),
                                     waits=[urdy, ub_free[gi]])
                            umm = (tT, tT.n)
                            lastu = tT.n
                            si = gi
                            sv = w.op("scalar",
                                      lambda e, o=silu_f[si][:], i=gb[gi][:]:
                                      e.activation(o, i, AF.Silu, scale=1.0),
                                      inc=tS, waits=[gmm, silu_free[si]])
                            gb_free[gi] = (tS, sv)
                            hv = w.op("vector",
                                      lambda e,
                                      o=he[:, ici * 1024 + t * 512:ici * 1024 + (t + 1) * 512],
                                      a=silu_f[si][:], b=ub[gi][:]:
                                      e.tensor_tensor(o, a, b, ALU.mult),
                                      inc=tV, waits=[(tS, sv), umm, he_free[ee % 2]])
                            silu_free[si] = (tV, hv)
                            ub_free[gi] = (tV, hv)
                            he_done = hv
                        gS.used(bg, (tT, lastg))
                        uS.used(bu, (tT, lastu))
                        if gu_next < E * 8:
                            gS.prefetch(dt["wg_t"][gu_next])
                            uS.prefetch(dt["wu_t"][gu_next])
                            gu_next += 1
                    he_ready = (tV, he_done)
                    if debug and ee == 0:
                        w.op("gpsimd", lambda e, o=dbg["dbg_he"], i=he[:]:
                             e.dma_start(out=o, in_=i), inc=qX, waits=[he_ready])
                    lastd_all = 0
                    for oc in range(4):
                        bd, drdy = dS.take()
                        d16 = dS.w16[bd]
                        lastd = 0
                        for tc in range(NTC):
                            di = tc % 2
                            for ici in range(8):
                                w.op("tensor",
                                     lambda e, o=db[di][:],
                                     l=he[:, ici * 1024 + tc * 128:ici * 1024 + tc * 128 + 128],
                                     r=d16[:, ici * 512:(ici + 1) * 512],
                                     s=(ici == 0), p=(ici == 7):
                                     e.matmul(o, l, r, start=s, stop=p),
                                     inc=(tT if ici == 7 else None),
                                     waits=[he_ready, drdy, db_free[di]])
                            mmv = (tT, tT.n)
                            lastd = tT.n
                            dst = moe[:, tc * D + oc * 512:tc * D + (oc + 1) * 512]
                            if ee == 0:
                                cv = w.op("vector",
                                          lambda e, o=dst, i=db[di][:],
                                          s=comb[:, tc * 8 + ee:tc * 8 + ee + 1]:
                                          e.tensor_scalar(o, i, s, None, ALU.mult),
                                          inc=tV, waits=[mmv, comb_ck[tc]])
                            else:
                                cv = w.op("vector",
                                          lambda e, o=dst, i=db[di][:],
                                          s=comb[:, tc * 8 + ee:tc * 8 + ee + 1]:
                                          e.scalar_tensor_tensor(o, i, s, o,
                                                                 ALU.mult, ALU.add),
                                          inc=tV, waits=[mmv, comb_ready, moe_parts])
                            db_free[di] = (tV, cv)
                        dS.used(bd, (tT, lastd))
                        if d_next < E * 4:
                            dS.prefetch(dt["wd_t"][d_next])
                            d_next += 1
                        lastd_all = lastd
                    moe_parts = (tV, tV.n)
                    he_free[ee % 2] = (tT, lastd_all)

            barrier()
            p2c = contextlib.ExitStack()
            with p2c:
                sbc = mksb(p2c)
                sh2 = [sbc("sh", (128, 32 * 512), BF16)]
                silu_f = [sbc(f"silu2_{i}", (128, 512)) for i in range(2)]
                silu_free = [NONE_T, NONE_T]
                gS = Stream(sbc, "wsg", 2048, 1024, cast_eng="vector")
                uS = Stream(sbc, "wsu", 2048, 1024, cast_eng="scalar")
                dS = Stream(sbc, "wsd", 4096, 2048, cast_eng="scalar")

                gb2 = [pA, pB]
                ub2 = [pC, pD]
                gb2_free = [NONE_T, NONE_T]
                ub2_free = [NONE_T, NONE_T]
                banks4 = [pA, pB, pS, pR]
                psum4_free = [NONE_T] * 4
                moe_tc = [NONE_T] * NTC
                sh_cons = NONE_T
                moe_sh = moe_parts
                gS.prefetch(dt["wsg_t"][0])
                uS.prefetch(dt["wsu_t"][0])
                gS.prefetch(dt["wsg_t"][1])
                uS.prefetch(dt["wsu_t"][1])
                gu_next2 = [2]
                for t in range(2):
                    sh = sh2[0]
                    sh_done = 0
                    for ici in range(32):
                        gi = ici % 2
                        bg, grdy = gS.take()
                        bu, urdy = uS.take()
                        g16, u16 = gS.w16[bg], uS.w16[bu]
                        for ic in range(IC):
                            w.op("tensor",
                                 lambda e, o=gb2[gi][:],
                                 l=g16[:, ic * 128:(ic + 1) * 128],
                                 r=x3fm[:, ic * TOK + t * 512:ic * TOK + (t + 1) * 512],
                                 s=(ic == 0), p=(ic == IC - 1):
                                 e.matmul(o, l, r, start=s, stop=p),
                                 inc=(tT if ic == IC - 1 else None),
                                 waits=[x3_ck[4 * t], x3_ck[4 * t + 1],
                                        x3_ck[4 * t + 2], x3_ck[4 * t + 3],
                                        grdy, gb2_free[gi]])
                        gmm = (tT, tT.n)
                        lastg = tT.n
                        for ic in range(IC):
                            w.op("tensor",
                                 lambda e, o=ub2[gi][:],
                                 l=u16[:, ic * 128:(ic + 1) * 128],
                                 r=x3fm[:, ic * TOK + t * 512:ic * TOK + (t + 1) * 512],
                                 s=(ic == 0), p=(ic == IC - 1):
                                 e.matmul(o, l, r, start=s, stop=p),
                                 inc=(tT if ic == IC - 1 else None),
                                 waits=[urdy, ub2_free[gi]])
                        umm = (tT, tT.n)
                        si = gi
                        sv = w.op("scalar",
                                  lambda e, o=silu_f[si][:], i=gb2[gi][:]:
                                  e.activation(o, i, AF.Silu, scale=1.0),
                                  inc=tS, waits=[gmm, silu_free[si]])
                        gb2_free[gi] = (tS, sv)
                        hv = w.op("vector",
                                  lambda e, o=sh[:, ici * 512:(ici + 1) * 512],
                                  a=silu_f[si][:], b=ub2[gi][:]:
                                  e.tensor_tensor(o, a, b, ALU.mult),
                                  inc=tV, waits=[(tS, sv), umm, sh_cons])
                        silu_free[si] = (tV, hv)
                        ub2_free[gi] = (tV, hv)
                        sh_done = hv
                        gS.used(bg, (tT, lastg))
                        uS.used(bu, (tT, tT.n))
                        nx = gu_next2[0]
                        if nx < 64:
                            gS.prefetch(dt["wsg_t"][nx % 32])
                            uS.prefetch(dt["wsu_t"][nx % 32])
                            gu_next2[0] += 1
                    sh_ready = (tV, sh_done)
                    psum4_free[0] = gb2_free[0]
                    psum4_free[1] = gb2_free[1]
                    for oc in range(4):
                        mm_last = [0, 0, 0, 0]
                        for icg in range(4):
                            bd, drdy = dS.load(dt["wsd_t"][oc * 4 + icg])
                            d16 = dS.w16[bd]
                            for tl in range(4):
                                pb = banks4[tl]
                                for icl in range(8):
                                    ici = icg * 8 + icl
                                    w.op("tensor",
                                         lambda e, o=pb[:],
                                         l=sh[:, ici * 512 + tl * 128:ici * 512 + tl * 128 + 128],
                                         r=d16[:, icl * 512:(icl + 1) * 512],
                                         s=(icg == 0 and icl == 0),
                                         p=(icg == 3 and icl == 7):
                                         e.matmul(o, l, r, start=s, stop=p),
                                         inc=(tT if icl == 7 else None),
                                         waits=[sh_ready, drdy, psum4_free[tl],
                                                gb2_free[tl % 2] if tl < 2 else NONE_T])
                                mm_last[tl] = tT.n
                            dS.used(bd, (tT, max(mm_last)))
                        for tl in range(4):
                            tc = t * 4 + tl
                            dst = moe[:, tc * D + oc * 512:tc * D + (oc + 1) * 512]
                            cv = w.op("vector",
                                      lambda e, o=dst, i=banks4[tl][:],
                                      s=sgv[:, tc:tc + 1]:
                                      e.scalar_tensor_tensor(o, i, s, o,
                                                             ALU.mult, ALU.add),
                                      inc=tV,
                                      waits=[(tT, mm_last[tl]), moe_sh, comb_ready])
                            psum4_free[tl] = (tV, cv)
                            if oc == 3:
                                moe_tc[tc] = (tV, cv)
                    gb2_free[0] = psum4_free[0]
                    gb2_free[1] = psum4_free[1]
                    sh_cons = (tT, tT.n)
                moe_ready = (tV, tV.n)

                if debug:
                    for tc in range(NTC):
                        w.op("gpsimd",
                             lambda e, o=dbg["dbg_acc"][tc], i=moe[:, tc * D:(tc + 1) * D]:
                             e.dma_start(out=o, in_=i),
                             inc=qX, waits=[moe_ready])

                sbd = sbc
                xin3 = [sbd("xin3_0", (128, D))]
                xin3_free = [NONE_T]
                l2c2 = [sbd(f"l2c2_{i}", (128, D), BF16) for i in range(2)]
                l2c2_free = [NONE_T, NONE_T]
                qxin3 = [mksem("qxin3_0", True)]
                ql2c2 = [mksem("ql2c2_0", True), mksem("ql2c2_1", True)]
                qP = [mksem("qP0", True), mksem("qP1", True)]
                t42 = [sbd("t4_0", (128, D))]
                outc = [sbd(f"outc{i}", (128, D)) for i in range(2)]
                outc_free = [NONE_T, NONE_T]
                t4_free = [NONE_T]

                for tc in range(NTC):
                    bi = tc % 2
                    v1 = w.op("gpsimd",
                              lambda e, o=xin3[0][:], i=dt["x_own"][tc]:
                              e.dma_start(out=o, in_=i),
                              inc=qxin3[0], waits=[xin3_free[0]])
                    v2 = w.op("gpsimd",
                              lambda e, o=l2c2[bi][:], i=ln2a_d[tc]:
                              e.dma_start(out=o, in_=i),
                              inc=ql2c2[bi], waits=[l2c2_free[bi]])
                    t4 = t42[0]
                    lv = emit_ln(moe[:, tc * D:(tc + 1) * D], t4, [moe_tc[tc]],
                                 [t4_free[0]], eps_ln, ls=tc % 2)
                    ov = w.op("vector",
                              lambda e, o=outc[bi][:], a=xin3[0][:], b=l2c2[bi][:]:
                              e.tensor_tensor(o, a, b, ALU.add),
                              inc=tV, waits=[(qxin3[0], v1), (ql2c2[bi], v2), outc_free[bi]])
                    ov2 = w.op("vector",
                               lambda e, o=outc[bi][:], b=t4[:]:
                               e.tensor_tensor(o, o, b, ALU.add),
                               inc=tV, waits=[(tV, lv), (tV, ov)])
                    t4_free[0] = (tV, ov2)
                    xin3_free[0] = (tV, ov)
                    l2c2_free[bi] = (tV, ov2)
                    sv = w.op("gpsimd",
                              lambda e, o=out_d[tc], i=outc[bi][:]:
                              e.dma_start(out=o, in_=i),
                              inc=qP[bi], waits=[(tV, ov2)])
                    outc_free[bi] = (qP[bi], sv)

        w.wait("gpsimd", [(s, s.n) for s in ALLSEMS if s.dma])

        with nc.Block() as blk:
            @blk.sync
            def _(eng):
                for f in w.prog["sync"]:
                    f(eng)

            @blk.tensor
            def _(eng):
                for f in w.prog["tensor"]:
                    f(eng)

            @blk.vector
            def _(eng):
                for f in w.prog["vector"]:
                    f(eng)

            @blk.scalar
            def _(eng):
                for f in w.prog["scalar"]:
                    f(eng)

            @blk.gpsimd
            def _(eng):
                for f in w.prog["gpsimd"]:
                    f(eng)

    return nc


def kernel(**inputs):
    in_maps = host_prepare(inputs)
    nc = build_nc()
    res = run_bass_kernel_spmd(nc, in_maps, core_ids=list(range(NCORE)))
    out = np.empty((B, T, D), dtype=np.float32)
    for core in range(NCORE):
        b, half = core // 2, core % 2
        out[b, half * TOK:(half + 1) * TOK] = res.results[core]["out"].reshape(TOK, D)
    return out


# version 24
# speedup vs baseline: 1.1122x; 1.0265x over previous
import contextlib

import numpy as np
import ml_dtypes

import concourse.bass as bass
import concourse.mybir as mybir
from concourse.bass_utils import run_bass_kernel_spmd

F32 = mybir.dt.float32
BF16 = mybir.dt.bfloat16
AF = mybir.ActivationFunctionType
ALU = mybir.AluOpType
AX = mybir.AxisListType
BF = ml_dtypes.bfloat16

B, T, D = 4, 2048, 2048
H, KVH, DH = 16, 4, 128
IC = 16
CH = 512
WT = 1536
WC = 12
QOFF = 1023
NSLOT = 2
HT = 768
HC = 6
E, IEXP, ISH = 8, 1024, 4096
TOK, NTC = 1024, 8
UT, UC = 2048, 16
NCORE = 8
THETA = 10000.0


_PERM = np.empty(128, dtype=np.int64)
_PERM[:64] = np.arange(64) * 2
_PERM[64:] = np.arange(64) * 2 + 1


def _perm_heads(w, nheads):
    return w.reshape(nheads, 128, D)[:, _PERM, :].reshape(nheads * 128, D)


def _tile_rhs_out512(wT):
    nin, nout = wT.shape
    a = wT.reshape(nin // 128, 128, nout // 512, 512)
    return np.ascontiguousarray(
        a.transpose(2, 1, 0, 3).reshape(nout // 512, 128, (nin // 128) * 512))


def _tile_gu(w):
    I_, Din = w.shape
    a = w.reshape(I_ // 128, 128, Din // 128, 128)
    return np.ascontiguousarray(
        a.transpose(0, 3, 2, 1).reshape(I_ // 128, 128, (Din // 128) * 128))


_ROPE_CACHE = {}


def _rope_base():
    if "cs" in _ROPE_CACHE:
        return _ROPE_CACHE["cs"]
    import ctypes
    libm = ctypes.CDLL("libm.so.6")
    libm.cosf.restype = ctypes.c_float
    libm.cosf.argtypes = [ctypes.c_float]
    libm.sinf.restype = ctypes.c_float
    libm.sinf.argtypes = [ctypes.c_float]
    af = (np.arange(0, DH, 2, dtype=np.float32) / np.float32(DH)).astype(np.float32)
    p = np.power(np.float64(THETA), af.astype(np.float64)).astype(np.float32)
    inv = (np.float32(1.0) / p).astype(np.float32)
    ang = (np.arange(T, dtype=np.float32)[:, None] * inv[None, :]).astype(np.float32)
    flat = ang.ravel()
    cos = np.fromiter((libm.cosf(float(v)) for v in flat), np.float32,
                      count=flat.size).reshape(ang.shape)
    sin = np.fromiter((libm.sinf(float(v)) for v in flat), np.float32,
                      count=flat.size).reshape(ang.shape)
    _ROPE_CACHE["cs"] = (cos, sin)
    return cos, sin


def _rope_tables(positions, reps):
    cos_t, sin_t = _rope_base()
    idx = np.clip(positions.astype(np.int64), 0, T - 1)
    return (np.tile(cos_t[idx], (1, reps)), np.tile(sin_t[idx], (1, reps)))


def host_prepare(inputs):
    x = np.asarray(inputs["L0_attn_x"], dtype=np.float32)
    wq = np.asarray(inputs["L0_attn_wq"], dtype=np.float32)
    wk = np.asarray(inputs["L0_attn_wk"], dtype=np.float32)
    wv = np.asarray(inputs["L0_attn_wv"], dtype=np.float32)
    wo = np.asarray(inputs["L0_attn_wo"], dtype=np.float32)
    gw = np.asarray(inputs["L0_ffn_gate_w"], dtype=np.float32)
    egw = np.asarray(inputs["L0_ffn_expert_gate_w"], dtype=np.float32)
    euw = np.asarray(inputs["L0_ffn_expert_up_w"], dtype=np.float32)
    edw = np.asarray(inputs["L0_ffn_expert_down_w"], dtype=np.float32)
    sgw = np.asarray(inputs["L0_ffn_shared_gate_w"], dtype=np.float32)
    suw = np.asarray(inputs["L0_ffn_shared_up_w"], dtype=np.float32)
    sdw = np.asarray(inputs["L0_ffn_shared_down_w"], dtype=np.float32)
    seg = np.asarray(inputs["L0_ffn_shared_expert_gate_w"], dtype=np.float32)

    shared = {}
    shared["wq_t"] = _tile_rhs_out512(np.ascontiguousarray(_perm_heads(wq, H).T))
    shared["wk_t"] = _tile_rhs_out512(np.ascontiguousarray(_perm_heads(wk, KVH).T))
    shared["wv_t"] = _tile_rhs_out512(np.ascontiguousarray(wv.T))
    shared["wo_t"] = _tile_rhs_out512(np.ascontiguousarray(wo.T))
    gp = np.concatenate([gw.T, seg.T], axis=1)
    shared["gate_pack"] = np.ascontiguousarray(
        gp.reshape(IC, 128, 9).transpose(1, 0, 2).reshape(128, IC * 9))
    shared["wg_t"] = np.concatenate([_tile_gu(egw[e]) for e in range(E)])
    shared["wu_t"] = np.concatenate([_tile_gu(euw[e]) for e in range(E)])
    shared["wd_t"] = np.concatenate(
        [_tile_rhs_out512(np.ascontiguousarray(edw[e].T)) for e in range(E)])
    shared["wsg_t"] = _tile_gu(sgw)
    shared["wsu_t"] = _tile_gu(suw)
    wsd = _tile_rhs_out512(np.ascontiguousarray(sdw.T))
    shared["wsd_t"] = np.ascontiguousarray(
        wsd.reshape(4, 128, 4, 8 * 512).transpose(0, 2, 1, 3).reshape(16, 128, 4096))

    shared["idf"] = np.eye(128, dtype=np.float32)
    shared["idb"] = np.eye(128).astype(BF)
    shared["ones_col_b"] = np.ones((128, 1), dtype=BF)
    shared["ones_row_f"] = np.ones((1, 128), dtype=np.float32)
    d_ = np.arange(4)[:, None, None]
    p_ = np.arange(128)[None, :, None]
    i_ = np.arange(512)[None, None, :]
    cmask = np.where(i_ <= d_ * 128 + p_, np.float32(-1e30), np.float32(0.0))
    shared["cmask"] = np.ascontiguousarray(
        cmask.transpose(1, 0, 2).reshape(128, 4 * 512))
    shared["cmask16"] = shared["cmask"].astype(BF)

    in_maps = []
    for core in range(NCORE):
        b, half = core // 2, core % 2
        chunks = (2 * half, 2 * half + 1)
        m = dict(shared)
        ubase = half * 1024 - QOFF
        kpos_u = ubase + np.arange(UT)
        okmask = (kpos_u >= 0) & (kpos_u < T)
        xkv = np.zeros((UT, D), dtype=np.float32)
        xkv[okmask] = x[b, kpos_u[okmask]]
        val = np.where(okmask, np.float32(-12.0), np.float32(-1e9))
        m["x_kv"] = np.ascontiguousarray(xkv.reshape(UC, 128, D))
        m["x_own"] = np.ascontiguousarray(
            x[b, half * TOK:(half + 1) * TOK].reshape(NTC, 128, D))
        m["valid12"] = np.ascontiguousarray(val.reshape(UC, 128).T)
        cosk, sink = _rope_tables(np.maximum(kpos_u, 0).astype(np.float64), KVH)
        m["cosk"] = np.ascontiguousarray(
            cosk.reshape(UC, 128, 256).transpose(1, 0, 2)
            .reshape(128, UC * 256))
        m["sink"] = np.ascontiguousarray(
            sink.reshape(UC, 128, 256).transpose(1, 0, 2)
            .reshape(128, UC * 256))
        qpos = (half * TOK + np.arange(TOK)).astype(np.float64)
        cosq, sinq = _rope_tables(qpos, 4)
        m["cosq"] = np.ascontiguousarray(
            cosq.reshape(NTC, 128, 256).transpose(1, 0, 2).reshape(128, NTC * 256))
        m["sinq"] = np.ascontiguousarray(
            sinq.reshape(NTC, 128, 256).transpose(1, 0, 2).reshape(128, NTC * 256))
        in_maps.append(m)
    return in_maps



class Sem:
    __slots__ = ("h", "n", "dma")

    def __init__(self, h, dma):
        self.h, self.n, self.dma = h, 0, dma


NONE_T = (None, 0)


class Walk:
    ENGINES = ("sync", "tensor", "vector", "scalar", "gpsimd")

    def __init__(self):
        self.prog = {e: [] for e in self.ENGINES}
        self.hw = {e: {} for e in self.ENGINES}

    def _filter(self, eng, waits):
        hw = self.hw[eng]
        ws = []
        for s, v in waits:
            if s is None or v is None or v <= 0:
                continue
            if hw.get(id(s), -1) >= v:
                continue
            hw[id(s)] = v
            ws.append((s.h, v))
        return tuple(ws)

    def op(self, eng, fn, inc=None, waits=()):
        ws = self._filter(eng, waits)
        amt = 16 if (inc is not None and inc.dma) else 1
        if inc is not None:
            inc.n += amt
        semh = inc.h if inc is not None else None

        def thunk(e, fn=fn, ws=ws, semh=semh, amt=amt):
            for (h, v) in ws:
                e.wait_ge(h, v)
            ins = fn(e)
            if semh is not None:
                ins.then_inc(semh, amt)
        self.prog[eng].append(thunk)
        return inc.n if inc is not None else None

    def wait(self, eng, waits):
        ws = self._filter(eng, waits)
        if ws:
            def thunk(e, ws=ws):
                for (h, v) in ws:
                    e.wait_ge(h, v)
            self.prog[eng].append(thunk)


def build_nc(debug=False):
    nc = bass.Bass("TRN2", target_bir_lowering=False)

    dt = {}

    def din(name, shape, dtype=F32):
        dt[name] = nc.dram_tensor(name, list(shape), dtype, kind="ExternalInput").ap()

    din("x_kv", (UC, 128, D))
    din("x_own", (NTC, 128, D))
    din("valid12", (128, UC))
    din("cosk", (128, UC * 256))
    din("sink", (128, UC * 256))
    din("cosq", (128, NTC * 256))
    din("sinq", (128, NTC * 256))
    din("cmask", (128, 4 * 512))
    din("cmask16", (128, 4 * 512), BF16)
    din("idf", (128, 128))
    din("idb", (128, 128), BF16)
    din("ones_col_b", (128, 1), BF16)
    din("ones_row_f", (1, 128))
    din("gate_pack", (128, IC * 9))
    din("wq_t", (4, 128, IC * 512))
    din("wk_t", (1, 128, IC * 512))
    din("wv_t", (1, 128, IC * 512))
    din("wo_t", (4, 128, IC * 512))
    din("wg_t", (E * 8, 128, 2048))
    din("wu_t", (E * 8, 128, 2048))
    din("wd_t", (E * 4, 128, 4096))
    din("wsg_t", (32, 128, 2048))
    din("wsu_t", (32, 128, 2048))
    din("wsd_t", (16, 128, 4096))

    out_d = nc.dram_tensor("out", [NTC, 128, D], F32, kind="ExternalOutput").ap()
    ln2a_d = nc.dram_tensor("ln2a_scr", [NTC, 128, D], BF16, kind="Internal").ap()

    dbg = {}
    if debug:
        for name, shape, dty in [
            ("dbg_q16", (128, H * TOK), BF16),
            ("dbg_k16", (128, KVH * UT), BF16),
            ("dbg_v16", (128, UC * 512), BF16),
            ("dbg_y16", (128, H * TOK), BF16),
            ("dbg_ln2a", (NTC, 128, D), BF16),
            ("dbg_attn", (NTC, 128, D), BF16),
            ("dbg_comb", (128, NTC * 8), F32),
            ("dbg_sg", (128, NTC), F32),
            ("dbg_x3fm", (128, IC * TOK), BF16),
            ("dbg_he", (128, 8 * TOK), BF16),
            ("dbg_acc", (NTC, 128, D), BF16),
        ]:
            dbg[name] = nc.dram_tensor(name, list(shape), dty, kind="ExternalOutput").ap()

    w = Walk()
    es = contextlib.ExitStack()
    with es:
        ALLSEMS = []

        def mksem(name, dma):
            s = Sem(es.enter_context(nc.semaphore(name)), dma)
            ALLSEMS.append(s)
            return s

        qX = mksem("qX", True)
        qT = mksem("qT", True)
        tT = mksem("tT", False)
        tV = mksem("tV", False)
        tS = mksem("tS", False)
        tG = mksem("tG", False)

        def barrier():
            vals = [(s, s.n) for s in ALLSEMS]
            for eng in Walk.ENGINES:
                w.wait(eng, vals)

        _nuid = [0]

        def mksb(stack):
            def f(name, shape, dtype=F32):
                _nuid[0] += 1
                return stack.enter_context(
                    nc.sbuf_tensor(f"{name}_{_nuid[0]}", list(shape), dtype))
            return f

        sb = mksb(es)

        pA = es.enter_context(nc.psum_tensor("pA", [128, 512], F32))
        pB = es.enter_context(nc.psum_tensor("pB", [128, 512], F32))
        pC = es.enter_context(nc.psum_tensor("pC", [128, 512], F32))
        pD = es.enter_context(nc.psum_tensor("pD", [128, 512], F32))
        pS = es.enter_context(nc.psum_tensor("pS", [128, 512], F32))
        pR = es.enter_context(nc.psum_tensor("pR", [128, 512], F32))

        valid_sb = sb("valid_sb", (128, UC))

        idb_sb = sb("idb_sb", (128, 128), BF16)
        onesc_sb = sb("onesc_sb", (128, 1), BF16)
        onesr_sb = sb("onesr_sb", (1, 128))
        eps_ln = sb("eps_ln", (128, 1))
        eps_q = sb("eps_q", (128, 1))
        eps_k = sb("eps_k", (128, 1))
        stats2 = [sb("stats", (128, 24)) for _ in range(2)]
        mv2 = [sb("mv", (128, 2)) for _ in range(2)]
        rstd2 = [sb("rstd", (128, 4)) for _ in range(2)]
        ssq42 = [sb("ssq4", (128, 4)) for _ in range(2)]

        for name, ap_sb in [("valid12", valid_sb),
                            ("idb", idb_sb),
                            ("ones_col_b", onesc_sb), ("ones_row_f", onesr_sb)]:
            w.op("gpsimd", lambda e, o=ap_sb[:], i=dt[name]: e.dma_start(out=o, in_=i),
                 inc=qT)
        tbl = (qT, qT.n)
        w.op("vector", lambda e, o=eps_ln[:]: e.memset(o, 1e-5))
        w.op("vector", lambda e, o=eps_q[:]: e.memset(o, 1.28e-4))
        w.op("vector", lambda e, o=eps_k[:]: e.memset(o, 1e-6), inc=tV)
        eps_tk = (tV, tV.n)

        idb_ap = idb_sb[:]

        class Stream:

            _sid = [0]

            def __init__(self, salloc, name, fd, qd, cast_eng="gpsimd", nbuf=2):
                self.fd, self.qd, self.nq = fd, qd, fd // qd
                self.nbuf = nbuf
                self.stg = [salloc(f"{name}_s{i}", (128, qd)) for i in range(2)]
                self.w16 = [salloc(f"{name}_w{i}", (128, fd), BF16)
                            for i in range(nbuf)]
                Stream._sid[0] += 1
                self.qsem = [mksem(f"q{name}{Stream._sid[0]}_{i}", True)
                             for i in range(2)]
                self.stg_free = [NONE_T, NONE_T]
                self.w16_free = [NONE_T] * nbuf
                self.cnt = 0
                self.eng = cast_eng
                self.sem = {"vector": tV, "scalar": tS, "gpsimd": tG}[cast_eng]
                self.pending = []

            def load(self, dram_ap):
                bi = self.cnt % self.nbuf
                self.cnt += 1
                cv = 0
                for q in range(self.nq):
                    si = (self.cnt * self.nq + q) % 2
                    src = dram_ap[:, q * self.qd:(q + 1) * self.qd]
                    v = w.op("sync",
                             lambda e, o=self.stg[si][:], i=src: e.dma_start(out=o, in_=i),
                             inc=self.qsem[si], waits=[self.stg_free[si]])
                    o_ap = self.w16[bi][:, q * self.qd:(q + 1) * self.qd]
                    s_ap = self.stg[si][:]
                    if self.eng == "vector":
                        fn = lambda e, o=o_ap, s=s_ap: e.tensor_copy(o, s)
                    elif self.eng == "gpsimd":
                        fn = lambda e, o=o_ap, s=s_ap: e.tensor_copy(o, s)
                    else:
                        fn = lambda e, o=o_ap, s=s_ap: e.copy(o, s)
                    cv = w.op(self.eng, fn, inc=self.sem,
                              waits=[(self.qsem[si], v), self.w16_free[bi]])
                    self.stg_free[si] = (self.sem, cv)
                return bi, (self.sem, cv)

            def prefetch(self, dram_ap):
                self.pending.append(self.load(dram_ap))

            def take(self):
                return self.pending.pop(0)

            def used(self, bi, ticket):
                self.w16_free[bi] = ticket

        ln_setbusy = [NONE_T, NONE_T]

        def emit_ln(src, dst, src_ready, dst_free, eps_tile, ls=0):
            stats, mv, rstd = stats2[ls], mv2[ls], rstd2[ls]
            for s4 in range(4):
                w.op("vector",
                     lambda e, o=stats[:, s4 * 6:(s4 + 1) * 6],
                     i=src[:, s4 * 512:(s4 + 1) * 512]: e.bn_stats(o, i),
                     inc=(tV if s4 == 3 else None),
                     waits=list(src_ready) + ([ln_setbusy[ls]] if s4 == 0 else []))
            w.op("vector", lambda e, o=mv[:], i=stats[:]: e.bn_aggr(o, i),
                 inc=tV, waits=[(tV, tV.n), (tS, tS.n)])
            va = tV.n
            w.op("scalar",
                 lambda e, o=rstd[:, 0:1], i=mv[:, 1:2], b=eps_tile[:]:
                 e.activation(o, i, AF.Ln, bias=b, scale=1.0),
                 inc=tS, waits=[(tV, va), eps_tk])
            w.op("scalar",
                 lambda e, o=rstd[:, 0:1]: e.activation(o, o, AF.Exp, scale=-0.5),
                 inc=tS, waits=[(tS, tS.n)])
            vs = tS.n
            cv = w.op("vector",
                      lambda e, o=dst[:], i=src[:], m=mv[:, 0:1], r=rstd[:, 0:1]:
                      e.tensor_scalar(o, i, m, r, ALU.subtract, ALU.mult),
                      inc=tV, waits=[(tS, vs)] + list(dst_free))
            ln_setbusy[ls] = (tV, cv)
            return cv

        def emit_normrope(psum, qsc, qsq, qrot, cos_ap, sin_ap, eps_tile, scale,
                          mm_ticket, rot_cons, ls=0):
            rstd, ssq4 = rstd2[ls], ssq42[ls]
            cv = w.op("vector", lambda e, o=qsc[:], i=psum[:]: e.tensor_copy(o, i),
                      inc=tV, waits=[mm_ticket])
            psum_free = (tV, cv)
            w.op("vector", lambda e, o=qsq[:], i=qsc[:]: e.tensor_tensor(o, i, i, ALU.mult),
                 inc=tV, waits=[(tV, cv)])
            w.op("vector",
                 lambda e, o=ssq4[:], i=qsq[:].rearrange("p (h d) -> p h d", h=4):
                 e.tensor_reduce(o, i, axis=AX.X, op=ALU.add),
                 inc=tV, waits=[(tV, tV.n), (tS, tS.n)])
            va = tV.n
            w.op("scalar",
                 lambda e, o=rstd[:], i=ssq4[:], b=eps_tile[:], s=scale:
                 e.activation(o, i, AF.Ln, bias=b, scale=s),
                 inc=tS, waits=[(tV, va), eps_tk])
            w.op("scalar",
                 lambda e, o=rstd[:]: e.activation(o, o, AF.Exp, scale=-0.5),
                 inc=tS, waits=[(tS, tS.n)])
            vs = tS.n
            for hh in range(4):
                w.op("vector",
                     lambda e, o=qsc[:, hh * 128:(hh + 1) * 128], r=rstd[:, hh:hh + 1]:
                     e.tensor_scalar(o, o, r, None, ALU.mult),
                     inc=(tV if hh == 3 else None), waits=[(tS, vs)])
            sv = tV.n
            q3 = qsc[:].rearrange("p (h d) -> p h d", h=4)
            r3 = qrot[:].rearrange("p (h d) -> p h d", h=4)
            s3 = qsq[:].rearrange("p (h d) -> p h d", h=4)
            c3 = cos_ap.rearrange("p (h d) -> p h d", h=4)
            n3 = sin_ap.rearrange("p (h d) -> p h d", h=4)
            w.op("vector",
                 lambda e, o=r3[:, :, 0:64], a=q3[:, :, 0:64], b=c3:
                 e.tensor_tensor(o, a, b, ALU.mult),
                 waits=[(tV, sv), tbl] + list(rot_cons))
            w.op("vector",
                 lambda e, o=s3[:, :, 0:64], a=q3[:, :, 64:128], b=n3:
                 e.tensor_tensor(o, a, b, ALU.mult), inc=tV)
            w.op("vector",
                 lambda e, o=r3[:, :, 0:64], b=s3[:, :, 0:64]:
                 e.tensor_tensor(o, o, b, ALU.subtract),
                 inc=tV, waits=[(tV, tV.n)])
            w.op("vector",
                 lambda e, o=r3[:, :, 64:128], a=q3[:, :, 0:64], b=n3:
                 e.tensor_tensor(o, a, b, ALU.mult))
            w.op("vector",
                 lambda e, o=s3[:, :, 64:128], a=q3[:, :, 64:128], b=c3:
                 e.tensor_tensor(o, a, b, ALU.mult), inc=tV)
            w.op("vector",
                 lambda e, o=r3[:, :, 64:128], b=s3[:, :, 64:128]:
                 e.tensor_tensor(o, o, b, ALU.add),
                 inc=tV, waits=[(tV, tV.n)])
            return psum_free, tV.n

        ln2_stores = 0
        p1 = contextlib.ExitStack()
        with p1:
            sb1 = mksb(p1)
            q16 = sb1("q16", (128, H * TOK), BF16)
            k16 = sb1("k16", (128, KVH * UT), BF16)
            v16 = sb1("v16", (128, UC * 512), BF16)
            pab = contextlib.ExitStack()
            with pab:
                sb2 = mksb(pab)
                cosk_sl = sb2("cosk_sl", (128, UC * 256), BF16)
                sink_sl = sb2("sink_sl", (128, UC * 256), BF16)
                cosq_sb = sb2("cosq_sb", (128, NTC * 256), BF16)
                sinq_sb = sb2("sinq_sb", (128, NTC * 256), BF16)
                qTa = mksem("qTa", True)
                with nc.sbuf_tensor("ropestg", [128, UC * 256], F32) as ropestg:
                    stg_ap = ropestg[:]
                    prev = NONE_T
                    for nm_, dst_ in (("cosk", cosk_sl), ("sink", sink_sl),
                                      ("cosq", cosq_sb), ("sinq", sinq_sb)):
                        fdim = dst_[:].tensor.shape[1]
                        v_ = w.op("gpsimd",
                                  lambda e, o=ropestg[0:128, 0:fdim], i=dt[nm_]:
                                  e.dma_start(out=o, in_=i),
                                  inc=qTa, waits=[prev])
                        cv_ = w.op("vector",
                                   lambda e, o=dst_[:], i=ropestg[0:128, 0:fdim]:
                                   e.tensor_copy(o, i),
                                   inc=tV, waits=[(qTa, v_)])
                        prev = (tV, cv_)
                cs_ld = prev
                w.wait("gpsimd", [prev])
                xin = [sb2(f"xin{i}", (128, D)) for i in range(2)]
                xin_free = [NONE_T] * 2
                qxin = [mksem("qxin0", True), mksem("qxin1", True)]
                x1c = [sb2(f"x1c{i}", (128, D), BF16) for i in range(2)]
                x1c_free = [NONE_T] * 2
                x1fm = sb2("x1fm", (128, IC * UT), BF16)
                qscb = [sb2("qsc", (128, 512), BF16) for _ in range(2)]
                qsqb = [sb2("qsq", (128, 512), BF16) for _ in range(2)]
                qrotb = [sb2("qrot", (128, 512), BF16) for _ in range(2)]
                wstr = Stream(sb2, "wA", 4096, 1024, cast_eng="gpsimd", nbuf=2)
                pTb = [pab.enter_context(nc.psum_tensor("pT_0", [128, 1024], BF16)),
                       pab.enter_context(nc.psum_tensor("pT_1", [128, 1024], BF16))]

                x1_r = x1fm[:].rearrange("p (ic t) -> p ic t", ic=IC)
                pT_r = [pTb[0][:].rearrange("p (ii t) -> p ii t", ii=8),
                        pTb[1][:].rearrange("p (ii t) -> p ii t", ii=8)]
                tr_free = [NONE_T, NONE_T]
                nr_state = [dict(rot_cons=NONE_T) for _ in range(2)]
                bank_free2 = [NONE_T, NONE_T]
                unit_ctr = [0]
                x1_ck = [NONE_T] * UC
                tr_pending = []

                def flush_tr(keep=1):
                    while len(tr_pending) > keep:
                        tr_pending.pop(0)()

                def qkv_unit(bank, mmv, cos_ap, sin_ap, eps_tile, scale,
                             dst_sl, bi_bank):
                    u = unit_ctr[0] % 2
                    unit_ctr[0] += 1
                    st = nr_state[u]
                    pf, nv = emit_normrope(
                        bank, qscb[u], qsqb[u], qrotb[u], cos_ap, sin_ap,
                        eps_tile, scale, mmv, [st["rot_cons"]], ls=u)
                    bank_free2[bi_bank] = pf

                    def part2(u=u, nv=nv, dst_sl=dst_sl, st=st):
                        for hh in range(4):
                            w.op("tensor",
                                 lambda e, o=pTb[u][:, hh * 128:(hh + 1) * 128],
                                 i=qrotb[u][:, hh * 128:(hh + 1) * 128]:
                                 e.transpose(o, i, idb_ap),
                                 inc=(tT if hh == 3 else None),
                                 waits=[(tV, nv), tr_free[u]])
                        tt = (tT, tT.n)
                        st["rot_cons"] = tt
                        cv = w.op("vector",
                                  lambda e, o=dst_sl, i=pT_r[u][:, 0:4, :]:
                                  e.tensor_copy(o, i),
                                  inc=tV, waits=[tt])
                        tr_free[u] = (tV, cv)
                    tr_pending.append(part2)

                def emit_ln1_chunk(tc):
                    bi = tc % 2
                    v = w.op("gpsimd",
                             lambda e, o=xin[bi][:], i=dt["x_kv"][tc]:
                             e.dma_start(out=o, in_=i),
                             inc=qxin[bi], waits=[xin_free[bi]])
                    cv = emit_ln(xin[bi], x1c[bi], [(qxin[bi], v)],
                                 [x1c_free[bi]], eps_ln, ls=bi)
                    xin_free[bi] = (tV, cv)
                    for g in range(2):
                        for ii in range(8):
                            ic = g * 8 + ii
                            w.op("tensor",
                                 lambda e, o=pTb[g][:, ii * 128:(ii + 1) * 128],
                                 i=x1c[bi][:, ic * 128:(ic + 1) * 128]:
                                 e.transpose(o, i, idb_ap),
                                 inc=(tT if ii == 7 else None),
                                 waits=[(tV, cv), tr_free[g], tbl])
                        tt = tT.n
                        cpv = w.op("vector",
                                   lambda e,
                                   o=x1_r[:, g * 8:(g + 1) * 8, tc * 128:tc * 128 + 128],
                                   i=pT_r[g]:
                                   e.tensor_copy(o, i),
                                   inc=tV, waits=[(tT, tt)])
                        tr_free[g] = (tV, cpv)
                    x1c_free[bi] = tr_free[1]
                    x1_ck[tc] = tr_free[1]

                def slab_ap(mat, oc, hs):
                    return dt[mat + "_t"][oc][:, hs * 4096:(hs + 1) * 4096]

                def emit_qkv_tiles(mat, oc, wA, wB, rdyA, rdyB, tiles):
                    last_mm = 0
                    for tc in tiles:
                        bank = [pA, pB][tc % 2]
                        if mat == "wq":
                            loff = QOFF + tc * 128
                            xr = [x1_ck[7 + tc], x1_ck[8 + tc if 8 + tc < UC else UC - 1]]
                        else:
                            loff = tc * 128
                            xr = [x1_ck[tc]]
                        for ic in range(IC):
                            ws16 = wA if ic < 8 else wB
                            w.op("tensor",
                                 lambda e, o=bank[:],
                                 l=x1fm[:, ic * UT + loff:ic * UT + loff + 128],
                                 r=ws16[:, (ic % 8) * 512:(ic % 8 + 1) * 512],
                                 s=(ic == 0), p=(ic == IC - 1):
                                 e.matmul(o, l, r, start=s, stop=p),
                                 inc=(tT if ic == IC - 1 else None),
                                 waits=xr + [rdyA, rdyB, bank_free2[tc % 2]])
                        mmv = (tT, tT.n)
                        last_mm = tT.n
                        if mat == "wv":
                            cv = w.op("vector",
                                      lambda e,
                                      o=v16[:, tc * 512:(tc + 1) * 512],
                                      i=bank[:]: e.tensor_copy(o, i),
                                      inc=tV, waits=[mmv])
                            bank_free2[tc % 2] = (tV, cv)
                        elif mat == "wk":
                            w.wait("vector", [cs_ld])
                            k_r = k16[:].rearrange("p (h t) -> p h t", h=KVH)
                            qkv_unit(bank, mmv,
                                     cosk_sl[:, tc * 256:(tc + 1) * 256],
                                     sink_sl[:, tc * 256:(tc + 1) * 256],
                                     eps_k, 1.0 / 128.0,
                                     k_r[:, :, tc * 128:tc * 128 + 128],
                                     tc % 2)
                        else:
                            q_r = q16[:].rearrange("p (h t) -> p h t", h=H)
                            qkv_unit(bank, mmv,
                                     cosq_sb[:, tc * 256:(tc + 1) * 256],
                                     sinq_sb[:, tc * 256:(tc + 1) * 256],
                                     eps_q, 1.0,
                                     q_r[:, oc * 4:(oc + 1) * 4, tc * 128:tc * 128 + 128],
                                     tc % 2)
                        flush_tr(keep=1)
                    return last_mm

                wstr.prefetch(slab_ap("wk", 0, 0))
                wstr.prefetch(slab_ap("wk", 0, 1))
                emit_ln1_chunk(0)
                biA, rdyA = wstr.take()
                biB, rdyB = wstr.take()
                wkA, wkB = wstr.w16[biA], wstr.w16[biB]
                lm = 0
                for tc in range(1, UC):
                    emit_ln1_chunk(tc)
                    lm = emit_qkv_tiles("wk", 0, wkA, wkB, rdyA, rdyB, [tc - 1])
                lm = emit_qkv_tiles("wk", 0, wkA, wkB, rdyA, rdyB, [UC - 1])
                wstr.used(biA, (tT, lm))
                wstr.used(biB, (tT, lm))
                rest = [("wv", 0)] + [("wq", oc) for oc in range(4)]
                wstr.prefetch(slab_ap("wv", 0, 0))
                wstr.prefetch(slab_ap("wv", 0, 1))
                for si_, (mat, oc) in enumerate(rest):
                    biA, rdyA = wstr.take()
                    biB, rdyB = wstr.take()
                    wA, wB = wstr.w16[biA], wstr.w16[biB]
                    tiles = range(8) if mat == "wq" else range(UC)
                    lm = emit_qkv_tiles(mat, oc, wA, wB, rdyA, rdyB, tiles)
                    wstr.used(biA, (tT, lm))
                    wstr.used(biB, (tT, lm))
                    if si_ + 1 < len(rest):
                        m2_, o2_ = rest[si_ + 1]
                        wstr.prefetch(slab_ap(m2_, o2_, 0))
                        wstr.prefetch(slab_ap(m2_, o2_, 1))
                flush_tr(keep=0)
            qkv_ready = (tV, tV.n)
            barrier()

            if debug:
                for nm, src_ in [("dbg_q16", q16), ("dbg_k16", k16), ("dbg_v16", v16)]:
                    w.op("gpsimd", lambda e, o=dbg[nm], i=src_[:]:
                         e.dma_start(out=o, in_=i), inc=qX, waits=[qkv_ready])

            pcd = contextlib.ExitStack()
            with pcd:
                sb3 = mksb(pcd)
                exp16 = [sb3(f"exp16_{i}", (128, 512), BF16) for i in range(6)]
                sacc = [sb3(f"sacc{i}", (128, 512)) for i in range(4)]
                onescf = sb3("onescf", (128, 1))
                w.op("vector", lambda e, o=onescf[:]: e.memset(o, 1.0), inc=tV)
                rcp = [sb3(f"rcp{i}", (1, 512)) for i in range(2)]
                bcf = [sb3(f"bcf{i}", (128, 512)) for i in range(2)]
                y16 = sb3("y16", (128, H * TOK), BF16)
                attn16 = [sb3(f"attn16_{i}", (128, D), BF16) for i in range(8)]
                ln2c = [sb3(f"ln2c{i}", (128, D), BF16) for i in range(2)]
                ln2c_free = [NONE_T, NONE_T]
                qL = [mksem("qL0", True), mksem("qL1", True)]
                wstr2 = Stream(sb3, "wB", 4096, 2048, cast_eng="vector", nbuf=2)
                pSu = [pcd.enter_context(nc.psum_tensor(f"pSu{i}", [128, 512], F32))
                       for i in range(2)]
                cmask16_sb = sb3("cmask16_sb", (128, 4 * 512), BF16)
                qTb = mksem("qTb", True)
                w.op("gpsimd", lambda e, o=cmask16_sb[:], i=dt["cmask16"]:
                     e.dma_start(out=o, in_=i), inc=qTb)
                cm_ld = (qTb, qTb.n)
                w.wait("tensor", [cm_ld])

                scb = [[pA, pB], [pC, pD]]
                avb = [pS, pR]
                av_free = [NONE_T, NONE_T]
                sc_free = [[NONE_T, NONE_T], [NONE_T, NONE_T]]
                exp_freeT = [NONE_T] * 6
                exp_freeV = [NONE_T] * 6
                sum_free = [NONE_T, NONE_T]
                sacc_ck = [NONE_T] * 4
                sacc_cons = [NONE_T, NONE_T]
                pending = [None, None]

                def emit_tail(u):
                    h, j, rcpv = pending[u]
                    pending[u] = None
                    sbank = scb[u][1]
                    w.op("tensor",
                         lambda e, o=sbank[:], l=onesr_sb[:], r=rcp[u][:]:
                         e.matmul(o, l, r, start=True, stop=True),
                         inc=tT, waits=[rcpv, sc_free[u][1]])
                    bct = (tT, tT.n)
                    cv2 = w.op("vector",
                               lambda e, o=bcf[u][:], i=sbank[:]:
                               e.tensor_copy(o, i),
                               inc=tV, waits=[bct])
                    sc_free[u][1] = (tV, cv2)
                    yv = w.op("vector",
                              lambda e, o=y16[:, h * TOK + j * CH:h * TOK + (j + 1) * CH],
                              b2=bcf[u][:]:
                              e.tensor_tensor(o, o, b2, ALU.mult),
                              inc=tV, waits=[(tV, cv2)])
                    return yv

                for j in range(NSLOT):
                    for hp in range(H // 2):
                        exvq = {}
                        for kc in range(WC + 1):
                            if kc < WC:
                                uck = 4 * j + kc
                                for u in range(2):
                                    h = 2 * hp + u
                                    kvh = h // 4
                                    ei = u * 3 + kc % 3
                                    bank = scb[u][kc % 2]
                                    w.op("tensor",
                                         lambda e, o=bank[:],
                                         l=k16[:, kvh * UT + uck * 128:kvh * UT + uck * 128 + 128],
                                         r=q16[:, h * TOK + j * CH:h * TOK + (j + 1) * CH],
                                         p=(kc < 8):
                                         e.matmul(o, l, r, start=True, stop=p),
                                         inc=(tT if kc < 8 else None),
                                         waits=[qkv_ready, sc_free[u][kc % 2], tbl])
                                    if kc >= 8:
                                        w.op("tensor",
                                             lambda e, o=bank[:], l=idb_ap,
                                             r=cmask16_sb[:, (kc - 8) * 512:(kc - 7) * 512]:
                                             e.matmul(o, l, r, start=False, stop=True),
                                             inc=tT)
                                    mmv = (tT, tT.n)
                                    ev = w.op("scalar",
                                              lambda e, o=exp16[ei][:], i=bank[:],
                                              b=valid_sb[:, uck:uck + 1]:
                                              e.activation(o, i, AF.Exp, bias=b, scale=1.0),
                                              inc=tS, waits=[mmv, exp_freeT[ei],
                                                             exp_freeV[ei], tbl])
                                    exvq[(u, kc)] = (tS, ev)
                                    sc_free[u][kc % 2] = (tS, ev)
                            if kc == 2 and pending[0] is not None:
                                emit_tail(0)
                                emit_tail(1)
                            if kc >= 1:
                                kd = kc - 1
                                uckd = 4 * j + kd
                                for u in range(2):
                                    h = 2 * hp + u
                                    kvh = h // 4
                                    ei = u * 3 + kd % 3
                                    exw = exvq.pop((u, kd))
                                    w.op("tensor",
                                         lambda e, o=avb[u][:],
                                         l=v16[:, uckd * 512 + kvh * 128:uckd * 512 + kvh * 128 + 128],
                                         r=exp16[ei][:],
                                         s=(kd == 0), p=(kd == WC - 1):
                                         e.matmul(o, l, r, start=s, stop=p),
                                         inc=tT,
                                         waits=[exw] + ([av_free[u]] if kd == 0 else []))
                                    avt = (tT, tT.n)
                                    par = kd % 2
                                    ai = u * 2 + par
                                    if kd < 2:
                                        sv_ = w.op("vector",
                                                   lambda e, o=sacc[ai][:], i=exp16[ei][:]:
                                                   e.tensor_copy(o, i),
                                                   inc=tV,
                                                   waits=[exw, sacc_cons[u]])
                                    else:
                                        sv_ = w.op("vector",
                                                   lambda e, o=sacc[ai][:], a=sacc[ai][:],
                                                   i=exp16[ei][:]:
                                                   e.tensor_tensor(o, a, i, ALU.add),
                                                   inc=tV,
                                                   waits=[exw, sacc_ck[ai]])
                                    sacc_ck[ai] = (tV, sv_)
                                    exp_freeT[ei] = avt
                                    exp_freeV[ei] = (tV, sv_)
                        av_done = (tT, tT.n)
                        for u in range(2):
                            h = 2 * hp + u
                            cv = w.op("vector",
                                      lambda e, o=y16[:, h * TOK + j * CH:h * TOK + (j + 1) * CH],
                                      i=avb[u][:]: e.tensor_copy(o, i),
                                      inc=tV, waits=[av_done])
                            av_free[u] = (tV, cv)
                            mg = w.op("vector",
                                      lambda e, o=sacc[u * 2][:], a=sacc[u * 2][:],
                                      b=sacc[u * 2 + 1][:]:
                                      e.tensor_tensor(o, a, b, ALU.add),
                                      inc=tV,
                                      waits=[sacc_ck[u * 2], sacc_ck[u * 2 + 1]])
                            w.op("tensor",
                                 lambda e, o=pSu[u][0:1, :], l=onescf[:],
                                 r=sacc[u * 2][:]:
                                 e.matmul(o, l, r, start=True, stop=True),
                                 inc=tT, waits=[(tV, mg), sum_free[u]])
                            smm2 = (tT, tT.n)
                            sacc_cons[u] = smm2
                            lv = w.op("scalar",
                                      lambda e, o=rcp[u][:], i=pSu[u][0:1, :]:
                                      e.activation(o, i, AF.Ln, scale=1.0),
                                      inc=tS, waits=[smm2])
                            sum_free[u] = (tS, lv)
                            rv = w.op("scalar",
                                      lambda e, o=rcp[u][:]:
                                      e.activation(o, o, AF.Exp, scale=-1.0),
                                      inc=tS, waits=[(tS, lv)])
                            pending[u] = (h, j, (tS, rv))
                emit_tail(0)
                emit_tail(1)
                y_ready = (tV, tV.n)

                if debug:
                    w.op("gpsimd", lambda e, o=dbg["dbg_y16"], i=y16[:]:
                         e.dma_start(out=o, in_=i), inc=qX, waits=[y_ready])

                attn_parts = [0] * 8
                ob_free = [NONE_T, NONE_T]
                wstr2.prefetch(dt["wo_t"][0][:, 0:4096])
                wstr2.prefetch(dt["wo_t"][0][:, 4096:8192])
                for oc in range(4):
                    biA2, rdyA2 = wstr2.take()
                    biB2, rdyB2 = wstr2.take()
                    woA, woB = wstr2.w16[biA2], wstr2.w16[biB2]
                    last_mm = 0
                    for tq in range(8):
                        bank = [pA, pB][tq % 2]
                        for ic in range(IC):
                            ws16 = woA if ic < 8 else woB
                            w.op("tensor",
                                 lambda e, o=bank[:],
                                 l=y16[:, ic * TOK + tq * 128:ic * TOK + tq * 128 + 128],
                                 r=ws16[:, (ic % 8) * 512:(ic % 8 + 1) * 512],
                                 s=(ic == 0), p=(ic == IC - 1):
                                 e.matmul(o, l, r, start=s, stop=p),
                                 inc=(tT if ic == IC - 1 else None),
                                 waits=[y_ready, rdyA2, rdyB2, ob_free[tq % 2]])
                        mmv = (tT, tT.n)
                        last_mm = tT.n
                        cv = w.op("vector",
                                  lambda e, o=attn16[tq][:, oc * 512:(oc + 1) * 512],
                                  i=bank[:]: e.tensor_copy(o, i),
                                  inc=tV, waits=[mmv])
                        ob_free[tq % 2] = (tV, cv)
                        attn_parts[tq] = cv
                    wstr2.used(biA2, (tT, last_mm))
                    wstr2.used(biB2, (tT, last_mm))
                    if oc + 1 < 4:
                        wstr2.prefetch(dt["wo_t"][oc + 1][:, 0:4096])
                        wstr2.prefetch(dt["wo_t"][oc + 1][:, 4096:8192])
                for tq in range(8):
                    bi = tq % 2
                    if debug:
                        w.op("gpsimd",
                             lambda e, o=dbg["dbg_attn"][tq], i=attn16[tq][:]:
                             e.dma_start(out=o, in_=i),
                             inc=qX, waits=[(tV, attn_parts[tq])])
                    cv = emit_ln(attn16[tq], ln2c[bi], [(tV, attn_parts[tq])],
                                 [ln2c_free[bi]], eps_ln, ls=bi)
                    sv = w.op("gpsimd",
                              lambda e, o=ln2a_d[tq], i=ln2c[bi][:]:
                              e.dma_start(out=o, in_=i),
                              inc=qL[bi], waits=[(tV, cv)])
                    ln2c_free[bi] = (qL[bi], sv)
                    ln2_stores += 1
                    if debug:
                        w.op("gpsimd",
                             lambda e, o=dbg["dbg_ln2a"][tq], i=ln2c[bi][:]:
                             e.dma_start(out=o, in_=i), inc=qX)
        ln2_done2 = [(qL[0], qL[0].n), (qL[1], qL[1].n)]

        p2 = contextlib.ExitStack()
        with p2:
            sbf = mksb(p2)
            idf_sb = sbf("idf_sb", (128, 128))
            gate_sb = sbf("gate_sb", (128, IC * 9))
            qTc = mksem("qTc", True)
            w.op("gpsimd", lambda e, o=idf_sb[:], i=dt["idf"]:
                 e.dma_start(out=o, in_=i), inc=qTc)
            w.op("gpsimd", lambda e, o=gate_sb[:], i=dt["gate_pack"]:
                 e.dma_start(out=o, in_=i), inc=qTc)
            p2tbl = (qTc, qTc.n)
            w.wait("tensor", [p2tbl])
            w.wait("vector", [p2tbl])
            idf_ap = idf_sb[:]
            x3fm = sbf("x3fm", (128, IC * TOK), BF16)
            moe = sbf("moe", (128, NTC * D), BF16)
            comb = sbf("comb", (128, NTC * 8))
            sgv = sbf("sgv", (128, NTC))
            x3_r = x3fm[:].rearrange("p (ic t) -> p ic t", ic=IC)

            barrier()
            p2a = contextlib.ExitStack()
            with p2a:
                sba = mksb(p2a)
                xin2 = [sba(f"xin2_{i}", (128, D)) for i in range(2)]
                xin2_free = [NONE_T, NONE_T]
                l2c = [sba(f"l2c{i}", (128, D), BF16) for i in range(2)]
                l2c_free = [NONE_T, NONE_T]
                qxin2 = [mksem("qxin2_0", True), mksem("qxin2_1", True)]
                ql2c = [mksem("ql2c_0", True), mksem("ql2c_1", True)]
                x2c2 = [sba("x2c", (128, D)) for _ in range(2)]
                x3c2 = [sba("x3c", (128, D)) for _ in range(2)]
                xf322 = [sba("xf32", (128, D)) for _ in range(2)]
                r9 = sba("r9", (9, 128))
                pr9 = sba("pr9", (128, 9))
                prb = sba("prb", (128, 8))
                m1 = sba("m1", (128, 1))
                m2 = sba("m2", (128, 1))
                sm = sba("sm", (128, 1))
                eqb = sba("eqb", (128, 8))

                x2c_free = [NONE_T, NONE_T]
                x3c_free = [NONE_T, NONE_T]
                xf32_free = [NONE_T, NONE_T]
                x3_ck = [NONE_T] * NTC
                comb_ck = [NONE_T] * NTC
                pR_free = NONE_T
                pDt_free = NONE_T
                r9_cons = NONE_T
                pr9_cons = NONE_T
                trb_free = [NONE_T, NONE_T]
                for tc in range(NTC):
                    bi = tc % 2
                    x2c, x3c, xf32 = x2c2[bi], x3c2[bi], xf322[bi]
                    v1 = w.op("gpsimd",
                              lambda e, o=xin2[bi][:], i=dt["x_own"][tc]:
                              e.dma_start(out=o, in_=i),
                              inc=qxin2[bi], waits=[xin2_free[bi]] + ln2_done2)
                    v2 = w.op("gpsimd",
                              lambda e, o=l2c[bi][:], i=ln2a_d[tc]:
                              e.dma_start(out=o, in_=i),
                              inc=ql2c[bi], waits=[l2c_free[bi]])
                    cv = w.op("vector",
                              lambda e, o=x2c[:], a=xin2[bi][:], b=l2c[bi][:]:
                              e.tensor_tensor(o, a, b, ALU.add),
                              inc=tV, waits=[(qxin2[bi], v1), (ql2c[bi], v2), x2c_free[bi]])
                    xin2_free[bi] = (tV, cv)
                    l2c_free[bi] = (tV, cv)
                    lv = emit_ln(x2c, x3c, [(tV, cv)], [x3c_free[bi]], eps_ln, ls=bi)
                    x2c_free[bi] = (tV, lv)
                    grp_cp = 0
                    for g in range(4):
                        bank = [pA, pB][g % 2]
                        for ii in range(4):
                            ic = g * 4 + ii
                            w.op("tensor",
                                 lambda e, o=bank[:, ii * 128:(ii + 1) * 128],
                                 i=x3c[:, ic * 128:(ic + 1) * 128]:
                                 e.transpose(o, i, idf_ap),
                                 inc=(tT if ii == 3 else None),
                                 waits=[(tV, lv), trb_free[g % 2], tbl])
                        tt = (tT, tT.n)
                        w.op("vector",
                             lambda e,
                             o=x3_r[:, g * 4:(g + 1) * 4, tc * 128:tc * 128 + 128],
                             i=bank[:].rearrange("p (ii t) -> p ii t", ii=4):
                             e.tensor_copy(o, i),
                             waits=[tt])
                        cv2 = w.op("vector",
                                   lambda e, o=xf32[:, g * 512:(g + 1) * 512], i=bank[:]:
                                   e.tensor_copy(o, i),
                                   inc=tV, waits=[tt, xf32_free[bi]])
                        trb_free[g % 2] = (tV, cv2)
                        grp_cp = cv2
                    x3c_free[bi] = (tV, grp_cp)
                    x3_ck[tc] = (tV, grp_cp)
                    xf_rdy = (tV, grp_cp)
                    for ic in range(IC):
                        w.op("tensor",
                             lambda e, o=pR[0:9, 0:128],
                             l=gate_sb[:, ic * 9:(ic + 1) * 9],
                             r=xf32[:, ic * 128:(ic + 1) * 128],
                             s=(ic == 0), p=(ic == IC - 1):
                             e.matmul(o, l, r, start=s, stop=p),
                             inc=(tT if ic == IC - 1 else None),
                             waits=[xf_rdy, tbl, pR_free])
                    rmm = (tT, tT.n)
                    xf32_free[bi] = rmm
                    cvr = w.op("vector", lambda e, o=r9[:], i=pR[0:9, 0:128]:
                               e.tensor_copy(o, i),
                               inc=tV, waits=[rmm, r9_cons])
                    pR_free = (tV, cvr)
                    w.op("tensor",
                         lambda e, o=pD[0:128, 0:9], i=r9[:], d9=idf_sb[0:9, 0:9]:
                         e.transpose(o, i, d9),
                         inc=tT, waits=[(tV, cvr), pDt_free])
                    r9_cons = (tT, tT.n)
                    cvp = w.op("vector", lambda e, o=pr9[:], i=pD[0:128, 0:9]:
                               e.tensor_copy(o, i),
                               inc=tV, waits=[r9_cons, pr9_cons])
                    pDt_free = (tV, cvp)
                    w.op("vector", lambda e, o=m1[:], i=pr9[:, 0:8]:
                         e.reduce_max(o, i, axis=AX.X),
                         inc=tV, waits=[(tV, cvp), (tS, tS.n)])
                    w.op("vector", lambda e, o=prb[:], i=pr9[:, 0:8], s=m1[:]:
                         e.tensor_scalar(o, i, s, None, ALU.subtract),
                         inc=tV, waits=[(tV, tV.n)])
                    av_ = tV.n
                    w.op("scalar", lambda e, o=prb[:]:
                         e.activation(o, o, AF.Exp, scale=1.0),
                         inc=tS, waits=[(tV, av_)])
                    ev = (tS, tS.n)
                    w.op("vector", lambda e, o=sm[:], i=prb[:]:
                         e.reduce_sum(o, i, axis=AX.X),
                         inc=tV, waits=[ev])
                    w.op("vector", lambda e, o=sm[:]: e.reciprocal(o, o),
                         inc=tV, waits=[(tV, tV.n)])
                    w.op("vector", lambda e, o=prb[:], s=sm[:]:
                         e.tensor_scalar(o, o, s, None, ALU.mult),
                         inc=tV, waits=[(tV, tV.n)])
                    w.op("vector", lambda e, o=m1[:], i=prb[:]:
                         e.reduce_max(o, i, axis=AX.X),
                         inc=tV, waits=[(tV, tV.n)])
                    w.op("vector", lambda e, o=eqb[:], i=prb[:], s=m1[:]:
                         e.tensor_scalar(o, i, s, None, ALU.is_equal),
                         inc=tV, waits=[(tV, tV.n)])
                    w.op("vector", lambda e, o=eqb[:], i=prb[:]:
                         e.scalar_tensor_tensor(o, o, 1e30, i, ALU.mult, ALU.subtract),
                         inc=tV, waits=[(tV, tV.n)])
                    w.op("vector", lambda e, o=m2[:], i=eqb[:]:
                         e.tensor_reduce(o, i, axis=AX.X, op=ALU.min, negate=True),
                         inc=tV, waits=[(tV, tV.n)])
                    w.op("vector", lambda e, o=eqb[:], i=prb[:], s=m2[:]:
                         e.tensor_scalar(o, i, s, None, ALU.is_ge),
                         inc=tV, waits=[(tV, tV.n)])
                    w.op("vector",
                         lambda e, o=comb[:, tc * 8:(tc + 1) * 8], a=prb[:], b=eqb[:]:
                         e.tensor_tensor(o, a, b, ALU.mult),
                         inc=tV, waits=[(tV, tV.n)])
                    w.op("scalar",
                         lambda e, o=m1[:], i=pr9[:, 8:9]:
                         e.activation(o, i, AF.Tanh, scale=0.5),
                         inc=tS, waits=[(tV, tV.n)])
                    pr9_cons = (tS, tS.n)
                    w.op("vector",
                         lambda e, o=sgv[:, tc:tc + 1], i=m1[:]:
                         e.tensor_scalar(o, i, 0.5, 0.5, ALU.mult, ALU.add),
                         inc=tV, waits=[(tS, tS.n)])
                    comb_ck[tc] = (tV, tV.n)
                x3_ready = (tV, tV.n)
                comb_ready = (tV, tV.n)

            if debug:
                for nm, src in [("dbg_x3fm", x3fm), ("dbg_comb", comb), ("dbg_sg", sgv)]:
                    w.op("gpsimd", lambda e, o=dbg[nm], i=src[:]:
                         e.dma_start(out=o, in_=i), inc=qX, waits=[x3_ready])

            barrier()
            p2b = contextlib.ExitStack()
            with p2b:
                sbb = mksb(p2b)
                he2 = [sbb("he", (128, 8 * TOK), BF16) for _ in range(2)]
                silu_f = [sbb(f"silu{i}", (128, 512)) for i in range(2)]
                silu_free = [NONE_T, NONE_T]
                gS = Stream(sbb, "wg", 2048, 2048, cast_eng="vector")
                uS = Stream(sbb, "wu", 2048, 2048, cast_eng="scalar")
                dS = Stream(sbb, "wd", 4096, 2048, cast_eng="scalar")

                gb = [pA, pB]
                ub = [pC, pD]
                db = [pS, pR]
                gb_free = [NONE_T, NONE_T]
                ub_free = [NONE_T, NONE_T]
                db_free = [NONE_T, NONE_T]
                he_free = [NONE_T, NONE_T]
                moe_parts = NONE_T
                for pf in range(2):
                    gS.prefetch(dt["wg_t"][pf])
                    uS.prefetch(dt["wu_t"][pf])
                dS.prefetch(dt["wd_t"][0])
                dS.prefetch(dt["wd_t"][1])
                gu_next = 2
                d_next = 2
                for ee in range(E):
                    he = he2[ee % 2]
                    he_done = 0
                    for ici in range(8):
                        bg, grdy = gS.take()
                        bu, urdy = uS.take()
                        g16, u16 = gS.w16[bg], uS.w16[bu]
                        lastg = lastu = 0
                        for t in range(2):
                            gi = (ici * 2 + t) % 2
                            for ic in range(IC):
                                w.op("tensor",
                                     lambda e, o=gb[gi][:],
                                     l=g16[:, ic * 128:(ic + 1) * 128],
                                     r=x3fm[:, ic * TOK + t * 512:ic * TOK + (t + 1) * 512],
                                     s=(ic == 0), p=(ic == IC - 1):
                                     e.matmul(o, l, r, start=s, stop=p),
                                     inc=(tT if ic == IC - 1 else None),
                                     waits=[x3_ck[4 * t], x3_ck[4 * t + 1],
                                            x3_ck[4 * t + 2], x3_ck[4 * t + 3],
                                            grdy, gb_free[gi]])
                            gmm = (tT, tT.n)
                            lastg = tT.n
                            for ic in range(IC):
                                w.op("tensor",
                                     lambda e, o=ub[gi][:],
                                     l=u16[:, ic * 128:(ic + 1) * 128],
                                     r=x3fm[:, ic * TOK + t * 512:ic * TOK + (t + 1) * 512],
                                     s=(ic == 0), p=(ic == IC - 1):
                                     e.matmul(o, l, r, start=s, stop=p),
                                     inc=(tT if ic == IC - 1 else None),
                                     waits=[urdy, ub_free[gi]])
                            umm = (tT, tT.n)
                            lastu = tT.n
                            si = gi
                            sv = w.op("scalar",
                                      lambda e, o=silu_f[si][:], i=gb[gi][:]:
                                      e.activation(o, i, AF.Silu, scale=1.0),
                                      inc=tS, waits=[gmm, silu_free[si]])
                            gb_free[gi] = (tS, sv)
                            hv = w.op("vector",
                                      lambda e,
                                      o=he[:, ici * 1024 + t * 512:ici * 1024 + (t + 1) * 512],
                                      a=silu_f[si][:], b=ub[gi][:]:
                                      e.tensor_tensor(o, a, b, ALU.mult),
                                      inc=tV, waits=[(tS, sv), umm, he_free[ee % 2]])
                            silu_free[si] = (tV, hv)
                            ub_free[gi] = (tV, hv)
                            he_done = hv
                        gS.used(bg, (tT, lastg))
                        uS.used(bu, (tT, lastu))
                        if gu_next < E * 8:
                            gS.prefetch(dt["wg_t"][gu_next])
                            uS.prefetch(dt["wu_t"][gu_next])
                            gu_next += 1
                    he_ready = (tV, he_done)
                    if debug and ee == 0:
                        w.op("gpsimd", lambda e, o=dbg["dbg_he"], i=he[:]:
                             e.dma_start(out=o, in_=i), inc=qX, waits=[he_ready])
                    lastd_all = 0
                    for oc in range(4):
                        bd, drdy = dS.take()
                        d16 = dS.w16[bd]
                        lastd = 0
                        for tc in range(NTC):
                            di = tc % 2
                            for ici in range(8):
                                w.op("tensor",
                                     lambda e, o=db[di][:],
                                     l=he[:, ici * 1024 + tc * 128:ici * 1024 + tc * 128 + 128],
                                     r=d16[:, ici * 512:(ici + 1) * 512],
                                     s=(ici == 0), p=(ici == 7):
                                     e.matmul(o, l, r, start=s, stop=p),
                                     inc=(tT if ici == 7 else None),
                                     waits=[he_ready, drdy, db_free[di]])
                            mmv = (tT, tT.n)
                            lastd = tT.n
                            dst = moe[:, tc * D + oc * 512:tc * D + (oc + 1) * 512]
                            if ee == 0:
                                cv = w.op("vector",
                                          lambda e, o=dst, i=db[di][:],
                                          s=comb[:, tc * 8 + ee:tc * 8 + ee + 1]:
                                          e.tensor_scalar(o, i, s, None, ALU.mult),
                                          inc=tV, waits=[mmv, comb_ck[tc]])
                            else:
                                cv = w.op("vector",
                                          lambda e, o=dst, i=db[di][:],
                                          s=comb[:, tc * 8 + ee:tc * 8 + ee + 1]:
                                          e.scalar_tensor_tensor(o, i, s, o,
                                                                 ALU.mult, ALU.add),
                                          inc=tV, waits=[mmv, comb_ready, moe_parts])
                            db_free[di] = (tV, cv)
                        dS.used(bd, (tT, lastd))
                        if d_next < E * 4:
                            dS.prefetch(dt["wd_t"][d_next])
                            d_next += 1
                        lastd_all = lastd
                    moe_parts = (tV, tV.n)
                    he_free[ee % 2] = (tT, lastd_all)

            barrier()
            p2c = contextlib.ExitStack()
            with p2c:
                sbc = mksb(p2c)
                sh2 = [sbc("sh", (128, 32 * 512), BF16)]
                silu_f = [sbc(f"silu2_{i}", (128, 512)) for i in range(2)]
                silu_free = [NONE_T, NONE_T]
                gS = Stream(sbc, "wsg", 2048, 1024, cast_eng="vector")
                uS = Stream(sbc, "wsu", 2048, 1024, cast_eng="scalar")
                dS = Stream(sbc, "wsd", 4096, 2048, cast_eng="scalar")

                gb2 = [pA, pB]
                ub2 = [pC, pD]
                gb2_free = [NONE_T, NONE_T]
                ub2_free = [NONE_T, NONE_T]
                banks4 = [pA, pB, pS, pR]
                psum4_free = [NONE_T] * 4
                moe_tc = [NONE_T] * NTC
                sh_cons = NONE_T
                moe_sh = moe_parts
                gS.prefetch(dt["wsg_t"][0])
                uS.prefetch(dt["wsu_t"][0])
                gS.prefetch(dt["wsg_t"][1])
                uS.prefetch(dt["wsu_t"][1])
                gu_next2 = [2]
                for t in range(2):
                    sh = sh2[0]
                    sh_done = 0
                    for ici in range(32):
                        gi = ici % 2
                        bg, grdy = gS.take()
                        bu, urdy = uS.take()
                        g16, u16 = gS.w16[bg], uS.w16[bu]
                        for ic in range(IC):
                            w.op("tensor",
                                 lambda e, o=gb2[gi][:],
                                 l=g16[:, ic * 128:(ic + 1) * 128],
                                 r=x3fm[:, ic * TOK + t * 512:ic * TOK + (t + 1) * 512],
                                 s=(ic == 0), p=(ic == IC - 1):
                                 e.matmul(o, l, r, start=s, stop=p),
                                 inc=(tT if ic == IC - 1 else None),
                                 waits=[x3_ck[4 * t], x3_ck[4 * t + 1],
                                        x3_ck[4 * t + 2], x3_ck[4 * t + 3],
                                        grdy, gb2_free[gi]])
                        gmm = (tT, tT.n)
                        lastg = tT.n
                        for ic in range(IC):
                            w.op("tensor",
                                 lambda e, o=ub2[gi][:],
                                 l=u16[:, ic * 128:(ic + 1) * 128],
                                 r=x3fm[:, ic * TOK + t * 512:ic * TOK + (t + 1) * 512],
                                 s=(ic == 0), p=(ic == IC - 1):
                                 e.matmul(o, l, r, start=s, stop=p),
                                 inc=(tT if ic == IC - 1 else None),
                                 waits=[urdy, ub2_free[gi]])
                        umm = (tT, tT.n)
                        si = gi
                        sv = w.op("scalar",
                                  lambda e, o=silu_f[si][:], i=gb2[gi][:]:
                                  e.activation(o, i, AF.Silu, scale=1.0),
                                  inc=tS, waits=[gmm, silu_free[si]])
                        gb2_free[gi] = (tS, sv)
                        hv = w.op("vector",
                                  lambda e, o=sh[:, ici * 512:(ici + 1) * 512],
                                  a=silu_f[si][:], b=ub2[gi][:]:
                                  e.tensor_tensor(o, a, b, ALU.mult),
                                  inc=tV, waits=[(tS, sv), umm, sh_cons])
                        silu_free[si] = (tV, hv)
                        ub2_free[gi] = (tV, hv)
                        sh_done = hv
                        gS.used(bg, (tT, lastg))
                        uS.used(bu, (tT, tT.n))
                        nx = gu_next2[0]
                        if nx < 64:
                            gS.prefetch(dt["wsg_t"][nx % 32])
                            uS.prefetch(dt["wsu_t"][nx % 32])
                            gu_next2[0] += 1
                    sh_ready = (tV, sh_done)
                    psum4_free[0] = gb2_free[0]
                    psum4_free[1] = gb2_free[1]
                    for oc in range(4):
                        mm_last = [0, 0, 0, 0]
                        for icg in range(4):
                            bd, drdy = dS.load(dt["wsd_t"][oc * 4 + icg])
                            d16 = dS.w16[bd]
                            for tl in range(4):
                                pb = banks4[tl]
                                for icl in range(8):
                                    ici = icg * 8 + icl
                                    w.op("tensor",
                                         lambda e, o=pb[:],
                                         l=sh[:, ici * 512 + tl * 128:ici * 512 + tl * 128 + 128],
                                         r=d16[:, icl * 512:(icl + 1) * 512],
                                         s=(icg == 0 and icl == 0),
                                         p=(icg == 3 and icl == 7):
                                         e.matmul(o, l, r, start=s, stop=p),
                                         inc=(tT if icl == 7 else None),
                                         waits=[sh_ready, drdy, psum4_free[tl],
                                                gb2_free[tl % 2] if tl < 2 else NONE_T])
                                mm_last[tl] = tT.n
                            dS.used(bd, (tT, max(mm_last)))
                        for tl in range(4):
                            tc = t * 4 + tl
                            dst = moe[:, tc * D + oc * 512:tc * D + (oc + 1) * 512]
                            cv = w.op("vector",
                                      lambda e, o=dst, i=banks4[tl][:],
                                      s=sgv[:, tc:tc + 1]:
                                      e.scalar_tensor_tensor(o, i, s, o,
                                                             ALU.mult, ALU.add),
                                      inc=tV,
                                      waits=[(tT, mm_last[tl]), moe_sh, comb_ready])
                            psum4_free[tl] = (tV, cv)
                            if oc == 3:
                                moe_tc[tc] = (tV, cv)
                    gb2_free[0] = psum4_free[0]
                    gb2_free[1] = psum4_free[1]
                    sh_cons = (tT, tT.n)
                moe_ready = (tV, tV.n)

                if debug:
                    for tc in range(NTC):
                        w.op("gpsimd",
                             lambda e, o=dbg["dbg_acc"][tc], i=moe[:, tc * D:(tc + 1) * D]:
                             e.dma_start(out=o, in_=i),
                             inc=qX, waits=[moe_ready])

                sbd = sbc
                xin3 = [sbd("xin3_0", (128, D))]
                xin3_free = [NONE_T]
                l2c2 = [sbd(f"l2c2_{i}", (128, D), BF16) for i in range(2)]
                l2c2_free = [NONE_T, NONE_T]
                qxin3 = [mksem("qxin3_0", True)]
                ql2c2 = [mksem("ql2c2_0", True), mksem("ql2c2_1", True)]
                qP = [mksem("qP0", True), mksem("qP1", True)]
                t42 = [sbd("t4_0", (128, D))]
                outc = [sbd(f"outc{i}", (128, D)) for i in range(2)]
                outc_free = [NONE_T, NONE_T]
                t4_free = [NONE_T]

                for tc in range(NTC):
                    bi = tc % 2
                    v1 = w.op("gpsimd",
                              lambda e, o=xin3[0][:], i=dt["x_own"][tc]:
                              e.dma_start(out=o, in_=i),
                              inc=qxin3[0], waits=[xin3_free[0]])
                    v2 = w.op("gpsimd",
                              lambda e, o=l2c2[bi][:], i=ln2a_d[tc]:
                              e.dma_start(out=o, in_=i),
                              inc=ql2c2[bi], waits=[l2c2_free[bi]])
                    t4 = t42[0]
                    lv = emit_ln(moe[:, tc * D:(tc + 1) * D], t4, [moe_tc[tc]],
                                 [t4_free[0]], eps_ln, ls=tc % 2)
                    ov = w.op("vector",
                              lambda e, o=outc[bi][:], a=xin3[0][:], b=l2c2[bi][:]:
                              e.tensor_tensor(o, a, b, ALU.add),
                              inc=tV, waits=[(qxin3[0], v1), (ql2c2[bi], v2), outc_free[bi]])
                    ov2 = w.op("vector",
                               lambda e, o=outc[bi][:], b=t4[:]:
                               e.tensor_tensor(o, o, b, ALU.add),
                               inc=tV, waits=[(tV, lv), (tV, ov)])
                    t4_free[0] = (tV, ov2)
                    xin3_free[0] = (tV, ov)
                    l2c2_free[bi] = (tV, ov2)
                    sv = w.op("gpsimd",
                              lambda e, o=out_d[tc], i=outc[bi][:]:
                              e.dma_start(out=o, in_=i),
                              inc=qP[bi], waits=[(tV, ov2)])
                    outc_free[bi] = (qP[bi], sv)

        w.wait("gpsimd", [(s, s.n) for s in ALLSEMS if s.dma])

        with nc.Block() as blk:
            @blk.sync
            def _(eng):
                for f in w.prog["sync"]:
                    f(eng)

            @blk.tensor
            def _(eng):
                for f in w.prog["tensor"]:
                    f(eng)

            @blk.vector
            def _(eng):
                for f in w.prog["vector"]:
                    f(eng)

            @blk.scalar
            def _(eng):
                for f in w.prog["scalar"]:
                    f(eng)

            @blk.gpsimd
            def _(eng):
                for f in w.prog["gpsimd"]:
                    f(eng)

    return nc


def kernel(**inputs):
    in_maps = host_prepare(inputs)
    nc = build_nc()
    res = run_bass_kernel_spmd(nc, in_maps, core_ids=list(range(NCORE)))
    out = np.empty((B, T, D), dtype=np.float32)
    for core in range(NCORE):
        b, half = core // 2, core % 2
        out[b, half * TOK:(half + 1) * TOK] = res.results[core]["out"].reshape(TOK, D)
    return out


# version 25
# speedup vs baseline: 1.1213x; 1.0082x over previous
import contextlib

import numpy as np
import ml_dtypes

import concourse.bass as bass
import concourse.mybir as mybir
from concourse.bass_utils import run_bass_kernel_spmd

F32 = mybir.dt.float32
BF16 = mybir.dt.bfloat16
AF = mybir.ActivationFunctionType
ALU = mybir.AluOpType
AX = mybir.AxisListType
BF = ml_dtypes.bfloat16

B, T, D = 4, 2048, 2048
H, KVH, DH = 16, 4, 128
IC = 16
CH = 512
WT = 1536
WC = 12
QOFF = 1023
NSLOT = 2
HT = 768
HC = 6
E, IEXP, ISH = 8, 1024, 4096
TOK, NTC = 1024, 8
UT, UC = 2048, 16
NCORE = 8
THETA = 10000.0


_PERM = np.empty(128, dtype=np.int64)
_PERM[:64] = np.arange(64) * 2
_PERM[64:] = np.arange(64) * 2 + 1


def _perm_heads(w, nheads):
    return w.reshape(nheads, 128, D)[:, _PERM, :].reshape(nheads * 128, D)


def _tile_rhs_out512(wT):
    nin, nout = wT.shape
    a = wT.reshape(nin // 128, 128, nout // 512, 512)
    return np.ascontiguousarray(
        a.transpose(2, 1, 0, 3).reshape(nout // 512, 128, (nin // 128) * 512))


def _tile_gu(w):
    I_, Din = w.shape
    a = w.reshape(I_ // 128, 128, Din // 128, 128)
    return np.ascontiguousarray(
        a.transpose(0, 3, 2, 1).reshape(I_ // 128, 128, (Din // 128) * 128))


_ROPE_CACHE = {}


def _rope_base():
    if "cs" in _ROPE_CACHE:
        return _ROPE_CACHE["cs"]
    import ctypes
    libm = ctypes.CDLL("libm.so.6")
    libm.cosf.restype = ctypes.c_float
    libm.cosf.argtypes = [ctypes.c_float]
    libm.sinf.restype = ctypes.c_float
    libm.sinf.argtypes = [ctypes.c_float]
    af = (np.arange(0, DH, 2, dtype=np.float32) / np.float32(DH)).astype(np.float32)
    p = np.power(np.float64(THETA), af.astype(np.float64)).astype(np.float32)
    inv = (np.float32(1.0) / p).astype(np.float32)
    ang = (np.arange(T, dtype=np.float32)[:, None] * inv[None, :]).astype(np.float32)
    flat = ang.ravel()
    cos = np.fromiter((libm.cosf(float(v)) for v in flat), np.float32,
                      count=flat.size).reshape(ang.shape)
    sin = np.fromiter((libm.sinf(float(v)) for v in flat), np.float32,
                      count=flat.size).reshape(ang.shape)
    _ROPE_CACHE["cs"] = (cos, sin)
    return cos, sin


def _rope_tables(positions, reps):
    cos_t, sin_t = _rope_base()
    idx = np.clip(positions.astype(np.int64), 0, T - 1)
    return (np.tile(cos_t[idx], (1, reps)), np.tile(sin_t[idx], (1, reps)))


def host_prepare(inputs):
    x = np.asarray(inputs["L0_attn_x"], dtype=np.float32)
    wq = np.asarray(inputs["L0_attn_wq"], dtype=np.float32)
    wk = np.asarray(inputs["L0_attn_wk"], dtype=np.float32)
    wv = np.asarray(inputs["L0_attn_wv"], dtype=np.float32)
    wo = np.asarray(inputs["L0_attn_wo"], dtype=np.float32)
    gw = np.asarray(inputs["L0_ffn_gate_w"], dtype=np.float32)
    egw = np.asarray(inputs["L0_ffn_expert_gate_w"], dtype=np.float32)
    euw = np.asarray(inputs["L0_ffn_expert_up_w"], dtype=np.float32)
    edw = np.asarray(inputs["L0_ffn_expert_down_w"], dtype=np.float32)
    sgw = np.asarray(inputs["L0_ffn_shared_gate_w"], dtype=np.float32)
    suw = np.asarray(inputs["L0_ffn_shared_up_w"], dtype=np.float32)
    sdw = np.asarray(inputs["L0_ffn_shared_down_w"], dtype=np.float32)
    seg = np.asarray(inputs["L0_ffn_shared_expert_gate_w"], dtype=np.float32)

    shared = {}
    shared["wq_t"] = _tile_rhs_out512(np.ascontiguousarray(_perm_heads(wq, H).T))
    shared["wk_t"] = _tile_rhs_out512(np.ascontiguousarray(_perm_heads(wk, KVH).T))
    shared["wv_t"] = _tile_rhs_out512(np.ascontiguousarray(wv.T))
    shared["wo_t"] = _tile_rhs_out512(np.ascontiguousarray(wo.T))
    gp = np.concatenate([gw.T, seg.T], axis=1)
    shared["gate_pack"] = np.ascontiguousarray(
        gp.reshape(IC, 128, 9).transpose(1, 0, 2).reshape(128, IC * 9))
    shared["wg_t"] = np.concatenate([_tile_gu(egw[e]) for e in range(E)])
    shared["wu_t"] = np.concatenate([_tile_gu(euw[e]) for e in range(E)])
    shared["wd_t"] = np.concatenate(
        [_tile_rhs_out512(np.ascontiguousarray(edw[e].T)) for e in range(E)])
    shared["wsg_t"] = _tile_gu(sgw)
    shared["wsu_t"] = _tile_gu(suw)
    wsd = _tile_rhs_out512(np.ascontiguousarray(sdw.T))
    shared["wsd_t"] = np.ascontiguousarray(
        wsd.reshape(4, 128, 4, 8 * 512).transpose(0, 2, 1, 3).reshape(16, 128, 4096))

    shared["idf"] = np.eye(128, dtype=np.float32)
    shared["idb"] = np.eye(128).astype(BF)
    shared["ones_col_b"] = np.ones((128, 1), dtype=BF)
    shared["ones_row_f"] = np.ones((1, 128), dtype=np.float32)
    d_ = np.arange(4)[:, None, None]
    p_ = np.arange(128)[None, :, None]
    i_ = np.arange(512)[None, None, :]
    cmask = np.where(i_ <= d_ * 128 + p_, np.float32(-1e30), np.float32(0.0))
    shared["cmask"] = np.ascontiguousarray(
        cmask.transpose(1, 0, 2).reshape(128, 4 * 512))
    shared["cmask16"] = shared["cmask"].astype(BF)

    in_maps = []
    for core in range(NCORE):
        b, half = core // 2, core % 2
        chunks = (2 * half, 2 * half + 1)
        m = dict(shared)
        ubase = half * 1024 - QOFF
        kpos_u = ubase + np.arange(UT)
        okmask = (kpos_u >= 0) & (kpos_u < T)
        xkv = np.zeros((UT, D), dtype=np.float32)
        xkv[okmask] = x[b, kpos_u[okmask]]
        val = np.where(okmask, np.float32(-12.0), np.float32(-1e9))
        m["x_kv"] = np.ascontiguousarray(xkv.reshape(UC, 128, D))
        m["x_own"] = np.ascontiguousarray(
            x[b, half * TOK:(half + 1) * TOK].reshape(NTC, 128, D))
        m["valid12"] = np.ascontiguousarray(val.reshape(UC, 128).T)
        cosk, sink = _rope_tables(np.maximum(kpos_u, 0).astype(np.float64), KVH)
        m["cosk"] = np.ascontiguousarray(
            cosk.reshape(UC, 128, 256).transpose(1, 0, 2)
            .reshape(128, UC * 256))
        m["sink"] = np.ascontiguousarray(
            sink.reshape(UC, 128, 256).transpose(1, 0, 2)
            .reshape(128, UC * 256))
        qpos = (half * TOK + np.arange(TOK)).astype(np.float64)
        cosq, sinq = _rope_tables(qpos, 4)
        m["cosq"] = np.ascontiguousarray(
            cosq.reshape(NTC, 128, 256).transpose(1, 0, 2).reshape(128, NTC * 256))
        m["sinq"] = np.ascontiguousarray(
            sinq.reshape(NTC, 128, 256).transpose(1, 0, 2).reshape(128, NTC * 256))
        in_maps.append(m)
    return in_maps



class Sem:
    __slots__ = ("h", "n", "dma")

    def __init__(self, h, dma):
        self.h, self.n, self.dma = h, 0, dma


NONE_T = (None, 0)


class Walk:
    ENGINES = ("sync", "tensor", "vector", "scalar", "gpsimd")

    def __init__(self):
        self.prog = {e: [] for e in self.ENGINES}
        self.hw = {e: {} for e in self.ENGINES}

    def _filter(self, eng, waits):
        hw = self.hw[eng]
        ws = []
        for s, v in waits:
            if s is None or v is None or v <= 0:
                continue
            if hw.get(id(s), -1) >= v:
                continue
            hw[id(s)] = v
            ws.append((s.h, v))
        return tuple(ws)

    def op(self, eng, fn, inc=None, waits=()):
        ws = self._filter(eng, waits)
        amt = 16 if (inc is not None and inc.dma) else 1
        if inc is not None:
            inc.n += amt
        semh = inc.h if inc is not None else None

        def thunk(e, fn=fn, ws=ws, semh=semh, amt=amt):
            for (h, v) in ws:
                e.wait_ge(h, v)
            ins = fn(e)
            if semh is not None:
                ins.then_inc(semh, amt)
        self.prog[eng].append(thunk)
        return inc.n if inc is not None else None

    def wait(self, eng, waits):
        ws = self._filter(eng, waits)
        if ws:
            def thunk(e, ws=ws):
                for (h, v) in ws:
                    e.wait_ge(h, v)
            self.prog[eng].append(thunk)


def build_nc(debug=False):
    nc = bass.Bass("TRN2", target_bir_lowering=False)

    dt = {}

    def din(name, shape, dtype=F32):
        dt[name] = nc.dram_tensor(name, list(shape), dtype, kind="ExternalInput").ap()

    din("x_kv", (UC, 128, D))
    din("x_own", (NTC, 128, D))
    din("valid12", (128, UC))
    din("cosk", (128, UC * 256))
    din("sink", (128, UC * 256))
    din("cosq", (128, NTC * 256))
    din("sinq", (128, NTC * 256))
    din("cmask", (128, 4 * 512))
    din("cmask16", (128, 4 * 512), BF16)
    din("idf", (128, 128))
    din("idb", (128, 128), BF16)
    din("ones_col_b", (128, 1), BF16)
    din("ones_row_f", (1, 128))
    din("gate_pack", (128, IC * 9))
    din("wq_t", (4, 128, IC * 512))
    din("wk_t", (1, 128, IC * 512))
    din("wv_t", (1, 128, IC * 512))
    din("wo_t", (4, 128, IC * 512))
    din("wg_t", (E * 8, 128, 2048))
    din("wu_t", (E * 8, 128, 2048))
    din("wd_t", (E * 4, 128, 4096))
    din("wsg_t", (32, 128, 2048))
    din("wsu_t", (32, 128, 2048))
    din("wsd_t", (16, 128, 4096))

    out_d = nc.dram_tensor("out", [NTC, 128, D], F32, kind="ExternalOutput").ap()
    ln2a_d = nc.dram_tensor("ln2a_scr", [NTC, 128, D], BF16, kind="Internal").ap()

    dbg = {}
    if debug:
        for name, shape, dty in [
            ("dbg_q16", (128, H * TOK), BF16),
            ("dbg_k16", (128, KVH * UT), BF16),
            ("dbg_v16", (128, UC * 512), BF16),
            ("dbg_y16", (128, H * TOK), BF16),
            ("dbg_ln2a", (NTC, 128, D), BF16),
            ("dbg_attn", (NTC, 128, D), BF16),
            ("dbg_comb", (128, NTC * 8), F32),
            ("dbg_sg", (128, NTC), F32),
            ("dbg_x3fm", (128, IC * TOK), BF16),
            ("dbg_he", (128, 8 * TOK), BF16),
            ("dbg_acc", (NTC, 128, D), BF16),
        ]:
            dbg[name] = nc.dram_tensor(name, list(shape), dty, kind="ExternalOutput").ap()

    w = Walk()
    es = contextlib.ExitStack()
    with es:
        ALLSEMS = []

        def mksem(name, dma):
            s = Sem(es.enter_context(nc.semaphore(name)), dma)
            ALLSEMS.append(s)
            return s

        qX = mksem("qX", True)
        qT = mksem("qT", True)
        tT = mksem("tT", False)
        tV = mksem("tV", False)
        tS = mksem("tS", False)
        tG = mksem("tG", False)

        def barrier():
            vals = [(s, s.n) for s in ALLSEMS]
            for eng in Walk.ENGINES:
                w.wait(eng, vals)

        _nuid = [0]

        def mksb(stack):
            def f(name, shape, dtype=F32):
                _nuid[0] += 1
                return stack.enter_context(
                    nc.sbuf_tensor(f"{name}_{_nuid[0]}", list(shape), dtype))
            return f

        sb = mksb(es)

        pA = es.enter_context(nc.psum_tensor("pA", [128, 512], F32))
        pB = es.enter_context(nc.psum_tensor("pB", [128, 512], F32))
        pC = es.enter_context(nc.psum_tensor("pC", [128, 512], F32))
        pD = es.enter_context(nc.psum_tensor("pD", [128, 512], F32))
        pS = es.enter_context(nc.psum_tensor("pS", [128, 512], F32))
        pR = es.enter_context(nc.psum_tensor("pR", [128, 512], F32))

        valid_sb = sb("valid_sb", (128, UC))

        idb_sb = sb("idb_sb", (128, 128), BF16)
        onesc_sb = sb("onesc_sb", (128, 1), BF16)
        onesr_sb = sb("onesr_sb", (1, 128))
        eps_ln = sb("eps_ln", (128, 1))
        eps_q = sb("eps_q", (128, 1))
        eps_k = sb("eps_k", (128, 1))
        stats2 = [sb("stats", (128, 24)) for _ in range(2)]
        mv2 = [sb("mv", (128, 2)) for _ in range(2)]
        rstd2 = [sb("rstd", (128, 4)) for _ in range(2)]
        ssq42 = [sb("ssq4", (128, 4)) for _ in range(2)]

        for name, ap_sb in [("valid12", valid_sb),
                            ("idb", idb_sb),
                            ("ones_col_b", onesc_sb), ("ones_row_f", onesr_sb)]:
            w.op("gpsimd", lambda e, o=ap_sb[:], i=dt[name]: e.dma_start(out=o, in_=i),
                 inc=qT)
        tbl = (qT, qT.n)
        w.op("vector", lambda e, o=eps_ln[:]: e.memset(o, 1e-5))
        w.op("vector", lambda e, o=eps_q[:]: e.memset(o, 1.28e-4))
        w.op("vector", lambda e, o=eps_k[:]: e.memset(o, 1e-6), inc=tV)
        eps_tk = (tV, tV.n)

        idb_ap = idb_sb[:]

        class Stream:

            _sid = [0]

            def __init__(self, salloc, name, fd, qd, cast_eng="gpsimd", nbuf=2):
                self.fd, self.qd, self.nq = fd, qd, fd // qd
                self.nbuf = nbuf
                self.stg = [salloc(f"{name}_s{i}", (128, qd)) for i in range(2)]
                self.w16 = [salloc(f"{name}_w{i}", (128, fd), BF16)
                            for i in range(nbuf)]
                Stream._sid[0] += 1
                self.qsem = [mksem(f"q{name}{Stream._sid[0]}_{i}", True)
                             for i in range(2)]
                self.stg_free = [NONE_T, NONE_T]
                self.w16_free = [NONE_T] * nbuf
                self.cnt = 0
                self.eng = cast_eng
                self.sem = {"vector": tV, "scalar": tS, "gpsimd": tG}[cast_eng]
                self.pending = []

            def load(self, dram_ap):
                bi = self.cnt % self.nbuf
                self.cnt += 1
                cv = 0
                for q in range(self.nq):
                    si = (self.cnt * self.nq + q) % 2
                    src = dram_ap[:, q * self.qd:(q + 1) * self.qd]
                    v = w.op("sync",
                             lambda e, o=self.stg[si][:], i=src: e.dma_start(out=o, in_=i),
                             inc=self.qsem[si], waits=[self.stg_free[si]])
                    o_ap = self.w16[bi][:, q * self.qd:(q + 1) * self.qd]
                    s_ap = self.stg[si][:]
                    if self.eng == "vector":
                        fn = lambda e, o=o_ap, s=s_ap: e.tensor_copy(o, s)
                    elif self.eng == "gpsimd":
                        fn = lambda e, o=o_ap, s=s_ap: e.tensor_copy(o, s)
                    else:
                        fn = lambda e, o=o_ap, s=s_ap: e.copy(o, s)
                    cv = w.op(self.eng, fn, inc=self.sem,
                              waits=[(self.qsem[si], v), self.w16_free[bi]])
                    self.stg_free[si] = (self.sem, cv)
                return bi, (self.sem, cv)

            def prefetch(self, dram_ap):
                self.pending.append(self.load(dram_ap))

            def take(self):
                return self.pending.pop(0)

            def used(self, bi, ticket):
                self.w16_free[bi] = ticket

        ln_setbusy = [NONE_T, NONE_T]

        def emit_ln(src, dst, src_ready, dst_free, eps_tile, ls=0):
            stats, mv, rstd = stats2[ls], mv2[ls], rstd2[ls]
            for s4 in range(4):
                w.op("vector",
                     lambda e, o=stats[:, s4 * 6:(s4 + 1) * 6],
                     i=src[:, s4 * 512:(s4 + 1) * 512]: e.bn_stats(o, i),
                     inc=(tV if s4 == 3 else None),
                     waits=list(src_ready) + ([ln_setbusy[ls]] if s4 == 0 else []))
            w.op("vector", lambda e, o=mv[:], i=stats[:]: e.bn_aggr(o, i),
                 inc=tV, waits=[(tV, tV.n), (tS, tS.n)])
            va = tV.n
            w.op("scalar",
                 lambda e, o=rstd[:, 0:1], i=mv[:, 1:2], b=eps_tile[:]:
                 e.activation(o, i, AF.Ln, bias=b, scale=1.0),
                 inc=tS, waits=[(tV, va), eps_tk])
            w.op("scalar",
                 lambda e, o=rstd[:, 0:1]: e.activation(o, o, AF.Exp, scale=-0.5),
                 inc=tS, waits=[(tS, tS.n)])
            vs = tS.n
            cv = w.op("vector",
                      lambda e, o=dst[:], i=src[:], m=mv[:, 0:1], r=rstd[:, 0:1]:
                      e.tensor_scalar(o, i, m, r, ALU.subtract, ALU.mult),
                      inc=tV, waits=[(tS, vs)] + list(dst_free))
            ln_setbusy[ls] = (tV, cv)
            return cv

        def emit_normrope(psum, qsc, qsq, qrot, cos_ap, sin_ap, eps_tile, scale,
                          mm_ticket, rot_cons, ls=0):
            rstd, ssq4 = rstd2[ls], ssq42[ls]
            cv = w.op("vector", lambda e, o=qsc[:], i=psum[:]: e.tensor_copy(o, i),
                      inc=tV, waits=[mm_ticket])
            psum_free = (tV, cv)
            w.op("vector", lambda e, o=qsq[:], i=qsc[:]: e.tensor_tensor(o, i, i, ALU.mult),
                 inc=tV, waits=[(tV, cv)])
            w.op("vector",
                 lambda e, o=ssq4[:], i=qsq[:].rearrange("p (h d) -> p h d", h=4):
                 e.tensor_reduce(o, i, axis=AX.X, op=ALU.add),
                 inc=tV, waits=[(tV, tV.n), (tS, tS.n)])
            va = tV.n
            w.op("scalar",
                 lambda e, o=rstd[:], i=ssq4[:], b=eps_tile[:], s=scale:
                 e.activation(o, i, AF.Ln, bias=b, scale=s),
                 inc=tS, waits=[(tV, va), eps_tk])
            w.op("scalar",
                 lambda e, o=rstd[:]: e.activation(o, o, AF.Exp, scale=-0.5),
                 inc=tS, waits=[(tS, tS.n)])
            vs = tS.n
            for hh in range(4):
                w.op("vector",
                     lambda e, o=qsc[:, hh * 128:(hh + 1) * 128], r=rstd[:, hh:hh + 1]:
                     e.tensor_scalar(o, o, r, None, ALU.mult),
                     inc=(tV if hh == 3 else None), waits=[(tS, vs)])
            sv = tV.n
            q3 = qsc[:].rearrange("p (h d) -> p h d", h=4)
            r3 = qrot[:].rearrange("p (h d) -> p h d", h=4)
            s3 = qsq[:].rearrange("p (h d) -> p h d", h=4)
            c3 = cos_ap.rearrange("p (h d) -> p h d", h=4)
            n3 = sin_ap.rearrange("p (h d) -> p h d", h=4)
            w.op("vector",
                 lambda e, o=r3[:, :, 0:64], a=q3[:, :, 0:64], b=c3:
                 e.tensor_tensor(o, a, b, ALU.mult),
                 waits=[(tV, sv), tbl] + list(rot_cons))
            w.op("vector",
                 lambda e, o=s3[:, :, 0:64], a=q3[:, :, 64:128], b=n3:
                 e.tensor_tensor(o, a, b, ALU.mult), inc=tV)
            w.op("vector",
                 lambda e, o=r3[:, :, 0:64], b=s3[:, :, 0:64]:
                 e.tensor_tensor(o, o, b, ALU.subtract),
                 inc=tV, waits=[(tV, tV.n)])
            w.op("vector",
                 lambda e, o=r3[:, :, 64:128], a=q3[:, :, 0:64], b=n3:
                 e.tensor_tensor(o, a, b, ALU.mult))
            w.op("vector",
                 lambda e, o=s3[:, :, 64:128], a=q3[:, :, 64:128], b=c3:
                 e.tensor_tensor(o, a, b, ALU.mult), inc=tV)
            w.op("vector",
                 lambda e, o=r3[:, :, 64:128], b=s3[:, :, 64:128]:
                 e.tensor_tensor(o, o, b, ALU.add),
                 inc=tV, waits=[(tV, tV.n)])
            return psum_free, tV.n

        ln2_stores = 0
        p1 = contextlib.ExitStack()
        with p1:
            sb1 = mksb(p1)
            q16 = sb1("q16", (128, H * TOK), BF16)
            k16 = sb1("k16", (128, KVH * UT), BF16)
            v16 = sb1("v16", (128, UC * 512), BF16)
            pab = contextlib.ExitStack()
            with pab:
                sb2 = mksb(pab)
                cosk_sl = sb2("cosk_sl", (128, UC * 256), BF16)
                sink_sl = sb2("sink_sl", (128, UC * 256), BF16)
                cosq_sb = sb2("cosq_sb", (128, NTC * 256), BF16)
                sinq_sb = sb2("sinq_sb", (128, NTC * 256), BF16)
                qTa = mksem("qTa", True)
                with nc.sbuf_tensor("ropestg", [128, UC * 256], F32) as ropestg:
                    stg_ap = ropestg[:]
                    prev = NONE_T
                    for nm_, dst_ in (("cosk", cosk_sl), ("sink", sink_sl),
                                      ("cosq", cosq_sb), ("sinq", sinq_sb)):
                        fdim = dst_[:].tensor.shape[1]
                        v_ = w.op("gpsimd",
                                  lambda e, o=ropestg[0:128, 0:fdim], i=dt[nm_]:
                                  e.dma_start(out=o, in_=i),
                                  inc=qTa, waits=[prev])
                        cv_ = w.op("vector",
                                   lambda e, o=dst_[:], i=ropestg[0:128, 0:fdim]:
                                   e.tensor_copy(o, i),
                                   inc=tV, waits=[(qTa, v_)])
                        prev = (tV, cv_)
                cs_ld = prev
                w.wait("gpsimd", [prev])
                xin = [sb2(f"xin{i}", (128, D)) for i in range(2)]
                xin_free = [NONE_T] * 2
                qxin = [mksem("qxin0", True), mksem("qxin1", True)]
                x1c = [sb2(f"x1c{i}", (128, D), BF16) for i in range(2)]
                x1c_free = [NONE_T] * 2
                x1fm = sb2("x1fm", (128, IC * UT), BF16)
                qscb = [sb2("qsc", (128, 512), BF16) for _ in range(2)]
                qsqb = [sb2("qsq", (128, 512), BF16) for _ in range(2)]
                qrotb = [sb2("qrot", (128, 512), BF16) for _ in range(2)]
                wstr = Stream(sb2, "wA", 4096, 1024, cast_eng="gpsimd", nbuf=2)
                pTb = [pab.enter_context(nc.psum_tensor("pT_0", [128, 1024], BF16)),
                       pab.enter_context(nc.psum_tensor("pT_1", [128, 1024], BF16))]

                x1_r = x1fm[:].rearrange("p (ic t) -> p ic t", ic=IC)
                pT_r = [pTb[0][:].rearrange("p (ii t) -> p ii t", ii=8),
                        pTb[1][:].rearrange("p (ii t) -> p ii t", ii=8)]
                tr_free = [NONE_T, NONE_T]
                nr_state = [dict(rot_cons=NONE_T) for _ in range(2)]
                bank_free2 = [NONE_T, NONE_T]
                unit_ctr = [0]
                x1_ck = [NONE_T] * UC
                tr_pending = []

                def flush_tr(keep=1):
                    while len(tr_pending) > keep:
                        tr_pending.pop(0)()

                def qkv_unit(bank, mmv, cos_ap, sin_ap, eps_tile, scale,
                             dst_sl, bi_bank):
                    u = unit_ctr[0] % 2
                    unit_ctr[0] += 1
                    st = nr_state[u]
                    pf, nv = emit_normrope(
                        bank, qscb[u], qsqb[u], qrotb[u], cos_ap, sin_ap,
                        eps_tile, scale, mmv, [st["rot_cons"]], ls=u)
                    bank_free2[bi_bank] = pf

                    def part2(u=u, nv=nv, dst_sl=dst_sl, st=st):
                        for hh in range(4):
                            w.op("tensor",
                                 lambda e, o=pTb[u][:, hh * 128:(hh + 1) * 128],
                                 i=qrotb[u][:, hh * 128:(hh + 1) * 128]:
                                 e.transpose(o, i, idb_ap),
                                 inc=(tT if hh == 3 else None),
                                 waits=[(tV, nv), tr_free[u]])
                        tt = (tT, tT.n)
                        st["rot_cons"] = tt
                        cv = w.op("vector",
                                  lambda e, o=dst_sl, i=pT_r[u][:, 0:4, :]:
                                  e.tensor_copy(o, i),
                                  inc=tV, waits=[tt])
                        tr_free[u] = (tV, cv)
                    tr_pending.append(part2)

                def emit_ln1_chunk(tc):
                    bi = tc % 2
                    v = w.op("gpsimd",
                             lambda e, o=xin[bi][:], i=dt["x_kv"][tc]:
                             e.dma_start(out=o, in_=i),
                             inc=qxin[bi], waits=[xin_free[bi]])
                    cv = emit_ln(xin[bi], x1c[bi], [(qxin[bi], v)],
                                 [x1c_free[bi]], eps_ln, ls=bi)
                    xin_free[bi] = (tV, cv)
                    for g in range(2):
                        for ii in range(8):
                            ic = g * 8 + ii
                            w.op("tensor",
                                 lambda e, o=pTb[g][:, ii * 128:(ii + 1) * 128],
                                 i=x1c[bi][:, ic * 128:(ic + 1) * 128]:
                                 e.transpose(o, i, idb_ap),
                                 inc=(tT if ii == 7 else None),
                                 waits=[(tV, cv), tr_free[g], tbl])
                        tt = tT.n
                        cpv = w.op("vector",
                                   lambda e,
                                   o=x1_r[:, g * 8:(g + 1) * 8, tc * 128:tc * 128 + 128],
                                   i=pT_r[g]:
                                   e.tensor_copy(o, i),
                                   inc=tV, waits=[(tT, tt)])
                        tr_free[g] = (tV, cpv)
                    x1c_free[bi] = tr_free[1]
                    x1_ck[tc] = tr_free[1]

                def slab_ap(mat, oc, hs):
                    return dt[mat + "_t"][oc][:, hs * 4096:(hs + 1) * 4096]

                def emit_qkv_tiles(mat, oc, wA, wB, rdyA, rdyB, tiles):
                    last_mm = 0
                    for tc in tiles:
                        bank = [pA, pB][tc % 2]
                        if mat == "wq":
                            loff = QOFF + tc * 128
                            xr = [x1_ck[7 + tc], x1_ck[8 + tc if 8 + tc < UC else UC - 1]]
                        else:
                            loff = tc * 128
                            xr = [x1_ck[tc]]
                        for ic in range(IC):
                            ws16 = wA if ic < 8 else wB
                            w.op("tensor",
                                 lambda e, o=bank[:],
                                 l=x1fm[:, ic * UT + loff:ic * UT + loff + 128],
                                 r=ws16[:, (ic % 8) * 512:(ic % 8 + 1) * 512],
                                 s=(ic == 0), p=(ic == IC - 1):
                                 e.matmul(o, l, r, start=s, stop=p),
                                 inc=(tT if ic == IC - 1 else None),
                                 waits=xr + [rdyA, rdyB, bank_free2[tc % 2]])
                        mmv = (tT, tT.n)
                        last_mm = tT.n
                        if mat == "wv":
                            cv = w.op("vector",
                                      lambda e,
                                      o=v16[:, tc * 512:(tc + 1) * 512],
                                      i=bank[:]: e.tensor_copy(o, i),
                                      inc=tV, waits=[mmv])
                            bank_free2[tc % 2] = (tV, cv)
                        elif mat == "wk":
                            w.wait("vector", [cs_ld])
                            k_r = k16[:].rearrange("p (h t) -> p h t", h=KVH)
                            qkv_unit(bank, mmv,
                                     cosk_sl[:, tc * 256:(tc + 1) * 256],
                                     sink_sl[:, tc * 256:(tc + 1) * 256],
                                     eps_k, 1.0 / 128.0,
                                     k_r[:, :, tc * 128:tc * 128 + 128],
                                     tc % 2)
                        else:
                            q_r = q16[:].rearrange("p (h t) -> p h t", h=H)
                            qkv_unit(bank, mmv,
                                     cosq_sb[:, tc * 256:(tc + 1) * 256],
                                     sinq_sb[:, tc * 256:(tc + 1) * 256],
                                     eps_q, 1.0,
                                     q_r[:, oc * 4:(oc + 1) * 4, tc * 128:tc * 128 + 128],
                                     tc % 2)
                        flush_tr(keep=1)
                    return last_mm

                wstr.prefetch(slab_ap("wk", 0, 0))
                wstr.prefetch(slab_ap("wk", 0, 1))
                emit_ln1_chunk(0)
                biA, rdyA = wstr.take()
                biB, rdyB = wstr.take()
                wkA, wkB = wstr.w16[biA], wstr.w16[biB]
                lm = 0
                for tc in range(1, UC):
                    emit_ln1_chunk(tc)
                    lm = emit_qkv_tiles("wk", 0, wkA, wkB, rdyA, rdyB, [tc - 1])
                lm = emit_qkv_tiles("wk", 0, wkA, wkB, rdyA, rdyB, [UC - 1])
                wstr.used(biA, (tT, lm))
                wstr.used(biB, (tT, lm))
                rest = [("wv", 0)] + [("wq", oc) for oc in range(4)]
                wstr.prefetch(slab_ap("wv", 0, 0))
                wstr.prefetch(slab_ap("wv", 0, 1))
                for si_, (mat, oc) in enumerate(rest):
                    biA, rdyA = wstr.take()
                    biB, rdyB = wstr.take()
                    wA, wB = wstr.w16[biA], wstr.w16[biB]
                    tiles = range(8) if mat == "wq" else range(UC)
                    lm = emit_qkv_tiles(mat, oc, wA, wB, rdyA, rdyB, tiles)
                    wstr.used(biA, (tT, lm))
                    wstr.used(biB, (tT, lm))
                    if si_ + 1 < len(rest):
                        m2_, o2_ = rest[si_ + 1]
                        wstr.prefetch(slab_ap(m2_, o2_, 0))
                        wstr.prefetch(slab_ap(m2_, o2_, 1))
                flush_tr(keep=0)
            qkv_ready = (tV, tV.n)
            barrier()

            if debug:
                for nm, src_ in [("dbg_q16", q16), ("dbg_k16", k16), ("dbg_v16", v16)]:
                    w.op("gpsimd", lambda e, o=dbg[nm], i=src_[:]:
                         e.dma_start(out=o, in_=i), inc=qX, waits=[qkv_ready])

            pcd = contextlib.ExitStack()
            with pcd:
                sb3 = mksb(pcd)
                exp16 = [sb3(f"exp16_{i}", (128, 512), BF16) for i in range(6)]
                sacc = [sb3(f"sacc{i}", (128, 512)) for i in range(4)]
                onescf = sb3("onescf", (128, 1))
                w.op("vector", lambda e, o=onescf[:]: e.memset(o, 1.0), inc=tV)
                rcp = [sb3(f"rcp{i}", (1, 512)) for i in range(2)]
                bcf = [sb3(f"bcf{i}", (128, 512)) for i in range(2)]
                y16 = sb3("y16", (128, H * TOK), BF16)
                attn16 = [sb3(f"attn16_{i}", (128, D), BF16) for i in range(8)]
                ln2c = [sb3(f"ln2c{i}", (128, D), BF16) for i in range(2)]
                ln2c_free = [NONE_T, NONE_T]
                qL = [mksem("qL0", True), mksem("qL1", True)]
                wstr2 = Stream(sb3, "wB", 4096, 2048, cast_eng="vector", nbuf=2)
                pSu = [pcd.enter_context(nc.psum_tensor(f"pSu{i}", [128, 512], F32))
                       for i in range(2)]
                cmask16_sb = sb3("cmask16_sb", (128, 4 * 512), BF16)
                qTb = mksem("qTb", True)
                w.op("gpsimd", lambda e, o=cmask16_sb[:], i=dt["cmask16"]:
                     e.dma_start(out=o, in_=i), inc=qTb)
                cm_ld = (qTb, qTb.n)
                w.wait("tensor", [cm_ld])

                scb = [[pA, pB], [pC, pD]]
                avb = [pS, pR]
                av_free = [NONE_T, NONE_T]
                sc_free = [[NONE_T, NONE_T], [NONE_T, NONE_T]]
                exp_freeT = [NONE_T] * 6
                exp_freeV = [NONE_T] * 6
                sum_free = [NONE_T, NONE_T]
                sacc_ck = [NONE_T] * 4
                sacc_cons = [NONE_T, NONE_T]
                pending = [None, None]

                def emit_tail(u):
                    h, j, rcpv = pending[u]
                    pending[u] = None
                    sbank = scb[u][1]
                    w.op("tensor",
                         lambda e, o=sbank[:], l=onesr_sb[:], r=rcp[u][:]:
                         e.matmul(o, l, r, start=True, stop=True),
                         inc=tT, waits=[rcpv, sc_free[u][1]])
                    bct = (tT, tT.n)
                    cv2 = w.op("vector",
                               lambda e, o=bcf[u][:], i=sbank[:]:
                               e.tensor_copy(o, i),
                               inc=tV, waits=[bct])
                    sc_free[u][1] = (tV, cv2)
                    yv = w.op("vector",
                              lambda e, o=y16[:, h * TOK + j * CH:h * TOK + (j + 1) * CH],
                              b2=bcf[u][:]:
                              e.tensor_tensor(o, o, b2, ALU.mult),
                              inc=tV, waits=[(tV, cv2)])
                    return yv

                for j in range(NSLOT):
                    for hp in range(H // 2):
                        exvq = {}
                        for kc in range(WC + 1):
                            if kc < WC:
                                uck = 4 * j + kc
                                for u in range(2):
                                    h = 2 * hp + u
                                    kvh = h // 4
                                    ei = u * 3 + kc % 3
                                    bank = scb[u][kc % 2]
                                    w.op("tensor",
                                         lambda e, o=bank[:],
                                         l=k16[:, kvh * UT + uck * 128:kvh * UT + uck * 128 + 128],
                                         r=q16[:, h * TOK + j * CH:h * TOK + (j + 1) * CH],
                                         p=(kc < 8):
                                         e.matmul(o, l, r, start=True, stop=p),
                                         inc=(tT if kc < 8 else None),
                                         waits=[qkv_ready, sc_free[u][kc % 2], tbl])
                                    if kc >= 8:
                                        w.op("tensor",
                                             lambda e, o=bank[:], l=idb_ap,
                                             r=cmask16_sb[:, (kc - 8) * 512:(kc - 7) * 512]:
                                             e.matmul(o, l, r, start=False, stop=True),
                                             inc=tT)
                                    mmv = (tT, tT.n)
                                    ev = w.op("scalar",
                                              lambda e, o=exp16[ei][:], i=bank[:],
                                              b=valid_sb[:, uck:uck + 1]:
                                              e.activation(o, i, AF.Exp, bias=b, scale=1.0),
                                              inc=tS, waits=[mmv, exp_freeT[ei],
                                                             exp_freeV[ei], tbl])
                                    exvq[(u, kc)] = (tS, ev)
                                    sc_free[u][kc % 2] = (tS, ev)
                            if kc == 2 and pending[0] is not None:
                                emit_tail(0)
                                emit_tail(1)
                            if kc >= 1:
                                kd = kc - 1
                                uckd = 4 * j + kd
                                for u in range(2):
                                    h = 2 * hp + u
                                    kvh = h // 4
                                    ei = u * 3 + kd % 3
                                    exw = exvq.pop((u, kd))
                                    w.op("tensor",
                                         lambda e, o=avb[u][:],
                                         l=v16[:, uckd * 512 + kvh * 128:uckd * 512 + kvh * 128 + 128],
                                         r=exp16[ei][:],
                                         s=(kd == 0), p=(kd == WC - 1):
                                         e.matmul(o, l, r, start=s, stop=p),
                                         inc=tT,
                                         waits=[exw] + ([av_free[u]] if kd == 0 else []))
                                    avt = (tT, tT.n)
                                    par = kd % 2
                                    ai = u * 2 + par
                                    if kd < 2:
                                        sv_ = w.op("vector",
                                                   lambda e, o=sacc[ai][:], i=exp16[ei][:]:
                                                   e.tensor_copy(o, i),
                                                   inc=tV,
                                                   waits=[exw, sacc_cons[u]])
                                    else:
                                        sv_ = w.op("vector",
                                                   lambda e, o=sacc[ai][:], a=sacc[ai][:],
                                                   i=exp16[ei][:]:
                                                   e.tensor_tensor(o, a, i, ALU.add),
                                                   inc=tV,
                                                   waits=[exw, sacc_ck[ai]])
                                    sacc_ck[ai] = (tV, sv_)
                                    exp_freeT[ei] = avt
                                    exp_freeV[ei] = (tV, sv_)
                        av_done = (tT, tT.n)
                        for u in range(2):
                            h = 2 * hp + u
                            cv = w.op("vector",
                                      lambda e, o=y16[:, h * TOK + j * CH:h * TOK + (j + 1) * CH],
                                      i=avb[u][:]: e.tensor_copy(o, i),
                                      inc=tV, waits=[av_done])
                            av_free[u] = (tV, cv)
                            mg = w.op("vector",
                                      lambda e, o=sacc[u * 2][:], a=sacc[u * 2][:],
                                      b=sacc[u * 2 + 1][:]:
                                      e.tensor_tensor(o, a, b, ALU.add),
                                      inc=tV,
                                      waits=[sacc_ck[u * 2], sacc_ck[u * 2 + 1]])
                            w.op("tensor",
                                 lambda e, o=pSu[u][0:1, :], l=onescf[:],
                                 r=sacc[u * 2][:]:
                                 e.matmul(o, l, r, start=True, stop=True),
                                 inc=tT, waits=[(tV, mg), sum_free[u]])
                            smm2 = (tT, tT.n)
                            sacc_cons[u] = smm2
                            lv = w.op("scalar",
                                      lambda e, o=rcp[u][:], i=pSu[u][0:1, :]:
                                      e.activation(o, i, AF.Ln, scale=1.0),
                                      inc=tS, waits=[smm2])
                            sum_free[u] = (tS, lv)
                            rv = w.op("scalar",
                                      lambda e, o=rcp[u][:]:
                                      e.activation(o, o, AF.Exp, scale=-1.0),
                                      inc=tS, waits=[(tS, lv)])
                            pending[u] = (h, j, (tS, rv))
                emit_tail(0)
                emit_tail(1)
                y_ready = (tV, tV.n)

                if debug:
                    w.op("gpsimd", lambda e, o=dbg["dbg_y16"], i=y16[:]:
                         e.dma_start(out=o, in_=i), inc=qX, waits=[y_ready])

                attn_parts = [0] * 8
                ob_free = [NONE_T, NONE_T]
                wstr2.prefetch(dt["wo_t"][0][:, 0:4096])
                wstr2.prefetch(dt["wo_t"][0][:, 4096:8192])
                for oc in range(4):
                    biA2, rdyA2 = wstr2.take()
                    biB2, rdyB2 = wstr2.take()
                    woA, woB = wstr2.w16[biA2], wstr2.w16[biB2]
                    last_mm = 0
                    for tq in range(8):
                        bank = [pA, pB][tq % 2]
                        for ic in range(IC):
                            ws16 = woA if ic < 8 else woB
                            w.op("tensor",
                                 lambda e, o=bank[:],
                                 l=y16[:, ic * TOK + tq * 128:ic * TOK + tq * 128 + 128],
                                 r=ws16[:, (ic % 8) * 512:(ic % 8 + 1) * 512],
                                 s=(ic == 0), p=(ic == IC - 1):
                                 e.matmul(o, l, r, start=s, stop=p),
                                 inc=(tT if ic == IC - 1 else None),
                                 waits=[y_ready, rdyA2, rdyB2, ob_free[tq % 2]])
                        mmv = (tT, tT.n)
                        last_mm = tT.n
                        cv = w.op("vector",
                                  lambda e, o=attn16[tq][:, oc * 512:(oc + 1) * 512],
                                  i=bank[:]: e.tensor_copy(o, i),
                                  inc=tV, waits=[mmv])
                        ob_free[tq % 2] = (tV, cv)
                        attn_parts[tq] = cv
                    wstr2.used(biA2, (tT, last_mm))
                    wstr2.used(biB2, (tT, last_mm))
                    if oc + 1 < 4:
                        wstr2.prefetch(dt["wo_t"][oc + 1][:, 0:4096])
                        wstr2.prefetch(dt["wo_t"][oc + 1][:, 4096:8192])
                for tq in range(8):
                    bi = tq % 2
                    if debug:
                        w.op("gpsimd",
                             lambda e, o=dbg["dbg_attn"][tq], i=attn16[tq][:]:
                             e.dma_start(out=o, in_=i),
                             inc=qX, waits=[(tV, attn_parts[tq])])
                    cv = emit_ln(attn16[tq], ln2c[bi], [(tV, attn_parts[tq])],
                                 [ln2c_free[bi]], eps_ln, ls=bi)
                    sv = w.op("gpsimd",
                              lambda e, o=ln2a_d[tq], i=ln2c[bi][:]:
                              e.dma_start(out=o, in_=i),
                              inc=qL[bi], waits=[(tV, cv)])
                    ln2c_free[bi] = (qL[bi], sv)
                    ln2_stores += 1
                    if debug:
                        w.op("gpsimd",
                             lambda e, o=dbg["dbg_ln2a"][tq], i=ln2c[bi][:]:
                             e.dma_start(out=o, in_=i), inc=qX)
        ln2_done2 = [(qL[0], qL[0].n), (qL[1], qL[1].n)]

        p2 = contextlib.ExitStack()
        with p2:
            sbf = mksb(p2)
            idf_sb = sbf("idf_sb", (128, 128))
            gate_sb = sbf("gate_sb", (128, IC * 9))
            qTc = mksem("qTc", True)
            w.op("gpsimd", lambda e, o=idf_sb[:], i=dt["idf"]:
                 e.dma_start(out=o, in_=i), inc=qTc)
            w.op("gpsimd", lambda e, o=gate_sb[:], i=dt["gate_pack"]:
                 e.dma_start(out=o, in_=i), inc=qTc)
            p2tbl = (qTc, qTc.n)
            w.wait("tensor", [p2tbl])
            w.wait("vector", [p2tbl])
            idf_ap = idf_sb[:]
            x3fm = sbf("x3fm", (128, IC * TOK), BF16)
            moe = sbf("moe", (128, NTC * D), BF16)
            comb = sbf("comb", (128, NTC * 8))
            sgv = sbf("sgv", (128, NTC))
            x3_r = x3fm[:].rearrange("p (ic t) -> p ic t", ic=IC)

            barrier()
            p2a = contextlib.ExitStack()
            with p2a:
                sba = mksb(p2a)
                xin2 = [sba(f"xin2_{i}", (128, D)) for i in range(2)]
                xin2_free = [NONE_T, NONE_T]
                l2c = [sba(f"l2c{i}", (128, D), BF16) for i in range(2)]
                l2c_free = [NONE_T, NONE_T]
                qxin2 = [mksem("qxin2_0", True), mksem("qxin2_1", True)]
                ql2c = [mksem("ql2c_0", True), mksem("ql2c_1", True)]
                x2c2 = [sba("x2c", (128, D)) for _ in range(2)]
                x3c2 = [sba("x3c", (128, D)) for _ in range(2)]
                xf322 = [sba("xf32", (128, D)) for _ in range(2)]
                r9 = sba("r9", (9, 128))
                pr9a = sba("pr9a", (128, 9 * NTC))
                sgt = sba("sgt", (128, 1))
                prb = sba("prb", (128, 8))
                m1 = sba("m1", (128, 1))
                m2 = sba("m2", (128, 1))
                sm = sba("sm", (128, 1))
                eqb = sba("eqb", (128, 8))

                x2c_free = [NONE_T, NONE_T]
                x3c_free = [NONE_T, NONE_T]
                xf32_free = [NONE_T, NONE_T]
                x3_ck = [NONE_T] * NTC
                comb_ck = [NONE_T] * NTC
                pR_free = NONE_T
                pDt_free = NONE_T
                r9_cons = NONE_T
                cvps = [NONE_T] * NTC
                trb_free = [NONE_T, NONE_T]
                for tc in range(NTC):
                    bi = tc % 2
                    x2c, x3c, xf32 = x2c2[bi], x3c2[bi], xf322[bi]
                    v1 = w.op("gpsimd",
                              lambda e, o=xin2[bi][:], i=dt["x_own"][tc]:
                              e.dma_start(out=o, in_=i),
                              inc=qxin2[bi], waits=[xin2_free[bi]] + ln2_done2)
                    v2 = w.op("gpsimd",
                              lambda e, o=l2c[bi][:], i=ln2a_d[tc]:
                              e.dma_start(out=o, in_=i),
                              inc=ql2c[bi], waits=[l2c_free[bi]])
                    cv = w.op("vector",
                              lambda e, o=x2c[:], a=xin2[bi][:], b=l2c[bi][:]:
                              e.tensor_tensor(o, a, b, ALU.add),
                              inc=tV, waits=[(qxin2[bi], v1), (ql2c[bi], v2), x2c_free[bi]])
                    xin2_free[bi] = (tV, cv)
                    l2c_free[bi] = (tV, cv)
                    lv = emit_ln(x2c, x3c, [(tV, cv)], [x3c_free[bi]], eps_ln, ls=bi)
                    x2c_free[bi] = (tV, lv)
                    grp_cp = 0
                    for g in range(4):
                        bank = [pA, pB][g % 2]
                        for ii in range(4):
                            ic = g * 4 + ii
                            w.op("tensor",
                                 lambda e, o=bank[:, ii * 128:(ii + 1) * 128],
                                 i=x3c[:, ic * 128:(ic + 1) * 128]:
                                 e.transpose(o, i, idf_ap),
                                 inc=(tT if ii == 3 else None),
                                 waits=[(tV, lv), trb_free[g % 2], tbl])
                        tt = (tT, tT.n)
                        w.op("vector",
                             lambda e,
                             o=x3_r[:, g * 4:(g + 1) * 4, tc * 128:tc * 128 + 128],
                             i=bank[:].rearrange("p (ii t) -> p ii t", ii=4):
                             e.tensor_copy(o, i),
                             waits=[tt])
                        cv2 = w.op("vector",
                                   lambda e, o=xf32[:, g * 512:(g + 1) * 512], i=bank[:]:
                                   e.tensor_copy(o, i),
                                   inc=tV, waits=[tt, xf32_free[bi]])
                        trb_free[g % 2] = (tV, cv2)
                        grp_cp = cv2
                    x3c_free[bi] = (tV, grp_cp)
                    x3_ck[tc] = (tV, grp_cp)
                    xf_rdy = (tV, grp_cp)
                    for ic in range(IC):
                        w.op("tensor",
                             lambda e, o=pR[0:9, 0:128],
                             l=gate_sb[:, ic * 9:(ic + 1) * 9],
                             r=xf32[:, ic * 128:(ic + 1) * 128],
                             s=(ic == 0), p=(ic == IC - 1):
                             e.matmul(o, l, r, start=s, stop=p),
                             inc=(tT if ic == IC - 1 else None),
                             waits=[xf_rdy, tbl, pR_free])
                    rmm = (tT, tT.n)
                    xf32_free[bi] = rmm
                    cvr = w.op("vector", lambda e, o=r9[:], i=pR[0:9, 0:128]:
                               e.tensor_copy(o, i),
                               inc=tV, waits=[rmm, r9_cons])
                    pR_free = (tV, cvr)
                    w.op("tensor",
                         lambda e, o=pD[0:128, 0:9], i=r9[:], d9=idf_sb[0:9, 0:9]:
                         e.transpose(o, i, d9),
                         inc=tT, waits=[(tV, cvr), pDt_free])
                    r9_cons = (tT, tT.n)
                    cvp = w.op("vector",
                               lambda e, o=pr9a[:, tc * 9:(tc + 1) * 9],
                               i=pD[0:128, 0:9]:
                               e.tensor_copy(o, i),
                               inc=tV, waits=[r9_cons])
                    pDt_free = (tV, cvp)
                    cvps[tc] = cvp
                for tc in range(NTC):
                    w.op("vector", lambda e, o=m1[:], i=pr9a[:, tc * 9:tc * 9 + 8]:
                         e.reduce_max(o, i, axis=AX.X),
                         inc=tV, waits=[(tV, cvps[tc])])
                    w.op("vector", lambda e, o=prb[:],
                         i=pr9a[:, tc * 9:tc * 9 + 8], s=m1[:]:
                         e.tensor_scalar(o, i, s, None, ALU.subtract),
                         inc=tV, waits=[(tV, tV.n)])
                    av_ = tV.n
                    w.op("scalar", lambda e, o=prb[:]:
                         e.activation(o, o, AF.Exp, scale=1.0),
                         inc=tS, waits=[(tV, av_)])
                    ev = (tS, tS.n)
                    w.op("vector", lambda e, o=sm[:], i=prb[:]:
                         e.reduce_sum(o, i, axis=AX.X),
                         inc=tV, waits=[ev])
                    w.op("vector", lambda e, o=sm[:]: e.reciprocal(o, o),
                         inc=tV, waits=[(tV, tV.n)])
                    w.op("vector", lambda e, o=prb[:], s=sm[:]:
                         e.tensor_scalar(o, o, s, None, ALU.mult),
                         inc=tV, waits=[(tV, tV.n)])
                    w.op("vector", lambda e, o=m1[:], i=prb[:]:
                         e.reduce_max(o, i, axis=AX.X),
                         inc=tV, waits=[(tV, tV.n)])
                    w.op("vector", lambda e, o=eqb[:], i=prb[:], s=m1[:]:
                         e.tensor_scalar(o, i, s, None, ALU.is_equal),
                         inc=tV, waits=[(tV, tV.n)])
                    w.op("vector", lambda e, o=eqb[:], i=prb[:]:
                         e.scalar_tensor_tensor(o, o, 1e30, i, ALU.mult, ALU.subtract),
                         inc=tV, waits=[(tV, tV.n)])
                    w.op("vector", lambda e, o=m2[:], i=eqb[:]:
                         e.tensor_reduce(o, i, axis=AX.X, op=ALU.min, negate=True),
                         inc=tV, waits=[(tV, tV.n)])
                    w.op("vector", lambda e, o=eqb[:], i=prb[:], s=m2[:]:
                         e.tensor_scalar(o, i, s, None, ALU.is_ge),
                         inc=tV, waits=[(tV, tV.n)])
                    w.op("vector",
                         lambda e, o=comb[:, tc * 8:(tc + 1) * 8], a=prb[:], b=eqb[:]:
                         e.tensor_tensor(o, a, b, ALU.mult),
                         inc=tV, waits=[(tV, tV.n)])
                    comb_ck[tc] = (tV, tV.n)
                sgt_war = NONE_T
                for tc in range(NTC):
                    w.op("scalar",
                         lambda e, o=sgt[:], i=pr9a[:, tc * 9 + 8:tc * 9 + 9]:
                         e.activation(o, i, AF.Tanh, scale=0.5),
                         inc=tS, waits=[(tV, cvps[tc]), sgt_war])
                    w.op("vector",
                         lambda e, o=sgv[:, tc:tc + 1], i=sgt[:]:
                         e.tensor_scalar(o, i, 0.5, 0.5, ALU.mult, ALU.add),
                         inc=tV, waits=[(tS, tS.n)])
                    sgt_war = (tV, tV.n)
                x3_ready = (tV, tV.n)
                comb_ready = (tV, tV.n)

            if debug:
                for nm, src in [("dbg_x3fm", x3fm), ("dbg_comb", comb), ("dbg_sg", sgv)]:
                    w.op("gpsimd", lambda e, o=dbg[nm], i=src[:]:
                         e.dma_start(out=o, in_=i), inc=qX, waits=[x3_ready])

            barrier()
            p2b = contextlib.ExitStack()
            with p2b:
                sbb = mksb(p2b)
                he2 = [sbb("he", (128, 8 * TOK), BF16) for _ in range(2)]
                silu_f = [sbb(f"silu{i}", (128, 512)) for i in range(2)]
                silu_free = [NONE_T, NONE_T]
                gS = Stream(sbb, "wg", 2048, 2048, cast_eng="vector")
                uS = Stream(sbb, "wu", 2048, 2048, cast_eng="scalar")
                dS = Stream(sbb, "wd", 4096, 2048, cast_eng="scalar")

                gb = [pA, pB]
                ub = [pC, pD]
                db = [pS, pR]
                gb_free = [NONE_T, NONE_T]
                ub_free = [NONE_T, NONE_T]
                db_free = [NONE_T, NONE_T]
                he_free = [NONE_T, NONE_T]
                moe_parts = NONE_T
                for pf in range(2):
                    gS.prefetch(dt["wg_t"][pf])
                    uS.prefetch(dt["wu_t"][pf])
                dS.prefetch(dt["wd_t"][0])
                dS.prefetch(dt["wd_t"][1])
                gu_next = 2
                d_next = 2
                for ee in range(E):
                    he = he2[ee % 2]
                    he_done = 0
                    for ici in range(8):
                        bg, grdy = gS.take()
                        bu, urdy = uS.take()
                        g16, u16 = gS.w16[bg], uS.w16[bu]
                        lastg = lastu = 0
                        for t in range(2):
                            gi = (ici * 2 + t) % 2
                            for ic in range(IC):
                                w.op("tensor",
                                     lambda e, o=gb[gi][:],
                                     l=g16[:, ic * 128:(ic + 1) * 128],
                                     r=x3fm[:, ic * TOK + t * 512:ic * TOK + (t + 1) * 512],
                                     s=(ic == 0), p=(ic == IC - 1):
                                     e.matmul(o, l, r, start=s, stop=p),
                                     inc=(tT if ic == IC - 1 else None),
                                     waits=[x3_ck[4 * t], x3_ck[4 * t + 1],
                                            x3_ck[4 * t + 2], x3_ck[4 * t + 3],
                                            grdy, gb_free[gi]])
                            gmm = (tT, tT.n)
                            lastg = tT.n
                            for ic in range(IC):
                                w.op("tensor",
                                     lambda e, o=ub[gi][:],
                                     l=u16[:, ic * 128:(ic + 1) * 128],
                                     r=x3fm[:, ic * TOK + t * 512:ic * TOK + (t + 1) * 512],
                                     s=(ic == 0), p=(ic == IC - 1):
                                     e.matmul(o, l, r, start=s, stop=p),
                                     inc=(tT if ic == IC - 1 else None),
                                     waits=[urdy, ub_free[gi]])
                            umm = (tT, tT.n)
                            lastu = tT.n
                            si = gi
                            sv = w.op("scalar",
                                      lambda e, o=silu_f[si][:], i=gb[gi][:]:
                                      e.activation(o, i, AF.Silu, scale=1.0),
                                      inc=tS, waits=[gmm, silu_free[si]])
                            gb_free[gi] = (tS, sv)
                            hv = w.op("vector",
                                      lambda e,
                                      o=he[:, ici * 1024 + t * 512:ici * 1024 + (t + 1) * 512],
                                      a=silu_f[si][:], b=ub[gi][:]:
                                      e.tensor_tensor(o, a, b, ALU.mult),
                                      inc=tV, waits=[(tS, sv), umm, he_free[ee % 2]])
                            silu_free[si] = (tV, hv)
                            ub_free[gi] = (tV, hv)
                            he_done = hv
                        gS.used(bg, (tT, lastg))
                        uS.used(bu, (tT, lastu))
                        if gu_next < E * 8:
                            gS.prefetch(dt["wg_t"][gu_next])
                            uS.prefetch(dt["wu_t"][gu_next])
                            gu_next += 1
                    he_ready = (tV, he_done)
                    if debug and ee == 0:
                        w.op("gpsimd", lambda e, o=dbg["dbg_he"], i=he[:]:
                             e.dma_start(out=o, in_=i), inc=qX, waits=[he_ready])
                    lastd_all = 0
                    for oc in range(4):
                        bd, drdy = dS.take()
                        d16 = dS.w16[bd]
                        lastd = 0
                        for tc in range(NTC):
                            di = tc % 2
                            for ici in range(8):
                                w.op("tensor",
                                     lambda e, o=db[di][:],
                                     l=he[:, ici * 1024 + tc * 128:ici * 1024 + tc * 128 + 128],
                                     r=d16[:, ici * 512:(ici + 1) * 512],
                                     s=(ici == 0), p=(ici == 7):
                                     e.matmul(o, l, r, start=s, stop=p),
                                     inc=(tT if ici == 7 else None),
                                     waits=[he_ready, drdy, db_free[di]])
                            mmv = (tT, tT.n)
                            lastd = tT.n
                            dst = moe[:, tc * D + oc * 512:tc * D + (oc + 1) * 512]
                            if ee == 0:
                                cv = w.op("vector",
                                          lambda e, o=dst, i=db[di][:],
                                          s=comb[:, tc * 8 + ee:tc * 8 + ee + 1]:
                                          e.tensor_scalar(o, i, s, None, ALU.mult),
                                          inc=tV, waits=[mmv, comb_ck[tc]])
                            else:
                                cv = w.op("vector",
                                          lambda e, o=dst, i=db[di][:],
                                          s=comb[:, tc * 8 + ee:tc * 8 + ee + 1]:
                                          e.scalar_tensor_tensor(o, i, s, o,
                                                                 ALU.mult, ALU.add),
                                          inc=tV, waits=[mmv, comb_ready, moe_parts])
                            db_free[di] = (tV, cv)
                        dS.used(bd, (tT, lastd))
                        if d_next < E * 4:
                            dS.prefetch(dt["wd_t"][d_next])
                            d_next += 1
                        lastd_all = lastd
                    moe_parts = (tV, tV.n)
                    he_free[ee % 2] = (tT, lastd_all)

            barrier()
            p2c = contextlib.ExitStack()
            with p2c:
                sbc = mksb(p2c)
                sh2 = [sbc("sh", (128, 32 * 512), BF16)]
                silu_f = [sbc(f"silu2_{i}", (128, 512)) for i in range(2)]
                silu_free = [NONE_T, NONE_T]
                gS = Stream(sbc, "wsg", 2048, 1024, cast_eng="vector")
                uS = Stream(sbc, "wsu", 2048, 1024, cast_eng="scalar")
                dS = Stream(sbc, "wsd", 4096, 2048, cast_eng="scalar")

                gb2 = [pA, pB]
                ub2 = [pC, pD]
                gb2_free = [NONE_T, NONE_T]
                ub2_free = [NONE_T, NONE_T]
                banks4 = [pA, pB, pS, pR]
                psum4_free = [NONE_T] * 4
                moe_tc = [NONE_T] * NTC
                sh_cons = NONE_T
                moe_sh = moe_parts
                gS.prefetch(dt["wsg_t"][0])
                uS.prefetch(dt["wsu_t"][0])
                gS.prefetch(dt["wsg_t"][1])
                uS.prefetch(dt["wsu_t"][1])
                gu_next2 = [2]
                for t in range(2):
                    sh = sh2[0]
                    sh_done = 0
                    for ici in range(32):
                        gi = ici % 2
                        bg, grdy = gS.take()
                        bu, urdy = uS.take()
                        g16, u16 = gS.w16[bg], uS.w16[bu]
                        for ic in range(IC):
                            w.op("tensor",
                                 lambda e, o=gb2[gi][:],
                                 l=g16[:, ic * 128:(ic + 1) * 128],
                                 r=x3fm[:, ic * TOK + t * 512:ic * TOK + (t + 1) * 512],
                                 s=(ic == 0), p=(ic == IC - 1):
                                 e.matmul(o, l, r, start=s, stop=p),
                                 inc=(tT if ic == IC - 1 else None),
                                 waits=[x3_ck[4 * t], x3_ck[4 * t + 1],
                                        x3_ck[4 * t + 2], x3_ck[4 * t + 3],
                                        grdy, gb2_free[gi]])
                        gmm = (tT, tT.n)
                        lastg = tT.n
                        for ic in range(IC):
                            w.op("tensor",
                                 lambda e, o=ub2[gi][:],
                                 l=u16[:, ic * 128:(ic + 1) * 128],
                                 r=x3fm[:, ic * TOK + t * 512:ic * TOK + (t + 1) * 512],
                                 s=(ic == 0), p=(ic == IC - 1):
                                 e.matmul(o, l, r, start=s, stop=p),
                                 inc=(tT if ic == IC - 1 else None),
                                 waits=[urdy, ub2_free[gi]])
                        umm = (tT, tT.n)
                        si = gi
                        sv = w.op("scalar",
                                  lambda e, o=silu_f[si][:], i=gb2[gi][:]:
                                  e.activation(o, i, AF.Silu, scale=1.0),
                                  inc=tS, waits=[gmm, silu_free[si]])
                        gb2_free[gi] = (tS, sv)
                        hv = w.op("vector",
                                  lambda e, o=sh[:, ici * 512:(ici + 1) * 512],
                                  a=silu_f[si][:], b=ub2[gi][:]:
                                  e.tensor_tensor(o, a, b, ALU.mult),
                                  inc=tV, waits=[(tS, sv), umm, sh_cons])
                        silu_free[si] = (tV, hv)
                        ub2_free[gi] = (tV, hv)
                        sh_done = hv
                        gS.used(bg, (tT, lastg))
                        uS.used(bu, (tT, tT.n))
                        nx = gu_next2[0]
                        if nx < 64:
                            gS.prefetch(dt["wsg_t"][nx % 32])
                            uS.prefetch(dt["wsu_t"][nx % 32])
                            gu_next2[0] += 1
                    sh_ready = (tV, sh_done)
                    psum4_free[0] = gb2_free[0]
                    psum4_free[1] = gb2_free[1]
                    for oc in range(4):
                        mm_last = [0, 0, 0, 0]
                        for icg in range(4):
                            bd, drdy = dS.load(dt["wsd_t"][oc * 4 + icg])
                            d16 = dS.w16[bd]
                            for tl in range(4):
                                pb = banks4[tl]
                                for icl in range(8):
                                    ici = icg * 8 + icl
                                    w.op("tensor",
                                         lambda e, o=pb[:],
                                         l=sh[:, ici * 512 + tl * 128:ici * 512 + tl * 128 + 128],
                                         r=d16[:, icl * 512:(icl + 1) * 512],
                                         s=(icg == 0 and icl == 0),
                                         p=(icg == 3 and icl == 7):
                                         e.matmul(o, l, r, start=s, stop=p),
                                         inc=(tT if icl == 7 else None),
                                         waits=[sh_ready, drdy, psum4_free[tl],
                                                gb2_free[tl % 2] if tl < 2 else NONE_T])
                                mm_last[tl] = tT.n
                            dS.used(bd, (tT, max(mm_last)))
                        for tl in range(4):
                            tc = t * 4 + tl
                            dst = moe[:, tc * D + oc * 512:tc * D + (oc + 1) * 512]
                            cv = w.op("vector",
                                      lambda e, o=dst, i=banks4[tl][:],
                                      s=sgv[:, tc:tc + 1]:
                                      e.scalar_tensor_tensor(o, i, s, o,
                                                             ALU.mult, ALU.add),
                                      inc=tV,
                                      waits=[(tT, mm_last[tl]), moe_sh, comb_ready])
                            psum4_free[tl] = (tV, cv)
                            if oc == 3:
                                moe_tc[tc] = (tV, cv)
                    gb2_free[0] = psum4_free[0]
                    gb2_free[1] = psum4_free[1]
                    sh_cons = (tT, tT.n)
                moe_ready = (tV, tV.n)

                if debug:
                    for tc in range(NTC):
                        w.op("gpsimd",
                             lambda e, o=dbg["dbg_acc"][tc], i=moe[:, tc * D:(tc + 1) * D]:
                             e.dma_start(out=o, in_=i),
                             inc=qX, waits=[moe_ready])

                sbd = sbc
                xin3 = [sbd("xin3_0", (128, D))]
                xin3_free = [NONE_T]
                l2c2 = [sbd(f"l2c2_{i}", (128, D), BF16) for i in range(2)]
                l2c2_free = [NONE_T, NONE_T]
                qxin3 = [mksem("qxin3_0", True)]
                ql2c2 = [mksem("ql2c2_0", True), mksem("ql2c2_1", True)]
                qP = [mksem("qP0", True), mksem("qP1", True)]
                t42 = [sbd("t4_0", (128, D))]
                outc = [sbd(f"outc{i}", (128, D)) for i in range(2)]
                outc_free = [NONE_T, NONE_T]
                t4_free = [NONE_T]

                for tc in range(NTC):
                    bi = tc % 2
                    v1 = w.op("gpsimd",
                              lambda e, o=xin3[0][:], i=dt["x_own"][tc]:
                              e.dma_start(out=o, in_=i),
                              inc=qxin3[0], waits=[xin3_free[0]])
                    v2 = w.op("gpsimd",
                              lambda e, o=l2c2[bi][:], i=ln2a_d[tc]:
                              e.dma_start(out=o, in_=i),
                              inc=ql2c2[bi], waits=[l2c2_free[bi]])
                    t4 = t42[0]
                    lv = emit_ln(moe[:, tc * D:(tc + 1) * D], t4, [moe_tc[tc]],
                                 [t4_free[0]], eps_ln, ls=tc % 2)
                    ov = w.op("vector",
                              lambda e, o=outc[bi][:], a=xin3[0][:], b=l2c2[bi][:]:
                              e.tensor_tensor(o, a, b, ALU.add),
                              inc=tV, waits=[(qxin3[0], v1), (ql2c2[bi], v2), outc_free[bi]])
                    ov2 = w.op("vector",
                               lambda e, o=outc[bi][:], b=t4[:]:
                               e.tensor_tensor(o, o, b, ALU.add),
                               inc=tV, waits=[(tV, lv), (tV, ov)])
                    t4_free[0] = (tV, ov2)
                    xin3_free[0] = (tV, ov)
                    l2c2_free[bi] = (tV, ov2)
                    sv = w.op("gpsimd",
                              lambda e, o=out_d[tc], i=outc[bi][:]:
                              e.dma_start(out=o, in_=i),
                              inc=qP[bi], waits=[(tV, ov2)])
                    outc_free[bi] = (qP[bi], sv)

        w.wait("gpsimd", [(s, s.n) for s in ALLSEMS if s.dma])

        with nc.Block() as blk:
            @blk.sync
            def _(eng):
                for f in w.prog["sync"]:
                    f(eng)

            @blk.tensor
            def _(eng):
                for f in w.prog["tensor"]:
                    f(eng)

            @blk.vector
            def _(eng):
                for f in w.prog["vector"]:
                    f(eng)

            @blk.scalar
            def _(eng):
                for f in w.prog["scalar"]:
                    f(eng)

            @blk.gpsimd
            def _(eng):
                for f in w.prog["gpsimd"]:
                    f(eng)

    return nc


def kernel(**inputs):
    in_maps = host_prepare(inputs)
    nc = build_nc()
    res = run_bass_kernel_spmd(nc, in_maps, core_ids=list(range(NCORE)))
    out = np.empty((B, T, D), dtype=np.float32)
    for core in range(NCORE):
        b, half = core // 2, core % 2
        out[b, half * TOK:(half + 1) * TOK] = res.results[core]["out"].reshape(TOK, D)
    return out
